# version 06
# baseline (speedup 1.0000x reference)
import numpy as np
import ml_dtypes
import concourse.bass as bass
import concourse.mybir as mybir
from concourse.bass import AP
from concourse.bass_utils import run_bass_kernel_spmd

F32 = mybir.dt.float32
BF16 = mybir.dt.bfloat16
AL = mybir.AluOpType
ACT = mybir.ActivationFunctionType
BF = ml_dtypes.bfloat16

P = 128
T = 16384
NB = 2
NH = 2
HD = 2048
NI = 4096
DM = 2048
NST = 16
RK = 128
TPC = 4096
HT = 2048
TT = 512
HCOLS = 2056
LN_EPS = 1e-5
DBG = False
PARANOID = 0
SP_CLAMP = 80.0

L1_TILES = [(j * TT, TT) for j in range(4)] + [(4 * TT, 8)]


class Sem:
    def __init__(self, handle):
        self.h = handle
        self.n = 0

    def inc(self, ins, k=1):
        ins.then_inc(self.h, k)
        self.n += k
        return ins


def wait_all(eng, *sems):
    for s in sems:
        if s.n > 0:
            eng.wait_ge(s.h, s.n)


class Ctx:
    def __init__(self, nc, block, sems, qsems=()):
        self.nc = nc
        self.block = block
        self.ld, self.st, self.mm, self.v, self.s = sems
        self.q = list(qsems)

    def barrier(self):
        c = self
        @c.block.sync
        def _(sync):
            wait_all(sync, c.st, c.mm, c.v, c.s)
        @c.block.tensor
        def _(tensor):
            wait_all(tensor, c.ld, c.st, c.v, c.s)
        @c.block.vector
        def _(vector):
            wait_all(vector, c.ld, c.st, c.mm, c.s)
        @c.block.scalar
        def _(scalar):
            wait_all(scalar, c.ld, c.st, c.mm, c.v)
        @c.block.gpsimd
        def _(gpsimd):
            wait_all(gpsimd, c.ld, c.mm, c.v, c.s)


def _store_win(j, w):
    c0 = j * TT
    lo = max(c0, 3)
    hi = min(c0 + w, 3 + HT)
    return (lo - c0, lo - 3, hi - lo)



def _l1_phase_a(c, r, half, xnb):
    sfx = str(half)
    """LN over feature dim -> xnb (bf16, feature-major)."""
    nc = c.nc
    with (
        nc.sbuf_tensor("xstage" + sfx, [P, 16, TT], F32) as xstage,
        nc.sbuf_tensor("sq" + sfx, [P, 2, TT], F32) as sq,
        nc.sbuf_tensor("stat" + sfx, [1, 6, TT], F32) as stat,
    ):
        c.barrier()
        @c.block.vector
        def _(vector):
            c.v.inc(vector.memset(xnb[:, :, 4 * TT + 3:], 0.0))
        for (c0, w) in L1_TILES:
            @c.block.sync
            def _(sync, c0=c0, w=w):
                wait_all(sync, c.mm, c.v)
                for fc in range(16):
                    sync.dma_start(out=xstage[:, fc, :w],
                                   in_=r.xin[fc * P:(fc + 1) * P, half, c0:c0 + w]
                                   ).then_inc(c.ld.h, 16)
                c.ld.n += 16 * 16
            @c.block.tensor
            def _(tensor, w=w):
                wait_all(tensor, c.ld, c.v)
                for fc in range(16):
                    tensor.matmul(out=r.psA[:, :w], lhsT=r.ones_col[:, :],
                                  rhs=xstage[:, fc, :w], start=(fc == 0), stop=(fc == 15))
            mm_base = c.mm.n
            s_base = c.s.n
            for fc in range(16):
                @c.block.scalar
                def _(scalar, w=w, fc=fc):
                    if fc == 0:
                        wait_all(scalar, c.ld, c.mm)
                    if fc >= 2:
                        scalar.wait_ge(c.mm.h, mm_base + fc - 1)
                    c.s.inc(scalar.activation(out=sq[:, fc % 2, :w],
                                              in_=xstage[:, fc, :w], func=ACT.Square))
                @c.block.tensor
                def _(tensor, w=w, fc=fc):
                    tensor.wait_ge(c.s.h, s_base + fc + 1)
                    c.mm.inc(tensor.matmul(out=r.psB[:, :w], lhsT=r.ones_col[:, :],
                                           rhs=sq[:, fc % 2, :w],
                                           start=(fc == 0), stop=(fc == 15)))
            @c.block.vector
            def _(vector, w=w):
                wait_all(vector, c.mm)
                vector.tensor_scalar_mul(out=stat[:, 0, :w], in0=r.psA[:, :w],
                                         scalar1=1.0 / DM)
                c.v.inc(vector.tensor_scalar_mul(out=stat[:, 1, :w], in0=r.psB[:, :w],
                                                 scalar1=1.0 / DM))
                vector.wait_ge(c.v.h, c.v.n)
                c.v.inc(vector.tensor_mul(out=stat[:, 2, :w], in0=stat[:, 0, :w],
                                          in1=stat[:, 0, :w]))
                vector.wait_ge(c.v.h, c.v.n)
                c.v.inc(vector.tensor_sub(out=stat[:, 3, :w], in0=stat[:, 1, :w],
                                          in1=stat[:, 2, :w]))
            @c.block.scalar
            def _(scalar, w=w):
                wait_all(scalar, c.v)
                scalar.activation(out=stat[:, 4, :w], in_=stat[:, 3, :w],
                                  func=ACT.Ln, bias=r.consts[:, 0:1])
                c.s.inc(scalar.activation(out=stat[:, 5, :w], in_=stat[:, 4, :w],
                                          func=ACT.Exp, scale=r.consts[:, 1:2]))
            @c.block.tensor
            def _(tensor, w=w):
                wait_all(tensor, c.s)
                tensor.matmul(out=r.psM[:, :w], lhsT=r.ones_row[:, :],
                              rhs=stat[:, 0, :w], start=True, stop=True)
                c.mm.inc(tensor.matmul(out=r.psR[:, :w], lhsT=r.ones_row[:, :],
                                       rhs=stat[:, 5, :w], start=True, stop=True))
            @c.block.vector
            def _(vector, c0=c0, w=w):
                wait_all(vector, c.mm)
                for fc in range(16):
                    vector.tensor_sub(out=xstage[:, fc, :w], in0=xstage[:, fc, :w],
                                      in1=r.psM[:, :w])
                for fc in range(16):
                    ins = vector.tensor_tensor(out=xnb[:, fc, c0:c0 + w],
                                               in0=xstage[:, fc, :w],
                                               in1=r.psR[:, :w], op=AL.mult)
                c.v.inc(ins)


def _l1_phase_b(c, r, half, xnb):
    nc = c.nc
    sfx = str(half)
    with (
        nc.sbuf_tensor("wst" + sfx, [P, 2, 16, P], F32) as wst,
        nc.sbuf_tensor("wsb" + sfx, [P, 2, 16, P], BF16) as wsb,
        nc.sbuf_tensor("xbuf" + sfx, [P, 3 + TT], F32) as xbuf,
        nc.sbuf_tensor("cs" + sfx, [P, TT], F32) as cs,
        nc.sbuf_tensor("xst" + sfx, [P, TT], F32) as xst,
        nc.sbuf_tensor("xst16" + sfx, [P, TT], BF16) as xst16,
        nc.sbuf_tensor("sgt" + sfx, [P, 2, TT], F32) as sgt,
        nc.sbuf_tensor("pjdbg" + sfx, [P, 2, 8], F32) as pjdbg,
        nc.psum_tensor("psH" + sfx, [P, 3], F32) as psH,
    ):
        c.barrier()
        for mc in range(64):
            is_x = mc < 32
            wT = r.wxT if is_x else r.wgT
            wc = mc if is_x else mc - 32
            h, dc = wc // 16, wc % 16
            ws = mc % 2
            @c.block.sync
            def _(sync, wT=wT, wc=wc, ws=ws):
                wait_all(sync, c.v)
                sync.dma_start(
                    out=wst[:, ws, :, :],
                    in_=wT[:, wc * P:(wc + 1) * P].rearrange("(k p) m -> p k m", p=P)
                ).then_inc(c.ld.h, 16)
                c.ld.n += 16
            @c.block.vector
            def _(vector, ws=ws):
                wait_all(vector, c.ld, c.mm)
                c.v.inc(vector.tensor_copy(out=wsb[:, ws, :, :], in_=wst[:, ws, :, :]))
            if is_x:
                @c.block.tensor
                def _(tensor, ws=ws):
                    wait_all(tensor, c.v)
                    for kc in range(16):
                        ins = tensor.matmul(out=psH[:, :], lhsT=wsb[:, ws, kc, :],
                                            rhs=xnb[:, kc, 0:3],
                                            start=(kc == 0), stop=(kc == 15))
                    c.mm.inc(ins)
                @c.block.vector
                def _(vector):
                    wait_all(vector, c.mm)
                    c.v.inc(vector.tensor_copy(out=xbuf[:, 0:3], in_=psH[:, :]))
            for ti, (c0, w) in enumerate(L1_TILES):
                pp = r.psP[ti % 2]
                @c.block.tensor
                def _(tensor, ws=ws, c0=c0, w=w, pp=pp):
                    wait_all(tensor, c.v, c.s)
                    for kc in range(16):
                        ins = tensor.matmul(out=pp[:, :w], lhsT=wsb[:, ws, kc, :],
                                            rhs=xnb[:, kc, c0:c0 + w],
                                            start=(kc == 0), stop=(kc == 15))
                    c.mm.inc(ins)
                so, do, wd = _store_win(ti, w)
                do_g = do + half * HT
                if DBG and half == 0 and ti == 4 and mc in (0, 32):
                    @c.block.vector
                    def _(vector, pp=pp, mc=mc):
                        wait_all(vector, c.mm)
                        c.v.inc(vector.tensor_copy(out=pjdbg[:, mc // 32, :],
                                                   in_=pp[:, :8]))
                    @c.block.gpsimd
                    def _(gpsimd, mc=mc):
                        wait_all(gpsimd, c.v)
                        gpsimd.dma_start(out=r.pj_o[mc // 32, :, :],
                                         in_=pjdbg[:, mc // 32, :]).then_inc(c.st.h, 16)
                        c.st.n += 16
                if is_x:
                    @c.block.vector
                    def _(vector, w=w, pp=pp, ti=ti, mc=mc):
                        small = w < 256
                        def step(ins):
                            c.v.inc(ins)
                            if small:
                                vector.wait_ge(c.v.h, c.v.n)
                        wait_all(vector, c.mm, c.st)
                        if ti > 0:
                            step(vector.tensor_copy(out=xbuf[:, 0:3],
                                                    in_=xbuf[:, TT:TT + 3]))
                        step(vector.tensor_copy(out=xbuf[:, 3:3 + w], in_=pp[:, :w]))
                        step(vector.tensor_scalar_mul(out=cs[:, :w], in0=xbuf[:, 0:w],
                                                      scalar1=r.cwt[:, mc, 0:1]))
                        for k in (1, 2, 3):
                            step(vector.scalar_tensor_tensor(
                                out=cs[:, :w], in0=xbuf[:, k:k + w],
                                scalar=r.cwt[:, mc, k:k + 1], in1=cs[:, :w],
                                op0=AL.mult, op1=AL.add))
                    @c.block.scalar
                    def _(scalar, w=w, mc=mc):
                        wait_all(scalar, c.v)
                        c.s.inc(scalar.activation(out=xst[:, :w], in_=cs[:, :w],
                                                  func=ACT.Silu, bias=r.cbt[:, mc:mc + 1]))
                    @c.block.vector
                    def _(vector, w=w):
                        wait_all(vector, c.s)
                        c.v.inc(vector.tensor_copy(out=xst16[:, :w], in_=xst[:, :w]))
                    if wd > 0:
                        @c.block.gpsimd
                        def _(gpsimd, h=h, dc=dc, so=so, do_g=do_g, wd=wd):
                            wait_all(gpsimd, c.v)
                            gpsimd.dma_start(
                                out=r.xs_o[h, dc * P:(dc + 1) * P, do_g:do_g + wd],
                                in_=xst[:, so:so + wd]).then_inc(c.st.h, 16)
                            gpsimd.dma_start(
                                out=r.xsb[h, dc * P:(dc + 1) * P, do_g:do_g + wd],
                                in_=xst16[:, so:so + wd]).then_inc(c.st.h, 16)
                            c.st.n += 32
                else:
                    @c.block.scalar
                    def _(scalar, w=w, pp=pp, ti=ti):
                        wait_all(scalar, c.mm, c.st)
                        c.s.inc(scalar.activation(out=sgt[:, ti % 2, :w], in_=pp[:, :w],
                                                  func=ACT.Silu))
                    if wd > 0:
                        @c.block.gpsimd
                        def _(gpsimd, h=h, dc=dc, so=so, do_g=do_g, wd=wd, ti=ti):
                            wait_all(gpsimd, c.s)
                            gpsimd.dma_start(
                                out=r.sg_o[h, dc * P:(dc + 1) * P, do_g:do_g + wd],
                                in_=sgt[:, ti % 2, so:so + wd]).then_inc(c.st.h, 16)
                            c.st.n += 16


def _l1_phase_cd(c, r):
    nc = c.nc
    with (
        nc.sbuf_tensor("xpst", [P, NH, 16, 160], F32) as xpst,
        nc.sbuf_tensor("xpsb", [P, NH, 16, 160], BF16) as xpsb,
        nc.sbuf_tensor("dtwst", [P, NH, HD], F32) as dtwst,
        nc.sbuf_tensor("dtwsb", [P, NH, HD], BF16) as dtwsb,
        nc.sbuf_tensor("dtbt", [P, NH, 16], F32) as dtbt,
        nc.sbuf_tensor("xsl", [P, 2, TT], BF16) as xsl,
        nc.sbuf_tensor("dtin", [P, TT], BF16) as dtin,
        nc.sbuf_tensor("bct", [32, TT], F32) as bct,
        nc.sbuf_tensor("dpre", [P, 2, TT], F32) as dpre,
        nc.sbuf_tensor("dex", [P, TT], F32) as dex,
        nc.sbuf_tensor("dtt", [P, 2, TT], F32) as dtt,
    ):
        c.barrier()
        @c.block.sync
        def _(sync):
            sync.dma_start(out=xpst[:, :, :, :],
                           in_=r.xpT.ap().rearrange("h (k p) r -> p h k r", p=P)
                           ).then_inc(c.ld.h, 16)
            sync.dma_start(out=dtwst[:, :, :],
                           in_=r.dtwT.ap().rearrange("h p d -> p h d")).then_inc(c.ld.h, 16)
            sync.dma_start(out=dtbt[:, :, :],
                           in_=r.dtb.ap().rearrange("h p k -> p h k")).then_inc(c.ld.h, 16)
            c.ld.n += 48
        @c.block.vector
        def _(vector):
            wait_all(vector, c.ld)
            vector.tensor_copy(out=xpsb[:, :, :, :], in_=xpst[:, :, :, :])
            c.v.inc(vector.tensor_copy(out=dtwsb[:, :, :], in_=dtwst[:, :, :]))
        c.barrier()
        for h in range(NH):
            for tj in range(TPC // TT):
                t0 = tj * TT
                for dc in range(16):
                    qs = c.q[dc % 2]
                    @c.block.sync
                    def _(sync, h=h, t0=t0, dc=dc, qs=qs):
                        wait_all(sync, c.mm)
                        sync.dma_start(out=xsl[:, dc % 2, :],
                                       in_=r.xsb[h, dc * P:(dc + 1) * P, t0:t0 + TT]
                                       ).then_inc(qs.h, 16)
                        qs.n += 16
                    @c.block.tensor
                    def _(tensor, h=h, dc=dc, qs=qs):
                        tensor.wait_ge(qs.h, qs.n)
                        if dc == 0:
                            wait_all(tensor, c.v)
                        tensor.matmul(out=r.psM[:, :], lhsT=xpsb[:, h, dc, 0:P],
                                      rhs=xsl[:, dc % 2, :],
                                      start=(dc == 0), stop=(dc == 15))
                        c.mm.inc(tensor.matmul(out=r.psR[0:32, :],
                                               lhsT=xpsb[:, h, dc, P:160],
                                               rhs=xsl[:, dc % 2, :],
                                               start=(dc == 0), stop=(dc == 15)))
                @c.block.vector
                def _(vector):
                    wait_all(vector, c.mm, c.st)
                    vector.tensor_copy(out=dtin[:, :], in_=r.psM[:, :])
                    c.v.inc(vector.tensor_copy(out=bct[:, :], in_=r.psR[0:32, :]))
                @c.block.gpsimd
                def _(gpsimd, h=h, t0=t0):
                    wait_all(gpsimd, c.v)
                    gpsimd.dma_start(out=r.bc_o[h, :, t0:t0 + TT],
                                     in_=bct[:, :]).then_inc(c.st.h, 16)
                    c.st.n += 16
                for oc in range(16):
                    pp = r.psP[oc % 2]
                    @c.block.tensor
                    def _(tensor, h=h, oc=oc, pp=pp):
                        wait_all(tensor, c.v)
                        if oc < 2:
                            wait_all(tensor, c.s)
                        c.mm.inc(tensor.matmul(out=pp[:, :],
                                               lhsT=dtwsb[:, h, oc * P:(oc + 1) * P],
                                               rhs=dtin[:, :], start=True, stop=True))
                    @c.block.vector
                    def _(vector, h=h, oc=oc, pp=pp):
                        wait_all(vector, c.mm, c.s)
                        c.v.inc(vector.tensor_scalar(
                            out=dpre[:, oc % 2, :], in0=pp[:, :],
                            scalar1=dtbt[:, h, oc:oc + 1], scalar2=SP_CLAMP,
                            op0=AL.add, op1=AL.min))
                    @c.block.scalar
                    def _(scalar, oc=oc):
                        wait_all(scalar, c.v, c.st)
                        scalar.activation(out=dex[:, :], in_=dpre[:, oc % 2, :],
                                          func=ACT.Exp)
                        c.s.inc(scalar.activation(out=dtt[:, oc % 2, :], in_=dex[:, :],
                                                  func=ACT.Ln, bias=r.ones_col[:, 0:1]))
                    @c.block.gpsimd
                    def _(gpsimd, h=h, t0=t0, oc=oc):
                        wait_all(gpsimd, c.s)
                        gpsimd.dma_start(out=r.dt_o[h, oc * P:(oc + 1) * P, t0:t0 + TT],
                                         in_=dtt[:, oc % 2, :]).then_inc(c.st.h, 16)
                        c.st.n += 16
        @c.block.gpsimd
        def _(gpsimd):
            gpsimd.wait_ge(c.st.h, c.st.n)


class _R:
    pass


def build_l1():
    nc = bass.Bass("TRN2", target_bir_lowering=False)
    r = _R()
    r.xin = nc.dram_tensor("xin", [DM, 2, HCOLS], F32, kind="ExternalInput")
    r.wxT = nc.dram_tensor("wxT", [DM, NI], F32, kind="ExternalInput")
    r.wgT = nc.dram_tensor("wgT", [DM, NI], F32, kind="ExternalInput")
    r.convw = nc.dram_tensor("convw", [P, 32, 4], F32, kind="ExternalInput")
    r.convb = nc.dram_tensor("convb", [P, 32], F32, kind="ExternalInput")
    r.xpT = nc.dram_tensor("xpT", [NH, HD, 160], F32, kind="ExternalInput")
    r.dtwT = nc.dram_tensor("dtwT", [NH, RK, HD], F32, kind="ExternalInput")
    r.dtb = nc.dram_tensor("dtb", [NH, P, 16], F32, kind="ExternalInput")
    r.dt_o = nc.dram_tensor("dt_o", [NH, HD, TPC], F32, kind="ExternalOutput")
    r.xs_o = nc.dram_tensor("xs_o", [NH, HD, TPC], F32, kind="ExternalOutput")
    r.sg_o = nc.dram_tensor("sg_o", [NH, HD, TPC], F32, kind="ExternalOutput")
    r.bc_o = nc.dram_tensor("bc_o", [NH, 32, TPC], F32, kind="ExternalOutput")
    r.xsb = nc.dram_tensor("xsb", [NH, HD, TPC], BF16)
    if DBG:
        r.xnb_o = nc.dram_tensor("xnb_o", [2, DM, HCOLS], BF16, kind="ExternalOutput")
        r.pj_o = nc.dram_tensor("pj_o", [2, P, 8], F32, kind="ExternalOutput")

    import contextlib
    _es = contextlib.ExitStack()
    ones_col = _es.enter_context(nc.sbuf_tensor("ones_col", [P, 1], F32))
    ones_row = _es.enter_context(nc.sbuf_tensor("ones_row", [1, P], F32))
    consts = _es.enter_context(nc.sbuf_tensor("consts", [1, 4], F32))
    cwt = _es.enter_context(nc.sbuf_tensor("cwt", [P, 32, 4], F32))
    cbt = _es.enter_context(nc.sbuf_tensor("cbt", [P, 32], F32))
    psA = _es.enter_context(nc.psum_tensor("psA", [1, TT], F32))
    psB = _es.enter_context(nc.psum_tensor("psB", [1, TT], F32))
    psM = _es.enter_context(nc.psum_tensor("psM", [P, TT], F32))
    psR = _es.enter_context(nc.psum_tensor("psR", [P, TT], F32))
    psP0 = _es.enter_context(nc.psum_tensor("psP0", [P, TT], F32))
    psP1 = _es.enter_context(nc.psum_tensor("psP1", [P, TT], F32))
    _s_ld = _es.enter_context(nc.semaphore("s_ld"))
    _s_st = _es.enter_context(nc.semaphore("s_st"))
    _s_mm = _es.enter_context(nc.semaphore("s_mm"))
    _s_v = _es.enter_context(nc.semaphore("s_v"))
    _s_s = _es.enter_context(nc.semaphore("s_s"))
    _s_q0 = _es.enter_context(nc.semaphore("s_q0"))
    _s_q1 = _es.enter_context(nc.semaphore("s_q1"))
    _s_q2 = _es.enter_context(nc.semaphore("s_q2"))
    _s_q3 = _es.enter_context(nc.semaphore("s_q3"))
    block = _es.enter_context(nc.Block())
    if True:
        r.ones_col, r.ones_row, r.cwt, r.cbt = ones_col, ones_row, cwt, cbt
        r.consts = consts
        r.psA, r.psB, r.psM, r.psR = psA, psB, psM, psR
        r.psP = [psP0, psP1]
        c = Ctx(nc, block, [Sem(_s_ld), Sem(_s_st), Sem(_s_mm), Sem(_s_v), Sem(_s_s)],
                [Sem(_s_q0), Sem(_s_q1), Sem(_s_q2), Sem(_s_q3)])

        @block.sync
        def _(sync):
            sync.dma_start(out=cwt[:, :, :], in_=r.convw[:, :, :]).then_inc(c.ld.h, 16)
            sync.dma_start(out=cbt[:, :], in_=r.convb[:, :]).then_inc(c.ld.h, 16)
            c.ld.n += 32

        @block.vector
        def _(vector):
            vector.memset(ones_col[:, :], 1.0)
            vector.memset(ones_row[:, :], 1.0)
            vector.memset(consts[:, 0:1], LN_EPS)
            c.v.inc(vector.memset(consts[:, 1:2], -0.5))

        for half in range(2):
            with nc.sbuf_tensor(f"xnb{half}", [P, 16, HCOLS], BF16) as xnb:
                _l1_phase_a(c, r, half, xnb)
                if DBG:
                    c.barrier()
                    @block.gpsimd
                    def _(gpsimd, half=half, xnb=xnb):
                        for fc in range(16):
                            gpsimd.dma_start(
                                out=r.xnb_o[half, fc * P:(fc + 1) * P, :],
                                in_=xnb[:, fc, :]).then_inc(c.st.h, 16)
                        c.st.n += 16 * 16
                _l1_phase_b(c, r, half, xnb)
        _l1_phase_cd(c, r)

    _es.close()
    return nc



def _l1_inputs(inputs, xT):
    w_in = np.asarray(inputs["L0_attn_in_proj_w"], np.float32)
    wxT = np.ascontiguousarray(w_in[0::2].T)
    wgT = np.ascontiguousarray(w_in[1::2].T)
    cw = np.asarray(inputs["L0_attn_conv_w"], np.float32)
    convw = np.ascontiguousarray(cw.reshape(32, P, 4).transpose(1, 0, 2))
    cb = np.asarray(inputs["L0_attn_conv_b"], np.float32)
    convb = np.ascontiguousarray(cb.reshape(32, P).T)
    xpT = np.ascontiguousarray(
        np.asarray(inputs["L0_attn_x_proj_weight"], np.float32).transpose(0, 2, 1))
    dtwT = np.ascontiguousarray(
        np.asarray(inputs["L0_attn_dt_proj_weight"], np.float32).transpose(0, 2, 1))
    dtb = np.ascontiguousarray(
        np.asarray(inputs["L0_attn_dt_proj_bias"], np.float32)
        .reshape(NH, 16, P).transpose(0, 2, 1))
    maps = []
    for c in range(8):
        b = (c // 2) % 2
        q = 2 * (c // 4) + (c % 2)
        s0 = q * TPC
        xin = np.zeros((DM, 2, HCOLS), np.float32)
        for k in range(2):
            base = s0 + HT * k
            lo = max(base - 3, 0)
            seg = xT[b, :, lo:base + HT]
            xin[:, k, 3 - (base - lo):3 + HT] = seg
        maps.append({
            "xin": xin, "wxT": wxT, "wgT": wgT, "convw": convw, "convb": convb,
            "xpT": xpT, "dtwT": dtwT, "dtb": dtb,
        })
    return maps


def run_l1(inputs, xT):
    nc = build_l1()
    maps = _l1_inputs(inputs, xT)
    res = run_bass_kernel_spmd(nc, maps, core_ids=list(range(8)))
    dtg = np.empty((NH, NB, HD, T), np.float32)
    xsg = np.empty((NH, NB, HD, T), np.float32)
    sgg = np.empty((NH, NB, HD, T), np.float32)
    bcg = np.empty((NH, NB, 32, T), np.float32)
    for c in range(8):
        b = (c // 2) % 2
        q = 2 * (c // 4) + (c % 2)
        s0 = q * TPC
        rr = res.results[c]
        dtg[:, b, :, s0:s0 + TPC] = rr["dt_o"]
        xsg[:, b, :, s0:s0 + TPC] = rr["xs_o"]
        sgg[:, b, :, s0:s0 + TPC] = rr["sg_o"]
        bcg[:, b, :, s0:s0 + TPC] = rr["bc_o"]
    return dtg, xsg, sgg, bcg



TW = 1024


def _l2_prologue(c, r):
    at, dvt, carry = r.at, r.dvt, r.carry
    block = c.block

    @block.sync
    def _(sync):
        sync.dma_start(out=at[:, :, :], in_=r.alog[:, :, :]).then_inc(c.ld.h, 16)
        sync.dma_start(out=dvt[:, :], in_=r.dvec[:, :]).then_inc(c.ld.h, 16)
        c.ld.n += 32

    @block.scalar
    def _(scalar):
        wait_all(scalar, c.ld)
        c.s.inc(scalar.activation(out=at[:, :, :], in_=at[:, :, :], func=ACT.Exp))

    @block.vector
    def _(vector):
        vector.memset(carry[:, :], 0.0)
        wait_all(vector, c.s)
        c.v.inc(vector.tensor_scalar_mul(out=at[:, :, :], in0=at[:, :, :],
                                         scalar1=-1.0))
        vector.wait_ge(c.v.h, c.v.n)


def _l2_body(c, r):
    if DBG and getattr(r, "at_d", None) is not None:
        @c.block.scalar
        def _(scalar):
            wait_all(scalar, c.v, c.s)
            scalar.dma_start(out=r.at_d.ap(),
                             in_=r.at.ap().rearrange("p a b -> p (a b)")).then_inc(c.ld.h, 16)
            scalar.dma_start(out=r.da_d[:, TW:2 * TW],
                             in_=r.dt_i[0:P, 0:TW]).then_inc(c.ld.h, 16)
            c.ld.n += 32
            scalar.wait_ge(c.ld.h, c.ld.n)
    bc0 = getattr(r, "bc_row0", 0)
    cbsrc = getattr(r, "cb_src", None)
    at, dvt, carry = r.at, r.dvt, r.carry
    dtt, ut, yt = r.dtt, r.ut, r.yt
    xstg, sgs, bb, cb = r.xstg, r.sgs, r.bb, r.cb
    dat, vt, ht, pt, ymt = r.dat, r.vt, r.ht, r.pt, r.ymt
    s_it, s_g, s_bc, s_sg = c.q[0], c.q[1], c.q[2], c.q[3]
    carry_prev = [None]
    it_ctr = [0]
    for tj in range(T // TW):
        t0 = tj * TW
        for dc in range(8):
            @c.block.sync
            def _(sync, t0=t0, dc=dc):
                wait_all(sync, c.v)
                sync.dma_start(out=dtt[:, dc, :],
                               in_=r.dt_i[dc * P:(dc + 1) * P, t0:t0 + TW]
                               ).then_inc(c.ld.h, 16)
                sync.dma_start(out=xstg[:, dc % 2, :],
                               in_=r.xs_i[dc * P:(dc + 1) * P, t0:t0 + TW]
                               ).then_inc(c.ld.h, 16)
                c.ld.n += 32
            @c.block.vector
            def _(vector, dc=dc):
                wait_all(vector, c.ld)
                vector.wait_ge(s_g.h, s_g.n)
                vector.tensor_mul(out=ut[:, dc, :], in0=dtt[:, dc, :],
                                  in1=xstg[:, dc % 2, :])
                c.v.inc(vector.tensor_scalar_mul(out=yt[:, dc, :],
                                                 in0=xstg[:, dc % 2, :],
                                                 scalar1=dvt[:, dc:dc + 1]))
        uv = c.v.n
        for n in range(16):
            ns = n % 2
            @c.block.sync
            def _(sync, t0=t0, n=n, ns=ns):
                if s_g.n >= 8:
                    sync.wait_ge(s_g.h, s_g.n - 7)
                if s_it.n >= 8:
                    sync.wait_ge(s_it.h, s_it.n - 7)
                sync.dma_start(out=bb[:, ns, :],
                               in_=AP(r.bc_i, (bc0 + n) * T + t0, [[0, P], [1, TW]])
                               ).then_inc(s_bc.h, 16)
                sync.dma_start(out=cb[:, ns, :],
                               in_=AP(cbsrc if cbsrc is not None else r.bc_i,
                                      ((0 if cbsrc is not None else bc0 + 16) + n) * T
                                      + t0, [[0, P], [1, TW]])
                               ).then_inc(s_bc.h, 16)
                s_bc.n += 32
            for dc in range(8):
                slot = dc % 2
                it = it_ctr[0]
                it_ctr[0] += 1
                g_adds = (it % 2 == 0)
                @c.block.scalar
                def _(scalar, n=n, dc=dc, slot=slot, uv=uv):
                    if carry_prev[0] is not None:
                        scalar.wait_ge(s_it.h, s_it.n)
                        c.s.inc(scalar.copy(
                            out=carry[:, carry_prev[0]:carry_prev[0] + 1],
                            in_=ht[:, TW - 1:TW]))
                    scalar.wait_ge(c.v.h, uv)
                    if s_it.n >= 2:
                        scalar.wait_ge(s_it.h, s_it.n - 1)
                    c.s.inc(scalar.activation(out=dat[:, slot, :],
                                              in_=dtt[:, dc, :], func=ACT.Exp,
                                              scale=at[:, dc, n:n + 1]))
                @c.block.gpsimd
                def _(gpsimd, n=n, dc=dc, ns=ns, slot=slot, uv=uv, g_adds=g_adds):
                    gpsimd.wait_ge(c.v.h, uv)
                    gpsimd.wait_ge(s_bc.h, s_bc.n)
                    if s_it.n >= 2:
                        gpsimd.wait_ge(s_it.h, s_it.n - 1)
                    s_g.inc(gpsimd.tensor_tensor(out=vt[:, slot, :],
                                                 in0=ut[:, dc, :],
                                                 in1=bb[:, ns, :], op=AL.mult))
                cidx = dc * 16 + n
                @c.block.vector
                def _(vector, n=n, dc=dc, ns=ns, slot=slot, cidx=cidx,
                      g_adds=g_adds):
                    vector.wait_ge(s_g.h, s_g.n)
                    vector.wait_ge(c.s.h, c.s.n)
                    vector.wait_ge(s_bc.h, s_bc.n)
                    s_it.inc(vector.tensor_tensor_scan(
                        out=ht[:, :], data0=dat[:, slot, :], data1=vt[:, slot, :],
                        initial=carry[:, cidx:cidx + 1], op0=AL.mult, op1=AL.add))
                    ins = vector.tensor_mul(out=pt[:, slot, :], in0=ht[:, :],
                                            in1=cb[:, ns, :])
                    if not g_adds:
                        ins = vector.tensor_add(out=yt[:, dc, :], in0=yt[:, dc, :],
                                                in1=pt[:, slot, :])
                    c.v.inc(ins)
                if g_adds:
                    @c.block.gpsimd
                    def _(gpsimd, dc=dc, slot=slot):
                        gpsimd.wait_ge(c.v.h, c.v.n)
                        s_g.inc(gpsimd.tensor_tensor(out=yt[:, dc, :],
                                                     in0=yt[:, dc, :],
                                                     in1=pt[:, slot, :], op=AL.add))
                carry_prev[0] = cidx
        for dc in range(8):
            @c.block.sync
            def _(sync, t0=t0, dc=dc):
                wait_all(sync, c.v)
                sync.dma_start(out=sgs[:, dc % 2, :],
                               in_=r.sg_i[dc * P:(dc + 1) * P, t0:t0 + TW]
                               ).then_inc(s_sg.h, 16)
                s_sg.n += 16
            @c.block.vector
            def _(vector, dc=dc):
                vector.wait_ge(s_sg.h, s_sg.n)
                vector.wait_ge(s_g.h, s_g.n)
                wait_all(vector, c.st)
                c.v.inc(vector.tensor_mul(out=ymt[:, dc % 2, :], in0=yt[:, dc, :],
                                          in1=sgs[:, dc % 2, :]))
            @c.block.gpsimd
            def _(gpsimd, t0=t0, dc=dc):
                wait_all(gpsimd, c.v)
                gpsimd.dma_start(out=r.ym_o[dc * P:(dc + 1) * P, t0:t0 + TW],
                                 in_=ymt[:, dc % 2, :]).then_inc(c.st.h, 16)
                c.st.n += 16


def build_l2():
    nc = bass.Bass("TRN2", target_bir_lowering=False)
    r = _R()
    r.dt_i = nc.dram_tensor("dt_i", [8 * P, T], F32, kind="ExternalInput")
    r.xs_i = nc.dram_tensor("xs_i", [8 * P, T], F32, kind="ExternalInput")
    r.sg_i = nc.dram_tensor("sg_i", [8 * P, T], F32, kind="ExternalInput")
    r.bc_i = nc.dram_tensor("bc_i", [32, T], F32, kind="ExternalInput")
    r.alog = nc.dram_tensor("alog", [P, 8, 16], F32, kind="ExternalInput")
    r.dvec = nc.dram_tensor("dvec", [P, 8], F32, kind="ExternalInput")
    r.ym_o = nc.dram_tensor("ym_o", [8 * P, T], BF16, kind="ExternalOutput")

    import contextlib
    _es = contextlib.ExitStack()
    at = _es.enter_context(nc.sbuf_tensor("at", [P, 8, 16], F32))
    dvt = _es.enter_context(nc.sbuf_tensor("dvt", [P, 8], F32))
    carry = _es.enter_context(nc.sbuf_tensor("carry", [P, P], F32))
    dtt = _es.enter_context(nc.sbuf_tensor("dtt", [P, 8, TW], F32))
    ut = _es.enter_context(nc.sbuf_tensor("ut", [P, 8, TW], F32))
    yt = _es.enter_context(nc.sbuf_tensor("yt", [P, 8, TW], F32))
    xstg = _es.enter_context(nc.sbuf_tensor("xstg", [P, 2, TW], F32))
    sgs = _es.enter_context(nc.sbuf_tensor("sgs", [P, 2, TW], F32))
    bb = _es.enter_context(nc.sbuf_tensor("bb", [P, 2, TW], F32))
    cb = _es.enter_context(nc.sbuf_tensor("cb", [P, 2, TW], F32))
    dat = _es.enter_context(nc.sbuf_tensor("dat", [P, 2, TW], F32))
    vt = _es.enter_context(nc.sbuf_tensor("vt", [P, TW], F32))
    ht = _es.enter_context(nc.sbuf_tensor("ht", [P, TW], F32))
    pt = _es.enter_context(nc.sbuf_tensor("pt", [P, TW], F32))
    ymt = _es.enter_context(nc.sbuf_tensor("ymt", [P, 2, TW], BF16))
    _s_ld = _es.enter_context(nc.semaphore("s_ld"))
    _s_st = _es.enter_context(nc.semaphore("s_st"))
    _s_mm = _es.enter_context(nc.semaphore("s_mm"))
    _s_v = _es.enter_context(nc.semaphore("s_v"))
    _s_s = _es.enter_context(nc.semaphore("s_s"))
    _qs = [_es.enter_context(nc.semaphore(f"s_q{i}")) for i in range(4)]
    block = _es.enter_context(nc.Block())
    if True:
        c = Ctx(nc, block, [Sem(_s_ld), Sem(_s_st), Sem(_s_mm), Sem(_s_v), Sem(_s_s)],
                [Sem(q) for q in _qs])
        r.at, r.dvt, r.carry = at, dvt, carry
        r.dtt, r.ut, r.yt = dtt, ut, yt
        r.xstg, r.sgs, r.bb, r.cb = xstg, sgs, bb, cb
        r.dat, r.vt, r.ht, r.pt, r.ymt = dat, vt, ht, pt, ymt
        _l2_prologue(c, r)
        _l2_body(c, r)
        @block.gpsimd
        def _(gpsimd):
            gpsimd.wait_ge(c.st.h, c.st.n)
    _es.close()
    return nc


def _l2_inputs(inputs, dtg, xsg, sgg, bcg):
    alog_full = np.asarray(inputs["L0_attn_A_log"], np.float32)
    d_full = np.asarray(inputs["L0_attn_D"], np.float32)
    maps = []
    for c in range(8):
        h = c // 4
        b = (c // 2) % 2
        dh = c % 2
        sl = slice(dh * 1024, dh * 1024 + 1024)
        maps.append({
            "dt_i": np.ascontiguousarray(dtg[h, b, sl]),
            "xs_i": np.ascontiguousarray(xsg[h, b, sl]),
            "sg_i": np.ascontiguousarray(sgg[h, b, sl]),
            "bc_i": np.ascontiguousarray(bcg[h, b]),
            "alog": np.ascontiguousarray(
                alog_full[h, sl].reshape(8, P, 16).transpose(1, 0, 2)),
            "dvec": np.ascontiguousarray(d_full[h, sl].reshape(8, P).T),
        })
    return maps


def run_l2(inputs, dtg, xsg, sgg, bcg):
    nc = build_l2()
    maps = _l2_inputs(inputs, dtg, xsg, sgg, bcg)
    res = run_bass_kernel_spmd(nc, maps, core_ids=list(range(8)))
    ymg = np.empty((NB, NI, T), BF)
    for c in range(8):
        h = c // 4
        b = (c // 2) % 2
        dh = c % 2
        ymg[b, h * HD + dh * 1024: h * HD + dh * 1024 + 1024] = res.results[c]["ym_o"]
    return ymg



TS = 512


def _bfly(c, vector, tiles, tiles2, nchunks):
    import math
    nst = int(math.log2(nchunks))
    assert nst % 2 == 0
    bufs = (tiles, tiles2)
    h = 1
    s = 0
    while h < nchunks:
        src_t, dst_t = bufs[s % 2], bufs[(s + 1) % 2]
        for j in range(0, nchunks, 2 * h):
            for k in range(j, j + h):
                a, b = src_t(k), src_t(k + h)
                vector.tensor_add(out=dst_t(k), in0=a, in1=b)
                vector.tensor_sub(out=dst_t(k + h), in0=a, in1=b)
        h *= 2
        s += 1


def _l3_transform(c, r):
    nc = c.nc
    import contextlib
    with contextlib.ExitStack() as es:
        wstg = es.enter_context(nc.sbuf_tensor("t_wstg", [P, 16, TS], F32))
        ps0, ps1 = r.ps0, r.ps1
        c.barrier()

        es_ug = contextlib.ExitStack()
        wb = es_ug.enter_context(nc.sbuf_tensor("t_wb", [P, 16, TS], BF16))
        wb2 = es_ug.enter_context(nc.sbuf_tensor("t_wb2", [P, 16, TS], BF16))
        for wsrc, wdst, sucol in ((r.wupT, r.wup2, 0), (r.wgT2, r.wg2, 1)):
            for mc in range(16):
                @c.block.sync
                def _(sync, wsrc=wsrc, mc=mc):
                    wait_all(sync, c.v, c.st)
                    sync.dma_start(
                        out=wstg[:, :, :],
                        in_=wsrc[:, mc * TS:(mc + 1) * TS]
                            .rearrange("(k p) m -> p k m", p=P)).then_inc(c.ld.h, 16)
                    c.ld.n += 16
                @c.block.vector
                def _(vector):
                    wait_all(vector, c.ld, c.mm)
                    c.v.inc(vector.tensor_copy(out=wb[:, :, :], in_=wstg[:, :, :]))
                for fc in range(16):
                    pp = ps0 if fc % 2 == 0 else ps1
                    @c.block.tensor
                    def _(tensor, fc=fc, pp=pp):
                        wait_all(tensor, c.v)
                        c.mm.inc(tensor.matmul(out=pp[:, :], lhsT=r.h128b[:, :],
                                               rhs=wb[:, fc, :], start=True, stop=True))
                    @c.block.vector
                    def _(vector, fc=fc, pp=pp):
                        wait_all(vector, c.mm)
                        c.v.inc(vector.tensor_copy(out=wb[:, fc, :], in_=pp[:, :]))
                @c.block.vector
                def _(vector, sucol=sucol):
                    wait_all(vector, c.mm)
                    _bfly(c, vector, lambda i: wb[:, i, :], lambda i: wb2[:, i, :], 16)
                    for fc in range(16):
                        ins = vector.tensor_scalar_mul(out=wb[:, fc, :], in0=wb[:, fc, :],
                                                       scalar1=r.suu2[:, fc:fc + 1])
                    c.v.inc(ins)
                @c.block.gpsimd
                def _(gpsimd, wdst=wdst, mc=mc):
                    wait_all(gpsimd, c.v)
                    gpsimd.dma_start(
                        out=wdst[:, mc * TS:(mc + 1) * TS]
                            .rearrange("(k p) m -> p k m", p=P),
                        in_=wb[:, :, :]).then_inc(c.st.h, 16)
                    c.st.n += 16

        es_ug.close()
        for mc in range(4):
            with contextlib.ExitStack() as es2:
                wd64 = es2.enter_context(
                    nc.sbuf_tensor(f"t_wd64_{mc}", [P, 64, TS], BF16))
                wd64b = es2.enter_context(
                    nc.sbuf_tensor(f"t_wd64b_{mc}", [P, 64, TS], BF16))
                for qq in range(4):
                    @c.block.sync
                    def _(sync, mc=mc, qq=qq):
                        wait_all(sync, c.v, c.st)
                        sync.dma_start(
                            out=wstg[:, :, :],
                            in_=r.wdT[qq * 2048:(qq + 1) * 2048, mc * TS:(mc + 1) * TS]
                                .rearrange("(k p) m -> p k m", p=P)).then_inc(c.ld.h, 16)
                        c.ld.n += 16
                    @c.block.vector
                    def _(vector, qq=qq):
                        wait_all(vector, c.ld)
                        c.v.inc(vector.tensor_copy(out=wd64[:, qq * 16:(qq + 1) * 16, :],
                                                   in_=wstg[:, :, :]))
                for ic in range(64):
                    pp = ps0 if ic % 2 == 0 else ps1
                    @c.block.tensor
                    def _(tensor, ic=ic, pp=pp):
                        wait_all(tensor, c.v)
                        c.mm.inc(tensor.matmul(out=pp[:, :], lhsT=r.h128b[:, :],
                                               rhs=wd64[:, ic, :], start=True, stop=True))
                    @c.block.vector
                    def _(vector, ic=ic, pp=pp):
                        wait_all(vector, c.mm)
                        c.v.inc(vector.tensor_copy(out=wd64[:, ic, :], in_=pp[:, :]))
                @c.block.vector
                def _(vector):
                    wait_all(vector, c.mm)
                    _bfly(c, vector, lambda i: wd64[:, i, :], lambda i: wd64b[:, i, :], 64)
                    for ic in range(64):
                        ins = vector.tensor_scalar_mul(out=wd64[:, ic, :],
                                                       in0=wd64[:, ic, :],
                                                       scalar1=r.sud2[:, ic:ic + 1])
                    c.v.inc(ins)
                @c.block.gpsimd
                def _(gpsimd, mc=mc):
                    wait_all(gpsimd, c.v)
                    gpsimd.dma_start(
                        out=r.wd2[:, mc * TS:(mc + 1) * TS]
                            .rearrange("(k p) m -> p k m", p=P),
                        in_=wd64[:, :, :]).then_inc(c.st.h, 16)
                    c.st.n += 16

        wob = es.enter_context(nc.sbuf_tensor("t_wob", [P, 16, TS], BF16))
        for mc in range(4):
            for qq in range(2):
                @c.block.sync
                def _(sync, mc=mc, qq=qq):
                    wait_all(sync, c.v, c.st)
                    sync.dma_start(
                        out=wstg[:, :, :],
                        in_=r.woT[qq * 2048:(qq + 1) * 2048, mc * TS:(mc + 1) * TS]
                            .rearrange("(k p) m -> p k m", p=P)).then_inc(c.ld.h, 16)
                    c.ld.n += 16
                @c.block.vector
                def _(vector):
                    wait_all(vector, c.ld, c.st)
                    c.v.inc(vector.tensor_copy(out=wob[:, :, :], in_=wstg[:, :, :]))
                @c.block.gpsimd
                def _(gpsimd, mc=mc, qq=qq):
                    wait_all(gpsimd, c.v)
                    gpsimd.dma_start(
                        out=r.wo2[qq * 2048:(qq + 1) * 2048, mc * TS:(mc + 1) * TS]
                            .rearrange("(k p) m -> p k m", p=P),
                        in_=wob[:, :, :]).then_inc(c.st.h, 16)
                    c.st.n += 16


def _l3_ln(c, r, ts, xst4, z):
    t0 = ts * TS + getattr(r, "x_off", 0)
    mm_base = c.mm.n
    s_base = c.s.n
    for fc in range(16):
        qs = c.q[fc % 2]
        @c.block.sync
        def _(sync, fc=fc, t0=t0, qs=qs):
            if fc == 0:
                wait_all(sync, c.mm, c.v)
            if fc >= 2:
                sync.wait_ge(c.mm.h, mm_base + fc - 1)
            sync.dma_start(out=xst4[:, fc % 2, :],
                           in_=r.x_i[fc * P:(fc + 1) * P, t0:t0 + TS]
                           ).then_inc(qs.h, 16)
            qs.n += 16
        @c.block.scalar
        def _(scalar, fc=fc, qs=qs):
            scalar.wait_ge(qs.h, qs.n)
            if fc == 0:
                wait_all(scalar, c.mm, c.v)
            if fc >= 2:
                scalar.wait_ge(c.mm.h, mm_base + fc - 1)
            c.s.inc(scalar.activation(out=r.sq[:, fc % 2, :], in_=xst4[:, fc % 2, :],
                                      func=ACT.Square))
        @c.block.tensor
        def _(tensor, fc=fc, qs=qs):
            tensor.wait_ge(qs.h, qs.n)
            if fc == 0:
                wait_all(tensor, c.v)
            tensor.matmul(out=r.psA[:, :], lhsT=r.ones_col[:, :],
                          rhs=xst4[:, fc % 2, :], start=(fc == 0), stop=(fc == 15))
            tensor.wait_ge(c.s.h, s_base + fc + 1)
            c.mm.inc(tensor.matmul(out=r.psB[:, :], lhsT=r.ones_col[:, :],
                                   rhs=r.sq[:, fc % 2, :],
                                   start=(fc == 0), stop=(fc == 15)))
    @c.block.vector
    def _(vector):
        wait_all(vector, c.mm)
        vector.tensor_scalar_mul(out=r.stat[:, 0, :], in0=r.psA[:, :], scalar1=1.0 / DM)
        c.v.inc(vector.tensor_scalar_mul(out=r.stat[:, 1, :], in0=r.psB[:, :],
                                         scalar1=1.0 / DM))
        vector.wait_ge(c.v.h, c.v.n)
        c.v.inc(vector.tensor_mul(out=r.stat[:, 2, :], in0=r.stat[:, 0, :],
                                  in1=r.stat[:, 0, :]))
        vector.wait_ge(c.v.h, c.v.n)
        c.v.inc(vector.tensor_sub(out=r.stat[:, 3, :], in0=r.stat[:, 1, :],
                                  in1=r.stat[:, 2, :]))
    @c.block.scalar
    def _(scalar):
        wait_all(scalar, c.v)
        scalar.activation(out=r.stat[:, 4, :], in_=r.stat[:, 3, :], func=ACT.Ln,
                          bias=r.consts[:, 0:1])
        c.s.inc(scalar.activation(out=r.stat[:, 5, :], in_=r.stat[:, 4, :],
                                  func=ACT.Exp, scale=r.consts[:, 1:2]))
    @c.block.tensor
    def _(tensor):
        wait_all(tensor, c.s)
        tensor.matmul(out=r.psM[:, :], lhsT=r.ones_row[:, :], rhs=r.stat[:, 0, :],
                      start=True, stop=True)
        c.mm.inc(tensor.matmul(out=r.psR[:, :], lhsT=r.ones_row[:, :],
                               rhs=r.stat[:, 5, :], start=True, stop=True))
    v_base = c.v.n
    for fc in range(16):
        qs = c.q[fc % 2]
        @c.block.sync
        def _(sync, fc=fc, t0=t0, qs=qs):
            if fc == 0:
                wait_all(sync, c.mm)
            if fc >= 2:
                sync.wait_ge(c.v.h, v_base + fc - 1)
            sync.dma_start(out=xst4[:, fc % 2, :],
                           in_=r.x_i[fc * P:(fc + 1) * P, t0:t0 + TS]
                           ).then_inc(qs.h, 16)
            qs.n += 16
        @c.block.vector
        def _(vector, fc=fc, qs=qs):
            vector.wait_ge(qs.h, qs.n)
            if fc == 0:
                wait_all(vector, c.mm)
                if DBG:
                    wait_all(vector, c.st)
            vector.tensor_sub(out=r.sq[:, fc % 2, :], in0=xst4[:, fc % 2, :],
                              in1=r.psM[:, :])
            c.v.inc(vector.tensor_tensor(out=z[:, fc, :], in0=r.sq[:, fc % 2, :],
                                         in1=r.psR[:, :], op=AL.mult))


def _l3_main(c, r):
    nc = c.nc
    ps0, ps1, ps2, ps3 = r.ps0, r.ps1, r.ps2, r.ps3
    with (
        nc.sbuf_tensor("xst4", [P, 2, TS], F32) as xst4,
        nc.sbuf_tensor("f2", [P, 64, TS], BF16) as f2,
        nc.sbuf_tensor("gbuf", [P, 16, TS], BF16) as gbuf,
        nc.sbuf_tensor("wu_s", [P, 2, 16, P], BF16) as wu_s,
        nc.sbuf_tensor("wg_s", [P, 2, 16, P], BF16) as wg_s,
        nc.sbuf_tensor("wd_s", [P, 2, 64, P], BF16) as wd_s,
        nc.sbuf_tensor("wo_s", [P, 2, 32, P], BF16) as wo_s,
        nc.sbuf_tensor("gs", [P, TS], BF16) as gs,
        nc.sbuf_tensor("xres", [P, 2, TS], F32) as xres,
        nc.sbuf_tensor("sout", [P, 2, TS], F32) as sout,
    ):
        for ts in range(TPC // TS):
            t0 = ts * TS
            import contextlib as _ctx
            _zs = _ctx.ExitStack()
            z = _zs.enter_context(nc.sbuf_tensor(f"z{id(r) % 97}_{ts}", [P, 16, TS], BF16))
            _l3_ln(c, r, ts, xst4, z)
            if DBG and getattr(r, "z_o", None) is not None:
                @c.block.gpsimd
                def _(gpsimd, t0=t0, z=z):
                    wait_all(gpsimd, c.v)
                    for fc in range(16):
                        gpsimd.dma_start(out=r.z_o[fc * P:(fc + 1) * P, t0:t0 + TS],
                                         in_=z[:, fc, :]).then_inc(c.st.h, 16)
                    c.st.n += 16 * 16
            mm_ug = c.mm.n
            for ic in range(64):
                ws = ic % 2
                qs = c.q[ws]
                psU = ps0 if ic % 2 == 0 else ps1
                psG = ps2 if ic % 2 == 0 else ps3
                @c.block.sync
                def _(sync, ic=ic, ws=ws, qs=qs, mm_ug=mm_ug):
                    if ic >= 2:
                        sync.wait_ge(c.mm.h, mm_ug + ic - 1)
                    else:
                        wait_all(sync, c.mm)
                    sync.dma_start(
                        out=wu_s[:, ws, :, :],
                        in_=r.wup2[:, ic * P:(ic + 1) * P]
                            .rearrange("(k p) m -> p k m", p=P)).then_inc(qs.h, 16)
                    sync.dma_start(
                        out=wg_s[:, ws, :, :],
                        in_=r.wg2[:, ic * P:(ic + 1) * P]
                            .rearrange("(k p) m -> p k m", p=P)).then_inc(qs.h, 16)
                    qs.n += 32
                @c.block.tensor
                def _(tensor, ic=ic, ws=ws, psU=psU, psG=psG, qs=qs):
                    tensor.wait_ge(qs.h, qs.n)
                    wait_all(tensor, c.v, c.s)
                    for kc in range(16):
                        tensor.matmul(out=psU[:, :], lhsT=wu_s[:, ws, kc, :],
                                      rhs=z[:, kc, :], start=(kc == 0),
                                      stop=(kc == 15))
                    for kc in range(16):
                        ins = tensor.matmul(out=psG[:, :], lhsT=wg_s[:, ws, kc, :],
                                            rhs=z[:, kc, :], start=(kc == 0),
                                            stop=(kc == 15))
                    c.mm.inc(ins)
                @c.block.scalar
                def _(scalar, ic=ic, psG=psG):
                    wait_all(scalar, c.mm, c.v)
                    c.s.inc(scalar.activation(out=gs[:, :], in_=psG[:, :],
                                              func=ACT.Silu,
                                              scale=r.wsg2[:, ic:ic + 1]))
                @c.block.vector
                def _(vector, ic=ic, psU=psU):
                    wait_all(vector, c.s)
                    if DBG and ic == 0:
                        wait_all(vector, c.st)
                    c.v.inc(vector.scalar_tensor_tensor(
                        out=f2[:, ic, :], in0=psU[:, :],
                        scalar=r.wsu2[:, ic:ic + 1], in1=gs[:, :],
                        op0=AL.mult, op1=AL.mult))
            mm_dn = c.mm.n
            for oc in range(16):
                ws = oc % 2
                qs = c.q[2 + ws]
                pp = ps0 if oc % 2 == 0 else ps1
                @c.block.sync
                def _(sync, oc=oc, ws=ws, qs=qs, mm_dn=mm_dn):
                    if oc >= 2:
                        sync.wait_ge(c.mm.h, mm_dn + oc - 1)
                    else:
                        wait_all(sync, c.mm)
                    sync.dma_start(
                        out=wd_s[:, ws, :, :],
                        in_=r.wd2[:, oc * P:(oc + 1) * P]
                            .rearrange("(k p) m -> p k m", p=P)).then_inc(qs.h, 16)
                    qs.n += 16
                @c.block.tensor
                def _(tensor, oc=oc, ws=ws, pp=pp, qs=qs):
                    tensor.wait_ge(qs.h, qs.n)
                    wait_all(tensor, c.s)
                    if oc == 0:
                        wait_all(tensor, c.v)
                    for ic in range(64):
                        ins = tensor.matmul(out=pp[:, :], lhsT=wd_s[:, ws, ic, :],
                                            rhs=f2[:, ic, :], start=(ic == 0),
                                            stop=(ic == 63))
                    c.mm.inc(ins)
                @c.block.scalar
                def _(scalar, oc=oc, pp=pp):
                    wait_all(scalar, c.mm)
                    c.s.inc(scalar.activation(out=gbuf[:, oc, :], in_=pp[:, :],
                                              func=ACT.Silu,
                                              scale=r.wsd2[:, oc:oc + 1]))
            if DBG and getattr(r, "g_o", None) is not None:
                @c.block.gpsimd
                def _(gpsimd, t0=t0):
                    wait_all(gpsimd, c.s, c.v)
                    for oc in range(16):
                        gpsimd.dma_start(out=r.g_o[oc * P:(oc + 1) * P, t0:t0 + TS],
                                         in_=gbuf[:, oc, :]).then_inc(c.st.h, 16)
                    for ic in range(64):
                        gpsimd.dma_start(out=r.f2_o[ic * P:(ic + 1) * P, t0:t0 + TS],
                                         in_=f2[:, ic, :]).then_inc(c.st.h, 16)
                    c.st.n += 80 * 16
            _zs.close()
            _ys = _ctx.ExitStack()
            ymst = _ys.enter_context(
                nc.sbuf_tensor(f"ym{id(r) % 97}_{ts}", [P, 32, TS], BF16))
            if getattr(r, "ym_reg", None) is not None:
                @c.block.gpsimd
                def _(gpsimd, t0=t0):
                    wait_all(gpsimd, c.mm)
                    rq, rtmp = r.ym_reg
                    for icc in range(32):
                        gpsimd.reg_add(rtmp, rq, icc * P * T + (t0 - getattr(r, "x_off", 0)))
                        gpsimd.dma_start(out=ymst[:, icc, :],
                                         in_=AP(r.ym_i, rtmp, [[T, P], [1, TS]])
                                         ).then_inc(c.ld.h, 16)
                    c.ld.n += 32 * 16
            else:
                @c.block.sync
                def _(sync, t0=t0):
                    wait_all(sync, c.mm)
                    for icc in range(32):
                        sync.dma_start(out=ymst[:, icc, :],
                                       in_=r.ym_i[icc * P:(icc + 1) * P, t0:t0 + TS]
                                       ).then_inc(c.ld.h, 16)
                    c.ld.n += 32 * 16
            mm_op = c.mm.n
            snapo = {}
            for oc in range(16):
                ws = oc % 2
                qs = c.q[ws]
                pp = ps2 if oc % 2 == 0 else ps3
                @c.block.sync
                def _(sync, oc=oc, ws=ws, t0=t0, qs=qs, mm_op=mm_op):
                    if oc >= 2:
                        sync.wait_ge(c.mm.h, mm_op + oc - 1)
                        sync.wait_ge(c.v.h, snapo[oc - 2])
                    else:
                        wait_all(sync, c.mm, c.v)
                    sync.dma_start(
                        out=wo_s[:, ws, :, :],
                        in_=r.wo2[:, oc * P:(oc + 1) * P]
                            .rearrange("(k p) m -> p k m", p=P)).then_inc(qs.h, 16)
                    sync.dma_start(out=xres[:, ws, :],
                                   in_=r.x_i[oc * P:(oc + 1) * P, t0:t0 + TS]
                                   ).then_inc(qs.h, 16)
                    qs.n += 32
                @c.block.tensor
                def _(tensor, oc=oc, ws=ws, pp=pp, qs=qs):
                    tensor.wait_ge(qs.h, qs.n)
                    if oc - 2 in snapo:
                        tensor.wait_ge(c.v.h, snapo[oc - 2])
                    else:
                        wait_all(tensor, c.v)
                    for icc in range(32):
                        ins = tensor.matmul(out=pp[:, :], lhsT=wo_s[:, ws, icc, :],
                                            rhs=ymst[:, icc, :], start=(icc == 0),
                                            stop=(icc == 31))
                    c.mm.inc(ins)
                @c.block.vector
                def _(vector, oc=oc, ws=ws, pp=pp):
                    wait_all(vector, c.mm, c.st)
                    if oc == 0:
                        wait_all(vector, c.s)
                    vector.tensor_mul(out=sout[:, ws, :], in0=pp[:, :],
                                      in1=gbuf[:, oc, :])
                    c.v.inc(vector.tensor_add(out=sout[:, ws, :],
                                              in0=sout[:, ws, :],
                                              in1=xres[:, ws, :]))
                snapo[oc] = c.v.n
                @c.block.gpsimd
                def _(gpsimd, oc=oc, ws=ws, t0=t0):
                    wait_all(gpsimd, c.v)
                    gpsimd.dma_start(out=r.out_o[oc * P:(oc + 1) * P, t0:t0 + TS],
                                     in_=sout[:, ws, :]).then_inc(c.st.h, 16)
                    c.st.n += 16
            _ys.close()



def build_l3():
    nc = bass.Bass("TRN2", target_bir_lowering=False)
    r = _R()
    r.x_i = nc.dram_tensor("x_i", [DM, TPC], F32, kind="ExternalInput")
    r.ym_i = nc.dram_tensor("ym_i", [NI, TPC], BF16, kind="ExternalInput")
    r.woT = nc.dram_tensor("woT", [NI, DM], F32, kind="ExternalInput")
    r.wupT = nc.dram_tensor("wupT", [DM, 8192], F32, kind="ExternalInput")
    r.wgT2 = nc.dram_tensor("wgT2", [DM, 8192], F32, kind="ExternalInput")
    r.wdT = nc.dram_tensor("wdT", [8192, DM], F32, kind="ExternalInput")
    r.suu = nc.dram_tensor("suu", [P, 16], F32, kind="ExternalInput")
    r.sud = nc.dram_tensor("sud", [P, 64], F32, kind="ExternalInput")
    r.wsu = nc.dram_tensor("wsu", [P, 64], F32, kind="ExternalInput")
    r.wsg = nc.dram_tensor("wsg", [P, 64], F32, kind="ExternalInput")
    r.wsd = nc.dram_tensor("wsd", [P, 16], F32, kind="ExternalInput")
    r.h128 = nc.dram_tensor("h128", [P, P], F32, kind="ExternalInput")
    r.out_o = nc.dram_tensor("out_o", [DM, TPC], F32, kind="ExternalOutput")
    if DBG:
        r.z_o = nc.dram_tensor("z_o", [DM, TPC], BF16, kind="ExternalOutput")
        r.g_o = nc.dram_tensor("g_o", [DM, TPC], BF16, kind="ExternalOutput")
        r.f2_o = nc.dram_tensor("f2_o", [8192, TPC], BF16, kind="ExternalOutput")
    r.wup2 = nc.dram_tensor("wup2", [DM, 8192], BF16)
    r.wg2 = nc.dram_tensor("wg2", [DM, 8192], BF16)
    r.wd2 = nc.dram_tensor("wd2", [8192, DM], BF16)
    r.wo2 = nc.dram_tensor("wo2", [NI, DM], BF16)
    if DBG:
        r.xs_d = nc.dram_tensor("xs_d", [1024, T], F32, kind="ExternalOutput")
        r.sg_d = nc.dram_tensor("sg_d", [1024, T], F32, kind="ExternalOutput")
        r.ssm_d = nc.dram_tensor("ssm_d", [160, T], F32, kind="ExternalOutput")
        r.dt_d = nc.dram_tensor("dt_d", [1024, T], F32, kind="ExternalOutput")
        r.ym_d = nc.dram_tensor("ym_d", [1024, T], BF16, kind="ExternalOutput")
        r.at_d = nc.dram_tensor("at_d", [P, 128], F32, kind="ExternalOutput")
        r.da_d = nc.dram_tensor("da_d", [P, 2048], F32, kind="ExternalOutput")
        r.vt_d = nc.dram_tensor("vt_d", [P, 5 * 1024], F32, kind="ExternalOutput")
        r.ymq_d = nc.dram_tensor("ymq_d", [NI, TPC], BF16, kind="ExternalOutput")

    import contextlib
    _es = contextlib.ExitStack()
    r.ones_col = _es.enter_context(nc.sbuf_tensor("ones_col", [P, 1], F32))
    r.ones_row = _es.enter_context(nc.sbuf_tensor("ones_row", [1, P], F32))
    r.consts = _es.enter_context(nc.sbuf_tensor("consts", [1, 4], F32))
    r.h128f = _es.enter_context(nc.sbuf_tensor("h128f", [P, P], F32))
    r.h128b = _es.enter_context(nc.sbuf_tensor("h128b", [P, P], BF16))
    r.suu2 = _es.enter_context(nc.sbuf_tensor("suu2", [P, 16], F32))
    r.sud2 = _es.enter_context(nc.sbuf_tensor("sud2", [P, 64], F32))
    r.wsu2 = _es.enter_context(nc.sbuf_tensor("wsu2", [P, 64], F32))
    r.wsg2 = _es.enter_context(nc.sbuf_tensor("wsg2", [P, 64], F32))
    r.wsd2 = _es.enter_context(nc.sbuf_tensor("wsd2", [P, 16], F32))
    r.sq = _es.enter_context(nc.sbuf_tensor("sq", [P, 2, TS], F32))
    r.stat = _es.enter_context(nc.sbuf_tensor("stat", [1, 6, TS], F32))
    r.psA = _es.enter_context(nc.psum_tensor("psA", [1, TS], F32))
    r.psB = _es.enter_context(nc.psum_tensor("psB", [1, TS], F32))
    r.psM = _es.enter_context(nc.psum_tensor("psM", [P, TS], F32))
    r.psR = _es.enter_context(nc.psum_tensor("psR", [P, TS], F32))
    ps0 = _es.enter_context(nc.psum_tensor("ps0", [P, TS], F32))
    ps1 = _es.enter_context(nc.psum_tensor("ps1", [P, TS], F32))
    ps2 = _es.enter_context(nc.psum_tensor("ps2", [P, TS], F32))
    ps3 = _es.enter_context(nc.psum_tensor("ps3", [P, TS], F32))
    r.ps0, r.ps1, r.ps2, r.ps3 = ps0, ps1, ps2, ps3
    _s_ld = _es.enter_context(nc.semaphore("s_ld"))
    _s_st = _es.enter_context(nc.semaphore("s_st"))
    _s_mm = _es.enter_context(nc.semaphore("s_mm"))
    _s_v = _es.enter_context(nc.semaphore("s_v"))
    _s_s = _es.enter_context(nc.semaphore("s_s"))
    _qs = [_es.enter_context(nc.semaphore(f"s_q{i}")) for i in range(4)]
    block = _es.enter_context(nc.Block())
    if True:
        c = Ctx(nc, block, [Sem(_s_ld), Sem(_s_st), Sem(_s_mm), Sem(_s_v), Sem(_s_s)],
                [Sem(q) for q in _qs])

        @block.sync
        def _(sync):
            sync.dma_start(out=r.h128f[:, :], in_=r.h128[:, :]).then_inc(c.ld.h, 16)
            sync.dma_start(out=r.suu2[:, :], in_=r.suu[:, :]).then_inc(c.ld.h, 16)
            sync.dma_start(out=r.sud2[:, :], in_=r.sud[:, :]).then_inc(c.ld.h, 16)
            sync.dma_start(out=r.wsu2[:, :], in_=r.wsu[:, :]).then_inc(c.ld.h, 16)
            sync.dma_start(out=r.wsg2[:, :], in_=r.wsg[:, :]).then_inc(c.ld.h, 16)
            sync.dma_start(out=r.wsd2[:, :], in_=r.wsd[:, :]).then_inc(c.ld.h, 16)
            c.ld.n += 6 * 16

        @block.vector
        def _(vector):
            vector.memset(r.ones_col[:, :], 1.0)
            vector.memset(r.ones_row[:, :], 1.0)
            vector.memset(r.consts[:, 0:1], LN_EPS)
            vector.memset(r.consts[:, 1:2], -0.5)
            wait_all(vector, c.ld)
            vector.tensor_copy(out=r.h128b[:, :], in_=r.h128f[:, :])
            vector.tensor_scalar_mul(out=r.suu2[:, :], in0=r.suu2[:, :],
                                     scalar1=1.0 / (64.0 * float(np.sqrt(2048.0))))
            vector.tensor_scalar_mul(out=r.sud2[:, :], in0=r.sud2[:, :],
                                     scalar1=1.0 / (64.0 * float(np.sqrt(8192.0))))
            vector.tensor_scalar_mul(out=r.wsu2[:, :], in0=r.wsu2[:, :], scalar1=64.0)
            vector.tensor_scalar_mul(out=r.wsg2[:, :], in0=r.wsg2[:, :], scalar1=64.0)
            c.v.inc(vector.tensor_scalar_mul(out=r.wsd2[:, :], in0=r.wsd2[:, :],
                                             scalar1=64.0))
            vector.wait_ge(c.v.h, c.v.n)

        _l3_transform(c, r)
        c.barrier()

        _l3_main(c, r)

        @block.gpsimd
        def _(gpsimd):
            gpsimd.wait_ge(c.st.h, c.st.n)
    _es.close()
    return nc


def _hadamard(n):
    h = np.array([[1.0]], np.float32)
    while h.shape[0] < n:
        h = np.block([[h, h], [h, -h]])
    return h.astype(np.float32)


def _l3_inputs(inputs, xT, ymg):
    woT = np.ascontiguousarray(np.asarray(inputs["L0_attn_out_proj_w"], np.float32).T)
    wupT = np.ascontiguousarray(np.asarray(inputs["L0_ffn_W_up"], np.float32).T)
    wgT2 = np.ascontiguousarray(np.asarray(inputs["L0_ffn_W_gate"], np.float32).T)
    wdT = np.ascontiguousarray(np.asarray(inputs["L0_ffn_W_down"], np.float32).T)
    suu = np.ascontiguousarray(
        np.asarray(inputs["L0_ffn_SU_ug"], np.float32).reshape(16, P).T)
    sud = np.ascontiguousarray(
        np.asarray(inputs["L0_ffn_SU_dp"], np.float32).reshape(64, P).T)
    wsall = np.asarray(inputs["L0_ffn_Wscale_ug"], np.float32)
    wsu = np.ascontiguousarray(wsall[:8192].reshape(64, P).T)
    wsg = np.ascontiguousarray(wsall[8192:].reshape(64, P).T)
    wsd = np.ascontiguousarray(
        np.asarray(inputs["L0_ffn_Wscale_dp"], np.float32).reshape(16, P).T)
    h128 = _hadamard(P)
    maps = []
    for cc in range(8):
        b = (cc // 2) % 2
        q = 2 * (cc // 4) + (cc % 2)
        s0 = q * TPC
        maps.append({
            "x_i": np.ascontiguousarray(xT[b, :, s0:s0 + TPC]),
            "ym_i": np.ascontiguousarray(ymg[b, :, s0:s0 + TPC]),
            "woT": woT, "wupT": wupT, "wgT2": wgT2, "wdT": wdT,
            "suu": suu, "sud": sud, "wsu": wsu, "wsg": wsg, "wsd": wsd,
            "h128": h128,
        })
    return maps


def run_l3(inputs, xT, ymg):
    nc = build_l3()
    maps = _l3_inputs(inputs, xT, ymg)
    res = run_bass_kernel_spmd(nc, maps, core_ids=list(range(8)))
    out = np.empty((NB, DM, T), np.float32)
    for cc in range(8):
        b = (cc // 2) % 2
        q = 2 * (cc // 4) + (cc % 2)
        s0 = q * TPC
        out[b, :, s0:s0 + TPC] = res.results[cc]["out_o"]
    return out



def kernel_v1(**inputs):
    xT = np.ascontiguousarray(
        np.asarray(inputs["L0_attn_hidden_states"], np.float32).transpose(0, 2, 1))
    dtg, xsg, sgg, bcg = run_l1(inputs, xT)
    ymg = run_l2(inputs, dtg, xsg, sgg, bcg)
    out = run_l3(inputs, xT, ymg)
    return np.ascontiguousarray(out.transpose(0, 2, 1))



XPAD = T + 8


def _f_ln_chunk(c, r, k, xnb, xstage, sq, stat):
    base = k * HT
    @c.block.vector
    def _(vector):
        wait_all(vector, c.mm, c.s)
        c.v.inc(vector.memset(xnb[:, :, 4 * TT + 3:], 0.0))
    for (c0, w) in L1_TILES:
        @c.block.sync
        def _(sync, c0=c0, w=w):
            wait_all(sync, c.mm, c.v)
            for fc in range(16):
                sync.dma_start(out=xstage[:, fc, :w],
                               in_=r.x_pad[fc * P:(fc + 1) * P, base + c0:base + c0 + w]
                               ).then_inc(c.ld.h, 16)
            c.ld.n += 16 * 16
        @c.block.tensor
        def _(tensor, w=w):
            wait_all(tensor, c.ld, c.v)
            for fc in range(16):
                tensor.matmul(out=r.psA[:, :w], lhsT=r.ones_col[:, :],
                              rhs=xstage[:, fc, :w], start=(fc == 0), stop=(fc == 15))
        mm_base = c.mm.n
        s_base = c.s.n
        for fc in range(16):
            @c.block.scalar
            def _(scalar, w=w, fc=fc):
                if fc == 0:
                    wait_all(scalar, c.ld, c.mm, c.v)
                if fc >= 2:
                    scalar.wait_ge(c.mm.h, mm_base + fc - 1)
                c.s.inc(scalar.activation(out=sq[:, fc % 2, :w],
                                          in_=xstage[:, fc, :w], func=ACT.Square))
            @c.block.tensor
            def _(tensor, w=w, fc=fc):
                tensor.wait_ge(c.s.h, s_base + fc + 1)
                c.mm.inc(tensor.matmul(out=r.psB[:, :w], lhsT=r.ones_col[:, :],
                                       rhs=sq[:, fc % 2, :w],
                                       start=(fc == 0), stop=(fc == 15)))
        @c.block.vector
        def _(vector, w=w):
            wait_all(vector, c.mm)
            vector.tensor_scalar_mul(out=stat[:, 0, :w], in0=r.psA[:, :w],
                                     scalar1=1.0 / DM)
            c.v.inc(vector.tensor_scalar_mul(out=stat[:, 1, :w], in0=r.psB[:, :w],
                                             scalar1=1.0 / DM))
            vector.wait_ge(c.v.h, c.v.n)
            c.v.inc(vector.tensor_mul(out=stat[:, 2, :w], in0=stat[:, 0, :w],
                                      in1=stat[:, 0, :w]))
            vector.wait_ge(c.v.h, c.v.n)
            c.v.inc(vector.tensor_sub(out=stat[:, 3, :w], in0=stat[:, 1, :w],
                                      in1=stat[:, 2, :w]))
        @c.block.scalar
        def _(scalar, w=w):
            wait_all(scalar, c.v)
            scalar.activation(out=stat[:, 4, :w], in_=stat[:, 3, :w],
                              func=ACT.Ln, bias=r.consts[:, 0:1])
            c.s.inc(scalar.activation(out=stat[:, 5, :w], in_=stat[:, 4, :w],
                                      func=ACT.Exp, scale=r.consts[:, 1:2]))
        @c.block.tensor
        def _(tensor, w=w):
            wait_all(tensor, c.s)
            tensor.matmul(out=r.psM[:, :w], lhsT=r.ones_row[:, :],
                          rhs=stat[:, 0, :w], start=True, stop=True)
            c.mm.inc(tensor.matmul(out=r.psR[:, :w], lhsT=r.ones_row[:, :],
                                   rhs=stat[:, 5, :w], start=True, stop=True))
        @c.block.vector
        def _(vector, c0=c0, w=w):
            wait_all(vector, c.mm)
            for fc in range(16):
                vector.tensor_sub(out=xstage[:, fc, :w], in0=xstage[:, fc, :w],
                                  in1=r.psM[:, :w])
            for fc in range(16):
                ins = vector.tensor_tensor(out=xnb[:, fc, c0:c0 + w],
                                           in0=xstage[:, fc, :w],
                                           in1=r.psR[:, :w], op=AL.mult)
            c.v.inc(ins)


def _f_inproj_chunk(c, r, k, xnb, bufs):
    xbuf, cs, xst, xst16, sgt = bufs
    base = k * HT
    snap_v = {}
    snap_s = {}
    ln_v = c.v.n
    halo_v = {}
    for mc in range(16):
        is_x = mc < 8
        wmc = mc
        ch = (mc if is_x else mc - 8) * P
        if is_x:
            @c.block.tensor
            def _(tensor, wmc=wmc, mc=mc):
                tensor.wait_ge(c.v.h, max(ln_v, halo_v.get(mc - 1, 0)))
                for kc in range(16):
                    ins = tensor.matmul(out=r.ps2[:, 0:3], lhsT=r.wres[:, kc, wmc, :],
                                        rhs=xnb[:, kc, 0:3],
                                        start=(kc == 0), stop=(kc == 15))
                c.mm.inc(ins)
            @c.block.vector
            def _(vector):
                wait_all(vector, c.mm)
                c.v.inc(vector.tensor_copy(out=xbuf[:, 0:3], in_=r.ps2[:, 0:3]))
            halo_v[mc] = c.v.n
        for ti, (c0, w) in enumerate(L1_TILES):
            li = mc * 5 + ti
            pp = r.ps0 if li % 2 == 0 else r.ps1
            @c.block.tensor
            def _(tensor, wmc=wmc, c0=c0, w=w, pp=pp, li=li):
                if li - 2 in snap_v:
                    tensor.wait_ge(c.v.h, snap_v[li - 2])
                    tensor.wait_ge(c.s.h, snap_s[li - 2])
                else:
                    wait_all(tensor, c.v, c.s)
                for kc in range(16):
                    ins = tensor.matmul(out=pp[:, :w], lhsT=r.wres[:, kc, wmc, :],
                                        rhs=xnb[:, kc, c0:c0 + w],
                                        start=(kc == 0), stop=(kc == 15))
                c.mm.inc(ins)
            so, do, wd = _store_win(ti, w)
            do_g = do + base
            if is_x:
                @c.block.vector
                def _(vector, w=w, pp=pp, ti=ti, mc=mc):
                    small = w < 256
                    def step(ins):
                        c.v.inc(ins)
                        if small:
                            vector.wait_ge(c.v.h, c.v.n)
                    wait_all(vector, c.mm, c.st)
                    if ti > 0:
                        step(vector.tensor_copy(out=xbuf[:, 0:3],
                                                in_=xbuf[:, TT:TT + 3]))
                    step(vector.tensor_copy(out=xbuf[:, 3:3 + w], in_=pp[:, :w]))
                    step(vector.tensor_scalar_mul(out=cs[:, :w], in0=xbuf[:, 0:w],
                                                  scalar1=r.cwt[:, mc, 0:1]))
                    for kk in (1, 2, 3):
                        step(vector.scalar_tensor_tensor(
                            out=cs[:, :w], in0=xbuf[:, kk:kk + w],
                            scalar=r.cwt[:, mc, kk:kk + 1], in1=cs[:, :w],
                            op0=AL.mult, op1=AL.add))
                @c.block.scalar
                def _(scalar, w=w, mc=mc):
                    wait_all(scalar, c.v)
                    c.s.inc(scalar.activation(out=xst[:, :w], in_=cs[:, :w],
                                              func=ACT.Silu, bias=r.cbt[:, mc:mc + 1]))
                @c.block.vector
                def _(vector, w=w):
                    wait_all(vector, c.s)
                    c.v.inc(vector.tensor_copy(out=xst16[:, :w], in_=xst[:, :w]))
                if wd > 0:
                    @c.block.gpsimd
                    def _(gpsimd, ch=ch, so=so, do_g=do_g, wd=wd):
                        wait_all(gpsimd, c.v)
                        gpsimd.dma_start(
                            out=r.xs_hbm[ch:ch + P, do_g:do_g + wd],
                            in_=xst[:, so:so + wd]).then_inc(c.st.h, 16)
                        gpsimd.dma_start(
                            out=r.xsb_hbm[ch:ch + P, do_g:do_g + wd],
                            in_=xst16[:, so:so + wd]).then_inc(c.st.h, 16)
                        c.st.n += 32
            else:
                @c.block.scalar
                def _(scalar, w=w, pp=pp, ti=ti):
                    wait_all(scalar, c.mm, c.st)
                    c.s.inc(scalar.activation(out=sgt[:, ti % 2, :w], in_=pp[:, :w],
                                              func=ACT.Silu))
                if wd > 0:
                    @c.block.gpsimd
                    def _(gpsimd, ch=ch, so=so, do_g=do_g, wd=wd, ti=ti):
                        wait_all(gpsimd, c.s)
                        gpsimd.dma_start(
                            out=r.sg_hbm[ch:ch + P, do_g:do_g + wd],
                            in_=sgt[:, ti % 2, so:so + wd]).then_inc(c.st.h, 16)
                        c.st.n += 16
            snap_v[li] = c.v.n
            snap_s[li] = c.s.n


def _f_xproj(c, r):
    nc = c.nc
    import contextlib
    with contextlib.ExitStack() as es:
        xpst = es.enter_context(nc.sbuf_tensor("f_xpst", [P, 8, 160], F32))
        xpsb = es.enter_context(nc.sbuf_tensor("f_xpsb", [P, 8, 160], BF16))
        xsl = es.enter_context(nc.sbuf_tensor("f_xsl", [P, 2, TT], BF16))
        spdt = es.enter_context(nc.sbuf_tensor("f_spdt", [P, 2, TT], F32))
        spbc = es.enter_context(nc.sbuf_tensor("f_spbc", [32, 2, TT], F32))
        c.barrier()
        @c.block.sync
        def _(sync):
            sync.dma_start(out=xpst[:, :, :],
                           in_=r.xpT_s.ap().rearrange("(k p) r -> p k r", p=P)
                           ).then_inc(c.ld.h, 16)
            c.ld.n += 16
        @c.block.vector
        def _(vector):
            wait_all(vector, c.ld)
            c.v.inc(vector.tensor_copy(out=xpsb[:, :, :], in_=xpst[:, :, :]))
        c.barrier()
        for tj in range(T // TT):
            t0 = tj * TT
            for dc in range(8):
                qs = c.q[dc % 2]
                @c.block.sync
                def _(sync, t0=t0, dc=dc, qs=qs):
                    wait_all(sync, c.mm)
                    sync.dma_start(out=xsl[:, dc % 2, :],
                                   in_=r.xsb_hbm[dc * P:(dc + 1) * P, t0:t0 + TT]
                                   ).then_inc(qs.h, 16)
                    qs.n += 16
                @c.block.tensor
                def _(tensor, dc=dc, qs=qs):
                    tensor.wait_ge(qs.h, qs.n)
                    if dc == 0:
                        wait_all(tensor, c.v)
                    tensor.matmul(out=r.psM[:, :], lhsT=xpsb[:, dc, 0:P],
                                  rhs=xsl[:, dc % 2, :],
                                  start=(dc == 0), stop=(dc == 7))
                    c.mm.inc(tensor.matmul(out=r.psR[0:32, :],
                                           lhsT=xpsb[:, dc, P:160],
                                           rhs=xsl[:, dc % 2, :],
                                           start=(dc == 0), stop=(dc == 7)))
            sl = tj % 2
            @c.block.vector
            def _(vector, sl=sl):
                wait_all(vector, c.mm, c.st)
                vector.tensor_copy(out=spdt[:, sl, :], in_=r.psM[:, :])
                c.v.inc(vector.tensor_copy(out=spbc[:, sl, :], in_=r.psR[0:32, :]))
            @c.block.gpsimd
            def _(gpsimd, t0=t0, sl=sl):
                wait_all(gpsimd, c.v)
                gpsimd.dma_start(out=r.ssm_part[0:P, t0:t0 + TT],
                                 in_=spdt[:, sl, :]).then_inc(c.st.h, 16)
                gpsimd.dma_start(out=r.ssm_part[P:160, t0:t0 + TT],
                                 in_=spbc[:, sl, :]).then_inc(c.st.h, 16)
                c.st.n += 32


def _f_dtproj(c, r):
    nc = c.nc
    import contextlib
    with contextlib.ExitStack() as es:
        dtwst = es.enter_context(nc.sbuf_tensor("f_dtwst", [P, 8, P], F32))
        dtwsb = es.enter_context(nc.sbuf_tensor("f_dtwsb", [P, 8, P], BF16))
        dtbt = es.enter_context(nc.sbuf_tensor("f_dtbt", [P, 8], F32))
        dtif = es.enter_context(nc.sbuf_tensor("f_dtif", [P, 2, TT], F32))
        dtin = es.enter_context(nc.sbuf_tensor("f_dtin", [P, 2, TT], BF16))
        dpre = es.enter_context(nc.sbuf_tensor("f_dpre", [P, 2, TT], F32))
        dex = es.enter_context(nc.sbuf_tensor("f_dex", [P, TT], F32))
        dtt = es.enter_context(nc.sbuf_tensor("f_dtt", [P, 2, TT], F32))
        c.barrier()
        @c.block.sync
        def _(sync):
            sync.dma_start(out=dtwst[:, :, :],
                           in_=r.dtwT_s.ap().rearrange("p (k m) -> p k m", m=P)
                           ).then_inc(c.ld.h, 16)
            sync.dma_start(out=dtbt[:, :], in_=r.dtb_s[:, :]).then_inc(c.ld.h, 16)
            c.ld.n += 32
        @c.block.vector
        def _(vector):
            wait_all(vector, c.ld)
            c.v.inc(vector.tensor_copy(out=dtwsb[:, :, :], in_=dtwst[:, :, :]))
        c.barrier()
        for tj in range(T // TT):
            t0 = tj * TT
            sl = tj % 2
            qs = c.q[2 + sl]
            @c.block.sync
            def _(sync, t0=t0, sl=sl, qs=qs):
                wait_all(sync, c.v)
                sync.dma_start(out=dtif[:, sl, :],
                               in_=r.ssm_red[0:P, t0:t0 + TT]).then_inc(qs.h, 16)
                qs.n += 16
            @c.block.vector
            def _(vector, sl=sl, qs=qs):
                vector.wait_ge(qs.h, qs.n)
                wait_all(vector, c.mm)
                c.v.inc(vector.tensor_copy(out=dtin[:, sl, :], in_=dtif[:, sl, :]))
            for oc in range(8):
                pp = r.ps0 if oc % 2 == 0 else r.ps1
                @c.block.tensor
                def _(tensor, oc=oc, sl=sl, pp=pp):
                    wait_all(tensor, c.v)
                    if oc < 2:
                        wait_all(tensor, c.s)
                    c.mm.inc(tensor.matmul(out=pp[:, :],
                                           lhsT=dtwsb[:, oc, :],
                                           rhs=dtin[:, sl, :], start=True, stop=True))
                @c.block.vector
                def _(vector, oc=oc, pp=pp):
                    wait_all(vector, c.mm, c.s)
                    c.v.inc(vector.tensor_scalar(
                        out=dpre[:, oc % 2, :], in0=pp[:, :],
                        scalar1=dtbt[:, oc:oc + 1], scalar2=SP_CLAMP,
                        op0=AL.add, op1=AL.min))
                @c.block.scalar
                def _(scalar, oc=oc):
                    wait_all(scalar, c.v, c.st)
                    scalar.activation(out=dex[:, :], in_=dpre[:, oc % 2, :],
                                      func=ACT.Exp)
                    c.s.inc(scalar.activation(out=dtt[:, oc % 2, :], in_=dex[:, :],
                                              func=ACT.Ln, bias=r.ones_col[:, 0:1]))
                @c.block.gpsimd
                def _(gpsimd, t0=t0, oc=oc):
                    wait_all(gpsimd, c.s)
                    gpsimd.dma_start(out=r.dt_hbm[oc * P:(oc + 1) * P, t0:t0 + TT],
                                     in_=dtt[:, oc % 2, :]).then_inc(c.st.h, 16)
                    c.st.n += 16


def build_fused():
    nc = bass.Bass("TRN2", target_bir_lowering=False)
    r = _R()
    r.x_pad = nc.dram_tensor("x_pad", [DM, XPAD], F32, kind="ExternalInput")
    r.x_q = nc.dram_tensor("x_q", [DM, TPC], F32, kind="ExternalInput")
    r.qoff = nc.dram_tensor("qoff", [1, 2], mybir.dt.int32, kind="ExternalInput")
    r.wxT_s = nc.dram_tensor("wxT_s", [DM, 1024], F32, kind="ExternalInput")
    r.wgT_s = nc.dram_tensor("wgT_s", [DM, 1024], F32, kind="ExternalInput")
    r.convw_s = nc.dram_tensor("convw_s", [P, 8, 4], F32, kind="ExternalInput")
    r.convb_s = nc.dram_tensor("convb_s", [P, 8], F32, kind="ExternalInput")
    r.xpT_s = nc.dram_tensor("xpT_s", [1024, 160], F32, kind="ExternalInput")
    r.dtwT_s = nc.dram_tensor("dtwT_s", [P, 1024], F32, kind="ExternalInput")
    r.dtb_s = nc.dram_tensor("dtb_s", [P, 8], F32, kind="ExternalInput")
    r.alog = nc.dram_tensor("alog", [P, 8, 16], F32, kind="ExternalInput")
    r.dvec = nc.dram_tensor("dvec", [P, 8], F32, kind="ExternalInput")
    r.woT = nc.dram_tensor("woT", [NI, DM], F32, kind="ExternalInput")
    r.wupT = nc.dram_tensor("wupT", [DM, 8192], F32, kind="ExternalInput")
    r.wgT2 = nc.dram_tensor("wgT2", [DM, 8192], F32, kind="ExternalInput")
    r.wdT = nc.dram_tensor("wdT", [8192, DM], F32, kind="ExternalInput")
    r.suu = nc.dram_tensor("suu", [P, 16], F32, kind="ExternalInput")
    r.sud = nc.dram_tensor("sud", [P, 64], F32, kind="ExternalInput")
    r.wsu = nc.dram_tensor("wsu", [P, 64], F32, kind="ExternalInput")
    r.wsg = nc.dram_tensor("wsg", [P, 64], F32, kind="ExternalInput")
    r.wsd = nc.dram_tensor("wsd", [P, 16], F32, kind="ExternalInput")
    r.h128 = nc.dram_tensor("h128", [P, P], F32, kind="ExternalInput")
    r.out_o = nc.dram_tensor("out_o", [DM, TPC], F32, kind="ExternalOutput")
    r.xs_hbm = nc.dram_tensor("xs_hbm", [1024, T], F32)
    r.xsb_hbm = nc.dram_tensor("xsb_hbm", [1024, T], BF16)
    r.sg_hbm = nc.dram_tensor("sg_hbm", [1024, T], F32)
    r.ssm_part = nc.dram_tensor("ssm_part", [160, T], F32)
    r.ssm_red = nc.dram_tensor("ssm_red", [160, T], F32)
    r.dt_hbm = nc.dram_tensor("dt_hbm", [1024, T], F32)
    r.ym_own = nc.dram_tensor("ym_own", [1024, T], BF16)
    r.ym_all = nc.dram_tensor("ym_all", [NI, T], BF16)
    r.ym_q = nc.dram_tensor("ym_q", [NI, TPC], BF16)
    r.bc16 = nc.dram_tensor("bc16", [16, T], BF16)
    r.wup2 = nc.dram_tensor("wup2", [DM, 8192], BF16)
    r.wg2 = nc.dram_tensor("wg2", [DM, 8192], BF16)
    r.wd2 = nc.dram_tensor("wd2", [8192, DM], BF16)
    r.wo2 = nc.dram_tensor("wo2", [NI, DM], BF16)
    if DBG:
        r.xs_d = nc.dram_tensor("xs_d", [1024, T], F32, kind="ExternalOutput")
        r.sg_d = nc.dram_tensor("sg_d", [1024, T], F32, kind="ExternalOutput")
        r.ssm_d = nc.dram_tensor("ssm_d", [160, T], F32, kind="ExternalOutput")
        r.dt_d = nc.dram_tensor("dt_d", [1024, T], F32, kind="ExternalOutput")
        r.ym_d = nc.dram_tensor("ym_d", [1024, T], BF16, kind="ExternalOutput")
        r.at_d = nc.dram_tensor("at_d", [P, 128], F32, kind="ExternalOutput")
        r.da_d = nc.dram_tensor("da_d", [P, 2048], F32, kind="ExternalOutput")
        r.vt_d = nc.dram_tensor("vt_d", [P, 5 * 1024], F32, kind="ExternalOutput")
        r.ymq_d = nc.dram_tensor("ymq_d", [NI, TPC], BF16, kind="ExternalOutput")

    import contextlib
    _es = contextlib.ExitStack()
    r.ones_col = _es.enter_context(nc.sbuf_tensor("ones_col", [P, 1], F32))
    r.ones_row = _es.enter_context(nc.sbuf_tensor("ones_row", [1, P], F32))
    r.consts = _es.enter_context(nc.sbuf_tensor("consts", [1, 4], F32))
    r.h128f = _es.enter_context(nc.sbuf_tensor("h128f", [P, P], F32))
    r.h128b = _es.enter_context(nc.sbuf_tensor("h128b", [P, P], BF16))
    r.suu2 = _es.enter_context(nc.sbuf_tensor("suu2", [P, 16], F32))
    r.sud2 = _es.enter_context(nc.sbuf_tensor("sud2", [P, 64], F32))
    r.wsu2 = _es.enter_context(nc.sbuf_tensor("wsu2", [P, 64], F32))
    r.wsg2 = _es.enter_context(nc.sbuf_tensor("wsg2", [P, 64], F32))
    r.wsd2 = _es.enter_context(nc.sbuf_tensor("wsd2", [P, 16], F32))
    r.cwt = _es.enter_context(nc.sbuf_tensor("cwt", [P, 8, 4], F32))
    r.cbt = _es.enter_context(nc.sbuf_tensor("cbt", [P, 8], F32))
    r.sq = _es.enter_context(nc.sbuf_tensor("sq", [P, 2, TS], F32))
    r.stat = _es.enter_context(nc.sbuf_tensor("stat", [1, 6, TS], F32))
    r.psA = _es.enter_context(nc.psum_tensor("psA", [1, TS], F32))
    r.psB = _es.enter_context(nc.psum_tensor("psB", [1, TS], F32))
    r.psM = _es.enter_context(nc.psum_tensor("psM", [P, TS], F32))
    r.psR = _es.enter_context(nc.psum_tensor("psR", [P, TS], F32))
    r.ps0 = _es.enter_context(nc.psum_tensor("ps0", [P, TS], F32))
    r.ps1 = _es.enter_context(nc.psum_tensor("ps1", [P, TS], F32))
    r.ps2 = _es.enter_context(nc.psum_tensor("ps2", [P, TS], F32))
    r.ps3 = _es.enter_context(nc.psum_tensor("ps3", [P, TS], F32))
    _s_ld = _es.enter_context(nc.semaphore("s_ld"))
    _s_st = _es.enter_context(nc.semaphore("s_st"))
    _s_mm = _es.enter_context(nc.semaphore("s_mm"))
    _s_v = _es.enter_context(nc.semaphore("s_v"))
    _s_s = _es.enter_context(nc.semaphore("s_s"))
    _qs = [_es.enter_context(nc.semaphore(f"s_q{i}")) for i in range(4)]
    _cc = _es.enter_context(nc.semaphore("s_cc"))
    block = _es.enter_context(nc.Block())
    if True:
        c = Ctx(nc, block, [Sem(_s_ld), Sem(_s_st), Sem(_s_mm), Sem(_s_v), Sem(_s_s)],
                [Sem(q) for q in _qs])
        cc = Sem(_cc)

        @block.sync
        def _(sync):
            sync.dma_start(out=r.h128f[:, :], in_=r.h128[:, :]).then_inc(c.ld.h, 16)
            sync.dma_start(out=r.suu2[:, :], in_=r.suu[:, :]).then_inc(c.ld.h, 16)
            sync.dma_start(out=r.sud2[:, :], in_=r.sud[:, :]).then_inc(c.ld.h, 16)
            sync.dma_start(out=r.wsu2[:, :], in_=r.wsu[:, :]).then_inc(c.ld.h, 16)
            sync.dma_start(out=r.wsg2[:, :], in_=r.wsg[:, :]).then_inc(c.ld.h, 16)
            sync.dma_start(out=r.wsd2[:, :], in_=r.wsd[:, :]).then_inc(c.ld.h, 16)
            sync.dma_start(out=r.cwt[:, :, :], in_=r.convw_s[:, :, :]).then_inc(c.ld.h, 16)
            sync.dma_start(out=r.cbt[:, :], in_=r.convb_s[:, :]).then_inc(c.ld.h, 16)
            c.ld.n += 8 * 16

        @block.vector
        def _(vector):
            vector.memset(r.ones_col[:, :], 1.0)
            vector.memset(r.ones_row[:, :], 1.0)
            vector.memset(r.consts[:, 0:1], LN_EPS)
            vector.memset(r.consts[:, 1:2], -0.5)
            wait_all(vector, c.ld)
            vector.tensor_copy(out=r.h128b[:, :], in_=r.h128f[:, :])
            vector.tensor_scalar_mul(out=r.suu2[:, :], in0=r.suu2[:, :],
                                     scalar1=1.0 / (64.0 * float(np.sqrt(2048.0))))
            vector.tensor_scalar_mul(out=r.sud2[:, :], in0=r.sud2[:, :],
                                     scalar1=1.0 / (64.0 * float(np.sqrt(8192.0))))
            vector.tensor_scalar_mul(out=r.wsu2[:, :], in0=r.wsu2[:, :], scalar1=64.0)
            vector.tensor_scalar_mul(out=r.wsg2[:, :], in0=r.wsg2[:, :], scalar1=64.0)
            c.v.inc(vector.tensor_scalar_mul(out=r.wsd2[:, :], in0=r.wsd2[:, :],
                                             scalar1=64.0))
            vector.wait_ge(c.v.h, c.v.n)

        _l3_transform(c, r)
        c.barrier()

        with nc.sbuf_tensor("wres", [P, 16, 16, P], BF16) as wres:
            r.wres = wres
            with nc.sbuf_tensor("wstage", [P, 16, P], F32) as wstage:
                for half, wsrc in ((0, r.wxT_s), (1, r.wgT_s)):
                    for mc in range(8):
                        @c.block.sync
                        def _(sync, wsrc=wsrc, mc=mc):
                            wait_all(sync, c.v)
                            sync.dma_start(
                                out=wstage[:, :, :],
                                in_=wsrc[:, mc * P:(mc + 1) * P]
                                    .rearrange("(k p) m -> p k m", p=P)
                                ).then_inc(c.ld.h, 16)
                            c.ld.n += 16
                        @c.block.vector
                        def _(vector, half=half, mc=mc):
                            wait_all(vector, c.ld)
                            c.v.inc(vector.tensor_copy(
                                out=wres[:, :, half * 8 + mc, :], in_=wstage[:, :, :]))
            with (
                nc.sbuf_tensor("s1_xnb", [P, 16, HCOLS], BF16) as xnb,
                nc.sbuf_tensor("s1_xstage", [P, 16, TT], F32) as xstage,
                nc.sbuf_tensor("s1_xbuf", [P, 3 + TT], F32) as xbuf,
                nc.sbuf_tensor("s1_cs", [P, TT], F32) as cs,
                nc.sbuf_tensor("s1_xst", [P, TT], F32) as xst,
                nc.sbuf_tensor("s1_xst16", [P, TT], BF16) as xst16,
                nc.sbuf_tensor("s1_sgt", [P, 2, TT], F32) as sgt,
            ):
                for k in range(T // HT):
                    _f_ln_chunk(c, r, k, xnb, xstage, r.sq, r.stat)
                    _f_inproj_chunk(c, r, k, xnb, (xbuf, cs, xst, xst16, sgt))

        _f_xproj(c, r)

        @block.gpsimd
        def _(gpsimd):
            gpsimd.wait_ge(c.st.h, c.st.n)
            wait_all(gpsimd, c.v, c.s, c.mm)
            gpsimd.collective_compute(
                "AllReduce", AL.add,
                replica_groups=[[0, 1], [2, 3], [4, 5], [6, 7]],
                ins=[r.ssm_part.ap().opt()], outs=[r.ssm_red.ap().opt()],
            ).then_inc(cc.h, 1)
            cc.n += 1
        for eng in ("sync", "tensor", "vector", "scalar", "gpsimd"):
            @getattr(block, eng)
            def _(e):
                e.wait_ge(cc.h, cc.n)

        _f_dtproj(c, r)

        with (
            nc.sbuf_tensor("cstg", [16, 4096], F32) as cstg,
            nc.sbuf_tensor("cstb", [16, 4096], BF16) as cstb,
        ):
            for j in range(4):
                @block.sync
                def _(sync, j=j):
                    wait_all(sync, c.v, c.st)
                    sync.dma_start(out=cstg[:, :],
                                   in_=r.ssm_red[144:160, j * 4096:(j + 1) * 4096]
                                   ).then_inc(c.ld.h, 16)
                    c.ld.n += 16
                @block.vector
                def _(vector, j=j):
                    wait_all(vector, c.ld)
                    c.v.inc(vector.tensor_copy(out=cstb[:, :], in_=cstg[:, :]))
                @block.gpsimd
                def _(gpsimd, j=j):
                    wait_all(gpsimd, c.v)
                    gpsimd.dma_start(out=r.bc16[:, j * 4096:(j + 1) * 4096],
                                     in_=cstb[:, :]).then_inc(c.st.h, 16)
                    c.st.n += 16
        with contextlib.ExitStack() as es5:
            r.at = es5.enter_context(nc.sbuf_tensor("l2_at", [P, 8, 16], F32))
            r.dvt = es5.enter_context(nc.sbuf_tensor("l2_dvt", [P, 8], F32))
            r.carry = es5.enter_context(nc.sbuf_tensor("l2_carry", [P, P], F32))
            r.dtt = es5.enter_context(nc.sbuf_tensor("l2_dtt", [P, 8, TW], F32))
            r.ut = es5.enter_context(nc.sbuf_tensor("l2_ut", [P, 8, TW], F32))
            r.yt = es5.enter_context(nc.sbuf_tensor("l2_yt", [P, 8, TW], F32))
            r.xstg = es5.enter_context(nc.sbuf_tensor("l2_xstg", [P, 2, TW], F32))
            r.sgs = es5.enter_context(nc.sbuf_tensor("l2_sgs", [P, 2, TW], F32))
            r.bb = es5.enter_context(nc.sbuf_tensor("l2_bb", [P, 2, TW], F32))
            r.cb = es5.enter_context(nc.sbuf_tensor("l2_cb", [P, 2, TW], F32))
            r.dat = es5.enter_context(nc.sbuf_tensor("l2_dat", [P, 2, TW], F32))
            r.vt = es5.enter_context(nc.sbuf_tensor("l2_vt", [P, 2, TW], F32))
            r.ht = es5.enter_context(nc.sbuf_tensor("l2_ht", [P, TW], F32))
            r.pt = es5.enter_context(nc.sbuf_tensor("l2_pt", [P, 2, TW], F32))
            r.ymt = es5.enter_context(nc.sbuf_tensor("l2_ymt", [P, 2, TW], BF16))
            c.barrier()
            r.dt_i, r.xs_i, r.sg_i = r.dt_hbm, r.xs_hbm, r.sg_hbm
            r.bc_i, r.bc_row0 = r.ssm_red, 128
            r.ym_o = r.ym_own
            _l2_prologue(c, r)
            _l2_body(c, r)

        @block.gpsimd
        def _(gpsimd):
            gpsimd.wait_ge(c.st.h, c.st.n)
            wait_all(gpsimd, c.v, c.s, c.mm)
            gpsimd.collective_compute(
                "AllGather", AL.bypass,
                replica_groups=[[0, 1, 4, 5], [2, 3, 6, 7]],
                ins=[r.ym_own.ap().opt()], outs=[r.ym_all.ap().opt()],
            ).then_inc(cc.h, 1)
            cc.n += 1
        for eng in ("sync", "tensor", "vector", "scalar", "gpsimd"):
            @getattr(block, eng)
            def _(e):
                e.wait_ge(cc.h, cc.n)

        with (
            nc.sbuf_tensor("qofft", [1, 2], mybir.dt.int32) as qofft,
            nc.gpsimd.register("rq") as rq,
        ):
            @block.sync
            def _(sync):
                sync.dma_start(out=qofft[:1, :], in_=r.qoff[:1, :]).then_inc(c.ld.h, 16)
                c.ld.n += 16
            @block.gpsimd
            def _(gpsimd):
                wait_all(gpsimd, c.ld)
                gpsimd.reg_load(rq, qofft[:1, :1])
                gpsimd.dma_start(out=r.ym_q.ap(),
                                 in_=AP(r.ym_all, rq, [[T, NI], [1, TPC]])
                                 ).then_inc(c.st.h, 16)
                c.st.n += 16
                gpsimd.wait_ge(c.st.h, c.st.n)
        if DBG:
            @block.scalar
            def _(scalar):
                wait_all(scalar, c.st, c.v, c.mm, c.s)
                scalar.dma_start(out=r.xs_d.ap(), in_=r.xs_hbm.ap()).then_inc(c.ld.h, 16)
                scalar.dma_start(out=r.sg_d.ap(), in_=r.sg_hbm.ap()).then_inc(c.ld.h, 16)
                scalar.dma_start(out=r.ssm_d.ap(), in_=r.ssm_red.ap()).then_inc(c.ld.h, 16)
                scalar.dma_start(out=r.dt_d.ap(), in_=r.dt_hbm.ap()).then_inc(c.ld.h, 16)
                scalar.dma_start(out=r.ym_d.ap(), in_=r.ym_own.ap()).then_inc(c.ld.h, 16)
                scalar.dma_start(out=r.ymq_d.ap(), in_=r.ym_q.ap()).then_inc(c.ld.h, 16)
                c.ld.n += 96
                scalar.wait_ge(c.ld.h, c.ld.n)
        c.barrier()
        r.ym_i = r.ym_q
        r.ym_reg = None
        r.x_i = r.x_q
        r.x_off = 0
        _l3_main(c, r)

        @block.gpsimd
        def _(gpsimd):
            gpsimd.wait_ge(c.st.h, c.st.n)
    _es.close()
    return nc


def _fused_inputs(inputs, xT):
    w_in = np.asarray(inputs["L0_attn_in_proj_w"], np.float32)
    wxT = np.ascontiguousarray(w_in[0::2].T)
    wgT = np.ascontiguousarray(w_in[1::2].T)
    cw = np.asarray(inputs["L0_attn_conv_w"], np.float32)
    cb = np.asarray(inputs["L0_attn_conv_b"], np.float32)
    xpT = np.asarray(inputs["L0_attn_x_proj_weight"], np.float32).transpose(0, 2, 1)
    dtwT = np.asarray(inputs["L0_attn_dt_proj_weight"], np.float32).transpose(0, 2, 1)
    dtb = np.asarray(inputs["L0_attn_dt_proj_bias"], np.float32)
    alog_full = np.asarray(inputs["L0_attn_A_log"], np.float32)
    d_full = np.asarray(inputs["L0_attn_D"], np.float32)
    woT = np.ascontiguousarray(np.asarray(inputs["L0_attn_out_proj_w"], np.float32).T)
    wupT = np.ascontiguousarray(np.asarray(inputs["L0_ffn_W_up"], np.float32).T)
    wgT2 = np.ascontiguousarray(np.asarray(inputs["L0_ffn_W_gate"], np.float32).T)
    wdT = np.ascontiguousarray(np.asarray(inputs["L0_ffn_W_down"], np.float32).T)
    suu = np.ascontiguousarray(
        np.asarray(inputs["L0_ffn_SU_ug"], np.float32).reshape(16, P).T)
    sud = np.ascontiguousarray(
        np.asarray(inputs["L0_ffn_SU_dp"], np.float32).reshape(64, P).T)
    wsall = np.asarray(inputs["L0_ffn_Wscale_ug"], np.float32)
    wsu = np.ascontiguousarray(wsall[:8192].reshape(64, P).T)
    wsg = np.ascontiguousarray(wsall[8192:].reshape(64, P).T)
    wsd = np.ascontiguousarray(
        np.asarray(inputs["L0_ffn_Wscale_dp"], np.float32).reshape(16, P).T)
    h128 = _hadamard(P)
    maps = []
    for cc in range(8):
        h = cc // 4
        b = (cc // 2) % 2
        dh = cc % 2
        q = 2 * h + dh
        s0 = q * TPC
        lo = h * HD + dh * 1024
        x_pad = np.zeros((DM, XPAD), np.float32)
        x_pad[:, 3:3 + T] = xT[b]
        maps.append({
            "x_pad": x_pad,
            "x_q": np.ascontiguousarray(xT[b, :, s0:s0 + TPC]),
            "qoff": np.array([[s0, 0]], np.int32),
            "wxT_s": np.ascontiguousarray(wxT[:, lo:lo + 1024]),
            "wgT_s": np.ascontiguousarray(wgT[:, lo:lo + 1024]),
            "convw_s": np.ascontiguousarray(
                cw[lo:lo + 1024].reshape(8, P, 4).transpose(1, 0, 2)),
            "convb_s": np.ascontiguousarray(cb[lo:lo + 1024].reshape(8, P).T),
            "xpT_s": np.ascontiguousarray(xpT[h, dh * 1024:dh * 1024 + 1024, :]),
            "dtwT_s": np.ascontiguousarray(dtwT[h][:, dh * 1024:dh * 1024 + 1024]),
            "dtb_s": np.ascontiguousarray(
                dtb[h, dh * 1024:dh * 1024 + 1024].reshape(8, P).T),
            "alog": np.ascontiguousarray(
                alog_full[h, dh * 1024:dh * 1024 + 1024].reshape(8, P, 16)
                .transpose(1, 0, 2)),
            "dvec": np.ascontiguousarray(
                d_full[h, dh * 1024:dh * 1024 + 1024].reshape(8, P).T),
            "woT": woT, "wupT": wupT, "wgT2": wgT2, "wdT": wdT,
            "suu": suu, "sud": sud, "wsu": wsu, "wsg": wsg, "wsd": wsd,
            "h128": h128,
        })
    return maps


def kernel(**inputs):
    xT = np.ascontiguousarray(
        np.asarray(inputs["L0_attn_hidden_states"], np.float32).transpose(0, 2, 1))
    nc = build_fused()
    maps = _fused_inputs(inputs, xT)
    res = run_bass_kernel_spmd(nc, maps, core_ids=list(range(8)))
    out = np.empty((NB, DM, T), np.float32)
    for cc in range(8):
        b = (cc // 2) % 2
        q = 2 * (cc // 4) + (cc % 2)
        s0 = q * TPC
        out[b, :, s0:s0 + TPC] = res.results[cc]["out_o"]
    return np.ascontiguousarray(out.transpose(0, 2, 1))


# version 08 (final)
# speedup vs baseline: 1.0591x; 1.0591x over previous
import numpy as np
import ml_dtypes
import concourse.bass as bass
import concourse.mybir as mybir
from concourse.bass import AP
from concourse.bass_utils import run_bass_kernel_spmd

F32 = mybir.dt.float32
BF16 = mybir.dt.bfloat16
AL = mybir.AluOpType
ACT = mybir.ActivationFunctionType
BF = ml_dtypes.bfloat16

P = 128
T = 16384
NB = 2
NH = 2
HD = 2048
NI = 4096
DM = 2048
NST = 16
RK = 128
TPC = 4096
HT = 2048
TT = 512
HCOLS = 2056
LN_EPS = 1e-5
DBG = False
PARANOID = 0
SP_CLAMP = 80.0

L1_TILES = [(j * TT, TT) for j in range(4)] + [(4 * TT, 8)]


class Sem:
    def __init__(self, handle):
        self.h = handle
        self.n = 0

    def inc(self, ins, k=1):
        ins.then_inc(self.h, k)
        self.n += k
        return ins


def wait_all(eng, *sems):
    for s in sems:
        if s.n > 0:
            eng.wait_ge(s.h, s.n)


class Ctx:
    def __init__(self, nc, block, sems, qsems=()):
        self.nc = nc
        self.block = block
        self.ld, self.st, self.mm, self.v, self.s = sems
        self.q = list(qsems)

    def barrier(self):
        c = self
        @c.block.sync
        def _(sync):
            wait_all(sync, c.st, c.mm, c.v, c.s)
        @c.block.tensor
        def _(tensor):
            wait_all(tensor, c.ld, c.st, c.v, c.s)
        @c.block.vector
        def _(vector):
            wait_all(vector, c.ld, c.st, c.mm, c.s)
        @c.block.scalar
        def _(scalar):
            wait_all(scalar, c.ld, c.st, c.mm, c.v)
        @c.block.gpsimd
        def _(gpsimd):
            wait_all(gpsimd, c.ld, c.mm, c.v, c.s)


def _store_win(j, w):
    c0 = j * TT
    lo = max(c0, 3)
    hi = min(c0 + w, 3 + HT)
    return (lo - c0, lo - 3, hi - lo)



def _l1_phase_a(c, r, half, xnb):
    sfx = str(half)
    """LN over feature dim -> xnb (bf16, feature-major)."""
    nc = c.nc
    with (
        nc.sbuf_tensor("xstage" + sfx, [P, 16, TT], F32) as xstage,
        nc.sbuf_tensor("sq" + sfx, [P, 2, TT], F32) as sq,
        nc.sbuf_tensor("stat" + sfx, [1, 6, TT], F32) as stat,
    ):
        c.barrier()
        @c.block.vector
        def _(vector):
            c.v.inc(vector.memset(xnb[:, :, 4 * TT + 3:], 0.0))
        for (c0, w) in L1_TILES:
            @c.block.sync
            def _(sync, c0=c0, w=w):
                wait_all(sync, c.mm, c.v)
                for fc in range(16):
                    sync.dma_start(out=xstage[:, fc, :w],
                                   in_=r.xin[fc * P:(fc + 1) * P, half, c0:c0 + w]
                                   ).then_inc(c.ld.h, 16)
                c.ld.n += 16 * 16
            @c.block.tensor
            def _(tensor, w=w):
                wait_all(tensor, c.ld, c.v)
                for fc in range(16):
                    tensor.matmul(out=r.psA[:, :w], lhsT=r.ones_col[:, :],
                                  rhs=xstage[:, fc, :w], start=(fc == 0), stop=(fc == 15))
            mm_base = c.mm.n
            s_base = c.s.n
            for fc in range(16):
                @c.block.scalar
                def _(scalar, w=w, fc=fc):
                    if fc == 0:
                        wait_all(scalar, c.ld, c.mm)
                    if fc >= 2:
                        scalar.wait_ge(c.mm.h, mm_base + fc - 1)
                    c.s.inc(scalar.activation(out=sq[:, fc % 2, :w],
                                              in_=xstage[:, fc, :w], func=ACT.Square))
                @c.block.tensor
                def _(tensor, w=w, fc=fc):
                    tensor.wait_ge(c.s.h, s_base + fc + 1)
                    c.mm.inc(tensor.matmul(out=r.psB[:, :w], lhsT=r.ones_col[:, :],
                                           rhs=sq[:, fc % 2, :w],
                                           start=(fc == 0), stop=(fc == 15)))
            @c.block.vector
            def _(vector, w=w):
                wait_all(vector, c.mm)
                vector.tensor_scalar_mul(out=stat[:, 0, :w], in0=r.psA[:, :w],
                                         scalar1=1.0 / DM)
                c.v.inc(vector.tensor_scalar_mul(out=stat[:, 1, :w], in0=r.psB[:, :w],
                                                 scalar1=1.0 / DM))
                vector.wait_ge(c.v.h, c.v.n)
                c.v.inc(vector.tensor_mul(out=stat[:, 2, :w], in0=stat[:, 0, :w],
                                          in1=stat[:, 0, :w]))
                vector.wait_ge(c.v.h, c.v.n)
                c.v.inc(vector.tensor_sub(out=stat[:, 3, :w], in0=stat[:, 1, :w],
                                          in1=stat[:, 2, :w]))
            @c.block.scalar
            def _(scalar, w=w):
                wait_all(scalar, c.v)
                scalar.activation(out=stat[:, 4, :w], in_=stat[:, 3, :w],
                                  func=ACT.Ln, bias=r.consts[:, 0:1])
                c.s.inc(scalar.activation(out=stat[:, 5, :w], in_=stat[:, 4, :w],
                                          func=ACT.Exp, scale=r.consts[:, 1:2]))
            @c.block.tensor
            def _(tensor, w=w):
                wait_all(tensor, c.s)
                tensor.matmul(out=r.psM[:, :w], lhsT=r.ones_row[:, :],
                              rhs=stat[:, 0, :w], start=True, stop=True)
                c.mm.inc(tensor.matmul(out=r.psR[:, :w], lhsT=r.ones_row[:, :],
                                       rhs=stat[:, 5, :w], start=True, stop=True))
            @c.block.vector
            def _(vector, c0=c0, w=w):
                wait_all(vector, c.mm)
                for fc in range(16):
                    vector.tensor_sub(out=xstage[:, fc, :w], in0=xstage[:, fc, :w],
                                      in1=r.psM[:, :w])
                for fc in range(16):
                    ins = vector.tensor_tensor(out=xnb[:, fc, c0:c0 + w],
                                               in0=xstage[:, fc, :w],
                                               in1=r.psR[:, :w], op=AL.mult)
                c.v.inc(ins)


def _l1_phase_b(c, r, half, xnb):
    nc = c.nc
    sfx = str(half)
    with (
        nc.sbuf_tensor("wst" + sfx, [P, 2, 16, P], F32) as wst,
        nc.sbuf_tensor("wsb" + sfx, [P, 2, 16, P], BF16) as wsb,
        nc.sbuf_tensor("xbuf" + sfx, [P, 3 + TT], F32) as xbuf,
        nc.sbuf_tensor("cs" + sfx, [P, TT], F32) as cs,
        nc.sbuf_tensor("xst" + sfx, [P, TT], F32) as xst,
        nc.sbuf_tensor("xst16" + sfx, [P, TT], BF16) as xst16,
        nc.sbuf_tensor("sgt" + sfx, [P, 2, TT], F32) as sgt,
        nc.sbuf_tensor("pjdbg" + sfx, [P, 2, 8], F32) as pjdbg,
        nc.psum_tensor("psH" + sfx, [P, 3], F32) as psH,
    ):
        c.barrier()
        for mc in range(64):
            is_x = mc < 32
            wT = r.wxT if is_x else r.wgT
            wc = mc if is_x else mc - 32
            h, dc = wc // 16, wc % 16
            ws = mc % 2
            @c.block.sync
            def _(sync, wT=wT, wc=wc, ws=ws):
                wait_all(sync, c.v)
                sync.dma_start(
                    out=wst[:, ws, :, :],
                    in_=wT[:, wc * P:(wc + 1) * P].rearrange("(k p) m -> p k m", p=P)
                ).then_inc(c.ld.h, 16)
                c.ld.n += 16
            @c.block.vector
            def _(vector, ws=ws):
                wait_all(vector, c.ld, c.mm)
                c.v.inc(vector.tensor_copy(out=wsb[:, ws, :, :], in_=wst[:, ws, :, :]))
            if is_x:
                @c.block.tensor
                def _(tensor, ws=ws):
                    wait_all(tensor, c.v)
                    for kc in range(16):
                        ins = tensor.matmul(out=psH[:, :], lhsT=wsb[:, ws, kc, :],
                                            rhs=xnb[:, kc, 0:3],
                                            start=(kc == 0), stop=(kc == 15))
                    c.mm.inc(ins)
                @c.block.vector
                def _(vector):
                    wait_all(vector, c.mm)
                    c.v.inc(vector.tensor_copy(out=xbuf[:, 0:3], in_=psH[:, :]))
            for ti, (c0, w) in enumerate(L1_TILES):
                pp = r.psP[ti % 2]
                @c.block.tensor
                def _(tensor, ws=ws, c0=c0, w=w, pp=pp):
                    wait_all(tensor, c.v, c.s)
                    for kc in range(16):
                        ins = tensor.matmul(out=pp[:, :w], lhsT=wsb[:, ws, kc, :],
                                            rhs=xnb[:, kc, c0:c0 + w],
                                            start=(kc == 0), stop=(kc == 15))
                    c.mm.inc(ins)
                so, do, wd = _store_win(ti, w)
                do_g = do + half * HT
                if DBG and half == 0 and ti == 4 and mc in (0, 32):
                    @c.block.vector
                    def _(vector, pp=pp, mc=mc):
                        wait_all(vector, c.mm)
                        c.v.inc(vector.tensor_copy(out=pjdbg[:, mc // 32, :],
                                                   in_=pp[:, :8]))
                    @c.block.gpsimd
                    def _(gpsimd, mc=mc):
                        wait_all(gpsimd, c.v)
                        gpsimd.dma_start(out=r.pj_o[mc // 32, :, :],
                                         in_=pjdbg[:, mc // 32, :]).then_inc(c.st.h, 16)
                        c.st.n += 16
                if is_x:
                    @c.block.vector
                    def _(vector, w=w, pp=pp, ti=ti, mc=mc):
                        small = w < 256
                        def step(ins):
                            c.v.inc(ins)
                            if small:
                                vector.wait_ge(c.v.h, c.v.n)
                        wait_all(vector, c.mm, c.st)
                        if ti > 0:
                            step(vector.tensor_copy(out=xbuf[:, 0:3],
                                                    in_=xbuf[:, TT:TT + 3]))
                        step(vector.tensor_copy(out=xbuf[:, 3:3 + w], in_=pp[:, :w]))
                        step(vector.tensor_scalar_mul(out=cs[:, :w], in0=xbuf[:, 0:w],
                                                      scalar1=r.cwt[:, mc, 0:1]))
                        for k in (1, 2, 3):
                            step(vector.scalar_tensor_tensor(
                                out=cs[:, :w], in0=xbuf[:, k:k + w],
                                scalar=r.cwt[:, mc, k:k + 1], in1=cs[:, :w],
                                op0=AL.mult, op1=AL.add))
                    @c.block.scalar
                    def _(scalar, w=w, mc=mc):
                        wait_all(scalar, c.v)
                        c.s.inc(scalar.activation(out=xst[:, :w], in_=cs[:, :w],
                                                  func=ACT.Silu, bias=r.cbt[:, mc:mc + 1]))
                    @c.block.vector
                    def _(vector, w=w):
                        wait_all(vector, c.s)
                        c.v.inc(vector.tensor_copy(out=xst16[:, :w], in_=xst[:, :w]))
                    if wd > 0:
                        @c.block.gpsimd
                        def _(gpsimd, h=h, dc=dc, so=so, do_g=do_g, wd=wd):
                            wait_all(gpsimd, c.v)
                            gpsimd.dma_start(
                                out=r.xs_o[h, dc * P:(dc + 1) * P, do_g:do_g + wd],
                                in_=xst[:, so:so + wd]).then_inc(c.st.h, 16)
                            gpsimd.dma_start(
                                out=r.xsb[h, dc * P:(dc + 1) * P, do_g:do_g + wd],
                                in_=xst16[:, so:so + wd]).then_inc(c.st.h, 16)
                            c.st.n += 32
                else:
                    @c.block.scalar
                    def _(scalar, w=w, pp=pp, ti=ti):
                        wait_all(scalar, c.mm, c.st)
                        c.s.inc(scalar.activation(out=sgt[:, ti % 2, :w], in_=pp[:, :w],
                                                  func=ACT.Silu))
                    if wd > 0:
                        @c.block.gpsimd
                        def _(gpsimd, h=h, dc=dc, so=so, do_g=do_g, wd=wd, ti=ti):
                            wait_all(gpsimd, c.s)
                            gpsimd.dma_start(
                                out=r.sg_o[h, dc * P:(dc + 1) * P, do_g:do_g + wd],
                                in_=sgt[:, ti % 2, so:so + wd]).then_inc(c.st.h, 16)
                            c.st.n += 16


def _l1_phase_cd(c, r):
    nc = c.nc
    with (
        nc.sbuf_tensor("xpst", [P, NH, 16, 160], F32) as xpst,
        nc.sbuf_tensor("xpsb", [P, NH, 16, 160], BF16) as xpsb,
        nc.sbuf_tensor("dtwst", [P, NH, HD], F32) as dtwst,
        nc.sbuf_tensor("dtwsb", [P, NH, HD], BF16) as dtwsb,
        nc.sbuf_tensor("dtbt", [P, NH, 16], F32) as dtbt,
        nc.sbuf_tensor("xsl", [P, 2, TT], BF16) as xsl,
        nc.sbuf_tensor("dtin", [P, TT], BF16) as dtin,
        nc.sbuf_tensor("bct", [32, TT], F32) as bct,
        nc.sbuf_tensor("dpre", [P, 2, TT], F32) as dpre,
        nc.sbuf_tensor("dex", [P, TT], F32) as dex,
        nc.sbuf_tensor("dtt", [P, 2, TT], F32) as dtt,
    ):
        c.barrier()
        @c.block.sync
        def _(sync):
            sync.dma_start(out=xpst[:, :, :, :],
                           in_=r.xpT.ap().rearrange("h (k p) r -> p h k r", p=P)
                           ).then_inc(c.ld.h, 16)
            sync.dma_start(out=dtwst[:, :, :],
                           in_=r.dtwT.ap().rearrange("h p d -> p h d")).then_inc(c.ld.h, 16)
            sync.dma_start(out=dtbt[:, :, :],
                           in_=r.dtb.ap().rearrange("h p k -> p h k")).then_inc(c.ld.h, 16)
            c.ld.n += 48
        @c.block.vector
        def _(vector):
            wait_all(vector, c.ld)
            vector.tensor_copy(out=xpsb[:, :, :, :], in_=xpst[:, :, :, :])
            c.v.inc(vector.tensor_copy(out=dtwsb[:, :, :], in_=dtwst[:, :, :]))
        c.barrier()
        for h in range(NH):
            for tj in range(TPC // TT):
                t0 = tj * TT
                for dc in range(16):
                    qs = c.q[dc % 2]
                    @c.block.sync
                    def _(sync, h=h, t0=t0, dc=dc, qs=qs):
                        wait_all(sync, c.mm)
                        sync.dma_start(out=xsl[:, dc % 2, :],
                                       in_=r.xsb[h, dc * P:(dc + 1) * P, t0:t0 + TT]
                                       ).then_inc(qs.h, 16)
                        qs.n += 16
                    @c.block.tensor
                    def _(tensor, h=h, dc=dc, qs=qs):
                        tensor.wait_ge(qs.h, qs.n)
                        if dc == 0:
                            wait_all(tensor, c.v)
                        tensor.matmul(out=r.psM[:, :], lhsT=xpsb[:, h, dc, 0:P],
                                      rhs=xsl[:, dc % 2, :],
                                      start=(dc == 0), stop=(dc == 15))
                        c.mm.inc(tensor.matmul(out=r.psR[0:32, :],
                                               lhsT=xpsb[:, h, dc, P:160],
                                               rhs=xsl[:, dc % 2, :],
                                               start=(dc == 0), stop=(dc == 15)))
                @c.block.vector
                def _(vector):
                    wait_all(vector, c.mm, c.st)
                    vector.tensor_copy(out=dtin[:, :], in_=r.psM[:, :])
                    c.v.inc(vector.tensor_copy(out=bct[:, :], in_=r.psR[0:32, :]))
                @c.block.gpsimd
                def _(gpsimd, h=h, t0=t0):
                    wait_all(gpsimd, c.v)
                    gpsimd.dma_start(out=r.bc_o[h, :, t0:t0 + TT],
                                     in_=bct[:, :]).then_inc(c.st.h, 16)
                    c.st.n += 16
                for oc in range(16):
                    pp = r.psP[oc % 2]
                    @c.block.tensor
                    def _(tensor, h=h, oc=oc, pp=pp):
                        wait_all(tensor, c.v)
                        if oc < 2:
                            wait_all(tensor, c.s)
                        c.mm.inc(tensor.matmul(out=pp[:, :],
                                               lhsT=dtwsb[:, h, oc * P:(oc + 1) * P],
                                               rhs=dtin[:, :], start=True, stop=True))
                    @c.block.vector
                    def _(vector, h=h, oc=oc, pp=pp):
                        wait_all(vector, c.mm, c.s)
                        c.v.inc(vector.tensor_scalar(
                            out=dpre[:, oc % 2, :], in0=pp[:, :],
                            scalar1=dtbt[:, h, oc:oc + 1], scalar2=SP_CLAMP,
                            op0=AL.add, op1=AL.min))
                    @c.block.scalar
                    def _(scalar, oc=oc):
                        wait_all(scalar, c.v, c.st)
                        scalar.activation(out=dex[:, :], in_=dpre[:, oc % 2, :],
                                          func=ACT.Exp)
                        c.s.inc(scalar.activation(out=dtt[:, oc % 2, :], in_=dex[:, :],
                                                  func=ACT.Ln, bias=r.ones_col[:, 0:1]))
                    @c.block.gpsimd
                    def _(gpsimd, h=h, t0=t0, oc=oc):
                        wait_all(gpsimd, c.s)
                        gpsimd.dma_start(out=r.dt_o[h, oc * P:(oc + 1) * P, t0:t0 + TT],
                                         in_=dtt[:, oc % 2, :]).then_inc(c.st.h, 16)
                        c.st.n += 16
        @c.block.gpsimd
        def _(gpsimd):
            gpsimd.wait_ge(c.st.h, c.st.n)


class _R:
    pass


def build_l1():
    nc = bass.Bass("TRN2", target_bir_lowering=False)
    r = _R()
    r.xin = nc.dram_tensor("xin", [DM, 2, HCOLS], F32, kind="ExternalInput")
    r.wxT = nc.dram_tensor("wxT", [DM, NI], F32, kind="ExternalInput")
    r.wgT = nc.dram_tensor("wgT", [DM, NI], F32, kind="ExternalInput")
    r.convw = nc.dram_tensor("convw", [P, 32, 4], F32, kind="ExternalInput")
    r.convb = nc.dram_tensor("convb", [P, 32], F32, kind="ExternalInput")
    r.xpT = nc.dram_tensor("xpT", [NH, HD, 160], F32, kind="ExternalInput")
    r.dtwT = nc.dram_tensor("dtwT", [NH, RK, HD], F32, kind="ExternalInput")
    r.dtb = nc.dram_tensor("dtb", [NH, P, 16], F32, kind="ExternalInput")
    r.dt_o = nc.dram_tensor("dt_o", [NH, HD, TPC], F32, kind="ExternalOutput")
    r.xs_o = nc.dram_tensor("xs_o", [NH, HD, TPC], F32, kind="ExternalOutput")
    r.sg_o = nc.dram_tensor("sg_o", [NH, HD, TPC], F32, kind="ExternalOutput")
    r.bc_o = nc.dram_tensor("bc_o", [NH, 32, TPC], F32, kind="ExternalOutput")
    r.xsb = nc.dram_tensor("xsb", [NH, HD, TPC], BF16)
    if DBG:
        r.xnb_o = nc.dram_tensor("xnb_o", [2, DM, HCOLS], BF16, kind="ExternalOutput")
        r.pj_o = nc.dram_tensor("pj_o", [2, P, 8], F32, kind="ExternalOutput")

    import contextlib
    _es = contextlib.ExitStack()
    ones_col = _es.enter_context(nc.sbuf_tensor("ones_col", [P, 1], F32))
    ones_row = _es.enter_context(nc.sbuf_tensor("ones_row", [1, P], F32))
    consts = _es.enter_context(nc.sbuf_tensor("consts", [1, 4], F32))
    cwt = _es.enter_context(nc.sbuf_tensor("cwt", [P, 32, 4], F32))
    cbt = _es.enter_context(nc.sbuf_tensor("cbt", [P, 32], F32))
    psA = _es.enter_context(nc.psum_tensor("psA", [1, TT], F32))
    psB = _es.enter_context(nc.psum_tensor("psB", [1, TT], F32))
    psM = _es.enter_context(nc.psum_tensor("psM", [P, TT], F32))
    psR = _es.enter_context(nc.psum_tensor("psR", [P, TT], F32))
    psP0 = _es.enter_context(nc.psum_tensor("psP0", [P, TT], F32))
    psP1 = _es.enter_context(nc.psum_tensor("psP1", [P, TT], F32))
    _s_ld = _es.enter_context(nc.semaphore("s_ld"))
    _s_st = _es.enter_context(nc.semaphore("s_st"))
    _s_mm = _es.enter_context(nc.semaphore("s_mm"))
    _s_v = _es.enter_context(nc.semaphore("s_v"))
    _s_s = _es.enter_context(nc.semaphore("s_s"))
    _s_q0 = _es.enter_context(nc.semaphore("s_q0"))
    _s_q1 = _es.enter_context(nc.semaphore("s_q1"))
    _s_q2 = _es.enter_context(nc.semaphore("s_q2"))
    _s_q3 = _es.enter_context(nc.semaphore("s_q3"))
    block = _es.enter_context(nc.Block())
    if True:
        r.ones_col, r.ones_row, r.cwt, r.cbt = ones_col, ones_row, cwt, cbt
        r.consts = consts
        r.psA, r.psB, r.psM, r.psR = psA, psB, psM, psR
        r.psP = [psP0, psP1]
        c = Ctx(nc, block, [Sem(_s_ld), Sem(_s_st), Sem(_s_mm), Sem(_s_v), Sem(_s_s)],
                [Sem(_s_q0), Sem(_s_q1), Sem(_s_q2), Sem(_s_q3)])

        @block.sync
        def _(sync):
            sync.dma_start(out=cwt[:, :, :], in_=r.convw[:, :, :]).then_inc(c.ld.h, 16)
            sync.dma_start(out=cbt[:, :], in_=r.convb[:, :]).then_inc(c.ld.h, 16)
            c.ld.n += 32

        @block.vector
        def _(vector):
            vector.memset(ones_col[:, :], 1.0)
            vector.memset(ones_row[:, :], 1.0)
            vector.memset(consts[:, 0:1], LN_EPS)
            c.v.inc(vector.memset(consts[:, 1:2], -0.5))

        for half in range(2):
            with nc.sbuf_tensor(f"xnb{half}", [P, 16, HCOLS], BF16) as xnb:
                _l1_phase_a(c, r, half, xnb)
                if DBG:
                    c.barrier()
                    @block.gpsimd
                    def _(gpsimd, half=half, xnb=xnb):
                        for fc in range(16):
                            gpsimd.dma_start(
                                out=r.xnb_o[half, fc * P:(fc + 1) * P, :],
                                in_=xnb[:, fc, :]).then_inc(c.st.h, 16)
                        c.st.n += 16 * 16
                _l1_phase_b(c, r, half, xnb)
        _l1_phase_cd(c, r)

    _es.close()
    return nc



def _l1_inputs(inputs, xT):
    w_in = np.asarray(inputs["L0_attn_in_proj_w"], np.float32)
    wxT = np.ascontiguousarray(w_in[0::2].T)
    wgT = np.ascontiguousarray(w_in[1::2].T)
    cw = np.asarray(inputs["L0_attn_conv_w"], np.float32)
    convw = np.ascontiguousarray(cw.reshape(32, P, 4).transpose(1, 0, 2))
    cb = np.asarray(inputs["L0_attn_conv_b"], np.float32)
    convb = np.ascontiguousarray(cb.reshape(32, P).T)
    xpT = np.ascontiguousarray(
        np.asarray(inputs["L0_attn_x_proj_weight"], np.float32).transpose(0, 2, 1))
    dtwT = np.ascontiguousarray(
        np.asarray(inputs["L0_attn_dt_proj_weight"], np.float32).transpose(0, 2, 1))
    dtb = np.ascontiguousarray(
        np.asarray(inputs["L0_attn_dt_proj_bias"], np.float32)
        .reshape(NH, 16, P).transpose(0, 2, 1))
    maps = []
    for c in range(8):
        b = (c // 2) % 2
        q = 2 * (c // 4) + (c % 2)
        s0 = q * TPC
        xin = np.zeros((DM, 2, HCOLS), np.float32)
        for k in range(2):
            base = s0 + HT * k
            lo = max(base - 3, 0)
            seg = xT[b, :, lo:base + HT]
            xin[:, k, 3 - (base - lo):3 + HT] = seg
        maps.append({
            "xin": xin, "wxT": wxT, "wgT": wgT, "convw": convw, "convb": convb,
            "xpT": xpT, "dtwT": dtwT, "dtb": dtb,
        })
    return maps


def run_l1(inputs, xT):
    nc = build_l1()
    maps = _l1_inputs(inputs, xT)
    res = run_bass_kernel_spmd(nc, maps, core_ids=list(range(8)))
    dtg = np.empty((NH, NB, HD, T), np.float32)
    xsg = np.empty((NH, NB, HD, T), np.float32)
    sgg = np.empty((NH, NB, HD, T), np.float32)
    bcg = np.empty((NH, NB, 32, T), np.float32)
    for c in range(8):
        b = (c // 2) % 2
        q = 2 * (c // 4) + (c % 2)
        s0 = q * TPC
        rr = res.results[c]
        dtg[:, b, :, s0:s0 + TPC] = rr["dt_o"]
        xsg[:, b, :, s0:s0 + TPC] = rr["xs_o"]
        sgg[:, b, :, s0:s0 + TPC] = rr["sg_o"]
        bcg[:, b, :, s0:s0 + TPC] = rr["bc_o"]
    return dtg, xsg, sgg, bcg



TW = 1024


def _l2_prologue(c, r):
    at, dvt, carry = r.at, r.dvt, r.carry
    block = c.block

    @block.sync
    def _(sync):
        sync.dma_start(out=at[:, :, :], in_=r.alog[:, :, :]).then_inc(c.ld.h, 16)
        sync.dma_start(out=dvt[:, :], in_=r.dvec[:, :]).then_inc(c.ld.h, 16)
        c.ld.n += 32

    @block.scalar
    def _(scalar):
        wait_all(scalar, c.ld)
        c.s.inc(scalar.activation(out=at[:, :, :], in_=at[:, :, :], func=ACT.Exp))

    @block.vector
    def _(vector):
        vector.memset(carry[:, :], 0.0)
        wait_all(vector, c.s)
        c.v.inc(vector.tensor_scalar_mul(out=at[:, :, :], in0=at[:, :, :],
                                         scalar1=-1.0))
        vector.wait_ge(c.v.h, c.v.n)


def _l2_body(c, r):
    if DBG and getattr(r, "at_d", None) is not None:
        @c.block.scalar
        def _(scalar):
            wait_all(scalar, c.v, c.s)
            scalar.dma_start(out=r.at_d.ap(),
                             in_=r.at.ap().rearrange("p a b -> p (a b)")).then_inc(c.ld.h, 16)
            scalar.dma_start(out=r.da_d[:, TW:2 * TW],
                             in_=r.dt_i[0:P, 0:TW]).then_inc(c.ld.h, 16)
            c.ld.n += 32
            scalar.wait_ge(c.ld.h, c.ld.n)
    bc0 = getattr(r, "bc_row0", 0)
    cbsrc = getattr(r, "cb_src", None)
    at, dvt, carry = r.at, r.dvt, r.carry
    dtt, ut, yt = r.dtt, r.ut, r.yt
    xstg, sgs, bb, cb = r.xstg, r.sgs, r.bb, r.cb
    dat, vt, ht, pt, ymt = r.dat, r.vt, r.ht, r.pt, r.ymt
    s_it, s_g, s_bc, s_sg = c.q[0], c.q[1], c.q[2], c.q[3]
    carry_prev = [None]
    it_ctr = [0]
    for tj in range(T // TW):
        t0 = tj * TW
        for dc in range(8):
            @c.block.sync
            def _(sync, t0=t0, dc=dc):
                wait_all(sync, c.v)
                sync.dma_start(out=dtt[:, dc, :],
                               in_=r.dt_i[dc * P:(dc + 1) * P, t0:t0 + TW]
                               ).then_inc(c.ld.h, 16)
                sync.dma_start(out=xstg[:, dc % 2, :],
                               in_=r.xs_i[dc * P:(dc + 1) * P, t0:t0 + TW]
                               ).then_inc(c.ld.h, 16)
                c.ld.n += 32
            @c.block.vector
            def _(vector, dc=dc):
                wait_all(vector, c.ld)
                vector.wait_ge(s_g.h, s_g.n)
                vector.tensor_mul(out=ut[:, dc, :], in0=dtt[:, dc, :],
                                  in1=xstg[:, dc % 2, :])
                c.v.inc(vector.tensor_scalar_mul(out=yt[:, dc, :],
                                                 in0=xstg[:, dc % 2, :],
                                                 scalar1=dvt[:, dc:dc + 1]))
        uv = c.v.n
        for n in range(16):
            ns = n % 2
            @c.block.sync
            def _(sync, t0=t0, n=n, ns=ns):
                if s_g.n >= 8:
                    sync.wait_ge(s_g.h, s_g.n - 7)
                if s_it.n >= 8:
                    sync.wait_ge(s_it.h, s_it.n - 7)
                sync.dma_start(out=bb[:, ns, :],
                               in_=AP(r.bc_i, (bc0 + n) * T + t0, [[0, P], [1, TW]])
                               ).then_inc(s_bc.h, 16)
                sync.dma_start(out=cb[:, ns, :],
                               in_=AP(cbsrc if cbsrc is not None else r.bc_i,
                                      ((0 if cbsrc is not None else bc0 + 16) + n) * T
                                      + t0, [[0, P], [1, TW]])
                               ).then_inc(s_bc.h, 16)
                s_bc.n += 32
            for dc in range(8):
                it = it_ctr[0]
                it_ctr[0] += 1
                slot = it % 4
                @c.block.scalar
                def _(scalar, n=n, dc=dc, slot=slot, uv=uv):
                    scalar.wait_ge(c.v.h, uv)
                    if s_it.n >= 4:
                        scalar.wait_ge(s_it.h, s_it.n - 3)
                    c.s.inc(scalar.activation(out=dat[:, slot, :],
                                              in_=dtt[:, dc, :], func=ACT.Exp,
                                              scale=at[:, dc, n:n + 1]))
                @c.block.gpsimd
                def _(gpsimd, n=n, dc=dc, ns=ns, slot=slot, uv=uv):
                    gpsimd.wait_ge(c.v.h, uv)
                    gpsimd.wait_ge(s_bc.h, s_bc.n)
                    if s_it.n >= 4:
                        gpsimd.wait_ge(s_it.h, s_it.n - 3)
                    s_g.inc(gpsimd.tensor_tensor(out=vt[:, slot, :],
                                                 in0=ut[:, dc, :],
                                                 in1=bb[:, ns, :], op=AL.mult))
                cidx = dc * 16 + n
                @c.block.vector
                def _(vector, n=n, dc=dc, ns=ns, slot=slot, cidx=cidx):
                    vector.wait_ge(s_g.h, s_g.n)
                    vector.wait_ge(c.s.h, c.s.n)
                    vector.wait_ge(s_bc.h, s_bc.n)
                    s_it.inc(vector.tensor_tensor_scan(
                        out=ht[:, :], data0=dat[:, slot, :], data1=vt[:, slot, :],
                        initial=carry[:, cidx:cidx + 1], op0=AL.mult, op1=AL.add))
                    vector.tensor_mul(out=pt[:, slot % 2, :], in0=ht[:, :],
                                      in1=cb[:, ns, :])
                    vector.tensor_copy(out=carry[:, cidx:cidx + 1],
                                       in_=ht[:, TW - 1:TW])
                    c.v.inc(vector.tensor_add(out=yt[:, dc, :], in0=yt[:, dc, :],
                                              in1=pt[:, slot % 2, :]))
        for dc in range(8):
            @c.block.sync
            def _(sync, t0=t0, dc=dc):
                wait_all(sync, c.v)
                sync.dma_start(out=sgs[:, dc % 2, :],
                               in_=r.sg_i[dc * P:(dc + 1) * P, t0:t0 + TW]
                               ).then_inc(s_sg.h, 16)
                s_sg.n += 16
            @c.block.vector
            def _(vector, dc=dc):
                vector.wait_ge(s_sg.h, s_sg.n)
                vector.wait_ge(s_g.h, s_g.n)
                wait_all(vector, c.st)
                c.v.inc(vector.tensor_mul(out=ymt[:, dc % 2, :], in0=yt[:, dc, :],
                                          in1=sgs[:, dc % 2, :]))
            @c.block.gpsimd
            def _(gpsimd, t0=t0, dc=dc):
                wait_all(gpsimd, c.v)
                gpsimd.dma_start(out=r.ym_o[dc * P:(dc + 1) * P, t0:t0 + TW],
                                 in_=ymt[:, dc % 2, :]).then_inc(c.st.h, 16)
                c.st.n += 16


def build_l2():
    nc = bass.Bass("TRN2", target_bir_lowering=False)
    r = _R()
    r.dt_i = nc.dram_tensor("dt_i", [8 * P, T], F32, kind="ExternalInput")
    r.xs_i = nc.dram_tensor("xs_i", [8 * P, T], F32, kind="ExternalInput")
    r.sg_i = nc.dram_tensor("sg_i", [8 * P, T], F32, kind="ExternalInput")
    r.bc_i = nc.dram_tensor("bc_i", [32, T], F32, kind="ExternalInput")
    r.alog = nc.dram_tensor("alog", [P, 8, 16], F32, kind="ExternalInput")
    r.dvec = nc.dram_tensor("dvec", [P, 8], F32, kind="ExternalInput")
    r.ym_o = nc.dram_tensor("ym_o", [8 * P, T], BF16, kind="ExternalOutput")

    import contextlib
    _es = contextlib.ExitStack()
    at = _es.enter_context(nc.sbuf_tensor("at", [P, 8, 16], F32))
    dvt = _es.enter_context(nc.sbuf_tensor("dvt", [P, 8], F32))
    carry = _es.enter_context(nc.sbuf_tensor("carry", [P, P], F32))
    dtt = _es.enter_context(nc.sbuf_tensor("dtt", [P, 8, TW], F32))
    ut = _es.enter_context(nc.sbuf_tensor("ut", [P, 8, TW], F32))
    yt = _es.enter_context(nc.sbuf_tensor("yt", [P, 8, TW], F32))
    xstg = _es.enter_context(nc.sbuf_tensor("xstg", [P, 2, TW], F32))
    sgs = _es.enter_context(nc.sbuf_tensor("sgs", [P, 2, TW], F32))
    bb = _es.enter_context(nc.sbuf_tensor("bb", [P, 2, TW], F32))
    cb = _es.enter_context(nc.sbuf_tensor("cb", [P, 2, TW], F32))
    dat = _es.enter_context(nc.sbuf_tensor("dat", [P, 2, TW], F32))
    vt = _es.enter_context(nc.sbuf_tensor("vt", [P, TW], F32))
    ht = _es.enter_context(nc.sbuf_tensor("ht", [P, TW], F32))
    pt = _es.enter_context(nc.sbuf_tensor("pt", [P, TW], F32))
    ymt = _es.enter_context(nc.sbuf_tensor("ymt", [P, 2, TW], BF16))
    _s_ld = _es.enter_context(nc.semaphore("s_ld"))
    _s_st = _es.enter_context(nc.semaphore("s_st"))
    _s_mm = _es.enter_context(nc.semaphore("s_mm"))
    _s_v = _es.enter_context(nc.semaphore("s_v"))
    _s_s = _es.enter_context(nc.semaphore("s_s"))
    _qs = [_es.enter_context(nc.semaphore(f"s_q{i}")) for i in range(4)]
    block = _es.enter_context(nc.Block())
    if True:
        c = Ctx(nc, block, [Sem(_s_ld), Sem(_s_st), Sem(_s_mm), Sem(_s_v), Sem(_s_s)],
                [Sem(q) for q in _qs])
        r.at, r.dvt, r.carry = at, dvt, carry
        r.dtt, r.ut, r.yt = dtt, ut, yt
        r.xstg, r.sgs, r.bb, r.cb = xstg, sgs, bb, cb
        r.dat, r.vt, r.ht, r.pt, r.ymt = dat, vt, ht, pt, ymt
        _l2_prologue(c, r)
        _l2_body(c, r)
        @block.gpsimd
        def _(gpsimd):
            gpsimd.wait_ge(c.st.h, c.st.n)
    _es.close()
    return nc


def _l2_inputs(inputs, dtg, xsg, sgg, bcg):
    alog_full = np.asarray(inputs["L0_attn_A_log"], np.float32)
    d_full = np.asarray(inputs["L0_attn_D"], np.float32)
    maps = []
    for c in range(8):
        h = c // 4
        b = (c // 2) % 2
        dh = c % 2
        sl = slice(dh * 1024, dh * 1024 + 1024)
        maps.append({
            "dt_i": np.ascontiguousarray(dtg[h, b, sl]),
            "xs_i": np.ascontiguousarray(xsg[h, b, sl]),
            "sg_i": np.ascontiguousarray(sgg[h, b, sl]),
            "bc_i": np.ascontiguousarray(bcg[h, b]),
            "alog": np.ascontiguousarray(
                alog_full[h, sl].reshape(8, P, 16).transpose(1, 0, 2)),
            "dvec": np.ascontiguousarray(d_full[h, sl].reshape(8, P).T),
        })
    return maps


def run_l2(inputs, dtg, xsg, sgg, bcg):
    nc = build_l2()
    maps = _l2_inputs(inputs, dtg, xsg, sgg, bcg)
    res = run_bass_kernel_spmd(nc, maps, core_ids=list(range(8)))
    ymg = np.empty((NB, NI, T), BF)
    for c in range(8):
        h = c // 4
        b = (c // 2) % 2
        dh = c % 2
        ymg[b, h * HD + dh * 1024: h * HD + dh * 1024 + 1024] = res.results[c]["ym_o"]
    return ymg



TS = 512


def _bfly(c, vector, tiles, tiles2, nchunks):
    import math
    nst = int(math.log2(nchunks))
    assert nst % 2 == 0
    bufs = (tiles, tiles2)
    h = 1
    s = 0
    while h < nchunks:
        src_t, dst_t = bufs[s % 2], bufs[(s + 1) % 2]
        for j in range(0, nchunks, 2 * h):
            for k in range(j, j + h):
                a, b = src_t(k), src_t(k + h)
                vector.tensor_add(out=dst_t(k), in0=a, in1=b)
                vector.tensor_sub(out=dst_t(k + h), in0=a, in1=b)
        h *= 2
        s += 1


def _l3_transform(c, r):
    nc = c.nc
    import contextlib
    with contextlib.ExitStack() as es:
        wstg = es.enter_context(nc.sbuf_tensor("t_wstg", [P, 16, TS], F32))
        ps0, ps1 = r.ps0, r.ps1
        c.barrier()

        es_ug = contextlib.ExitStack()
        wb = es_ug.enter_context(nc.sbuf_tensor("t_wb", [P, 16, TS], BF16))
        wb2 = es_ug.enter_context(nc.sbuf_tensor("t_wb2", [P, 16, TS], BF16))
        for wsrc, wdst, sucol in ((r.wupT, r.wup2, 0), (r.wgT2, r.wg2, 1)):
            for mc in range(16):
                @c.block.sync
                def _(sync, wsrc=wsrc, mc=mc):
                    wait_all(sync, c.v, c.st)
                    sync.dma_start(
                        out=wstg[:, :, :],
                        in_=wsrc[:, mc * TS:(mc + 1) * TS]
                            .rearrange("(k p) m -> p k m", p=P)).then_inc(c.ld.h, 16)
                    c.ld.n += 16
                @c.block.vector
                def _(vector):
                    wait_all(vector, c.ld, c.mm)
                    c.v.inc(vector.tensor_copy(out=wb[:, :, :], in_=wstg[:, :, :]))
                for fc in range(16):
                    pp = ps0 if fc % 2 == 0 else ps1
                    @c.block.tensor
                    def _(tensor, fc=fc, pp=pp):
                        wait_all(tensor, c.v)
                        c.mm.inc(tensor.matmul(out=pp[:, :], lhsT=r.h128b[:, :],
                                               rhs=wb[:, fc, :], start=True, stop=True))
                    @c.block.vector
                    def _(vector, fc=fc, pp=pp):
                        wait_all(vector, c.mm)
                        c.v.inc(vector.tensor_copy(out=wb[:, fc, :], in_=pp[:, :]))
                @c.block.vector
                def _(vector, sucol=sucol):
                    wait_all(vector, c.mm)
                    _bfly(c, vector, lambda i: wb[:, i, :], lambda i: wb2[:, i, :], 16)
                    for fc in range(16):
                        ins = vector.tensor_scalar_mul(out=wb[:, fc, :], in0=wb[:, fc, :],
                                                       scalar1=r.suu2[:, fc:fc + 1])
                    c.v.inc(ins)
                @c.block.gpsimd
                def _(gpsimd, wdst=wdst, mc=mc):
                    wait_all(gpsimd, c.v)
                    gpsimd.dma_start(
                        out=wdst[:, mc * TS:(mc + 1) * TS]
                            .rearrange("(k p) m -> p k m", p=P),
                        in_=wb[:, :, :]).then_inc(c.st.h, 16)
                    c.st.n += 16

        es_ug.close()
        for mc in range(4):
            with contextlib.ExitStack() as es2:
                wd64 = es2.enter_context(
                    nc.sbuf_tensor(f"t_wd64_{mc}", [P, 64, TS], BF16))
                wd64b = es2.enter_context(
                    nc.sbuf_tensor(f"t_wd64b_{mc}", [P, 64, TS], BF16))
                for qq in range(4):
                    @c.block.sync
                    def _(sync, mc=mc, qq=qq):
                        wait_all(sync, c.v, c.st)
                        sync.dma_start(
                            out=wstg[:, :, :],
                            in_=r.wdT[qq * 2048:(qq + 1) * 2048, mc * TS:(mc + 1) * TS]
                                .rearrange("(k p) m -> p k m", p=P)).then_inc(c.ld.h, 16)
                        c.ld.n += 16
                    @c.block.vector
                    def _(vector, qq=qq):
                        wait_all(vector, c.ld)
                        c.v.inc(vector.tensor_copy(out=wd64[:, qq * 16:(qq + 1) * 16, :],
                                                   in_=wstg[:, :, :]))
                for ic in range(64):
                    pp = ps0 if ic % 2 == 0 else ps1
                    @c.block.tensor
                    def _(tensor, ic=ic, pp=pp):
                        wait_all(tensor, c.v)
                        c.mm.inc(tensor.matmul(out=pp[:, :], lhsT=r.h128b[:, :],
                                               rhs=wd64[:, ic, :], start=True, stop=True))
                    @c.block.vector
                    def _(vector, ic=ic, pp=pp):
                        wait_all(vector, c.mm)
                        c.v.inc(vector.tensor_copy(out=wd64[:, ic, :], in_=pp[:, :]))
                @c.block.vector
                def _(vector):
                    wait_all(vector, c.mm)
                    _bfly(c, vector, lambda i: wd64[:, i, :], lambda i: wd64b[:, i, :], 64)
                    for ic in range(64):
                        ins = vector.tensor_scalar_mul(out=wd64[:, ic, :],
                                                       in0=wd64[:, ic, :],
                                                       scalar1=r.sud2[:, ic:ic + 1])
                    c.v.inc(ins)
                @c.block.gpsimd
                def _(gpsimd, mc=mc):
                    wait_all(gpsimd, c.v)
                    gpsimd.dma_start(
                        out=r.wd2[:, mc * TS:(mc + 1) * TS]
                            .rearrange("(k p) m -> p k m", p=P),
                        in_=wd64[:, :, :]).then_inc(c.st.h, 16)
                    c.st.n += 16

        wob = es.enter_context(nc.sbuf_tensor("t_wob", [P, 16, TS], BF16))
        for mc in range(4):
            for qq in range(2):
                @c.block.sync
                def _(sync, mc=mc, qq=qq):
                    wait_all(sync, c.v, c.st)
                    sync.dma_start(
                        out=wstg[:, :, :],
                        in_=r.woT[qq * 2048:(qq + 1) * 2048, mc * TS:(mc + 1) * TS]
                            .rearrange("(k p) m -> p k m", p=P)).then_inc(c.ld.h, 16)
                    c.ld.n += 16
                @c.block.vector
                def _(vector):
                    wait_all(vector, c.ld, c.st)
                    c.v.inc(vector.tensor_copy(out=wob[:, :, :], in_=wstg[:, :, :]))
                @c.block.gpsimd
                def _(gpsimd, mc=mc, qq=qq):
                    wait_all(gpsimd, c.v)
                    gpsimd.dma_start(
                        out=r.wo2[qq * 2048:(qq + 1) * 2048, mc * TS:(mc + 1) * TS]
                            .rearrange("(k p) m -> p k m", p=P),
                        in_=wob[:, :, :]).then_inc(c.st.h, 16)
                    c.st.n += 16


def _l3_ln(c, r, ts, xst4, z):
    t0 = ts * TS + getattr(r, "x_off", 0)
    mm_base = c.mm.n
    s_base = c.s.n
    for fc in range(16):
        qs = c.q[fc % 2]
        @c.block.sync
        def _(sync, fc=fc, t0=t0, qs=qs):
            if fc == 0:
                wait_all(sync, c.mm, c.v)
            if fc >= 2:
                sync.wait_ge(c.mm.h, mm_base + fc - 1)
            sync.dma_start(out=xst4[:, fc % 2, :],
                           in_=r.x_i[fc * P:(fc + 1) * P, t0:t0 + TS]
                           ).then_inc(qs.h, 16)
            qs.n += 16
        @c.block.scalar
        def _(scalar, fc=fc, qs=qs):
            scalar.wait_ge(qs.h, qs.n)
            if fc == 0:
                wait_all(scalar, c.mm, c.v)
            if fc >= 2:
                scalar.wait_ge(c.mm.h, mm_base + fc - 1)
            c.s.inc(scalar.activation(out=r.sq[:, fc % 2, :], in_=xst4[:, fc % 2, :],
                                      func=ACT.Square))
        @c.block.tensor
        def _(tensor, fc=fc, qs=qs):
            tensor.wait_ge(qs.h, qs.n)
            if fc == 0:
                wait_all(tensor, c.v)
            tensor.matmul(out=r.psA[:, :], lhsT=r.ones_col[:, :],
                          rhs=xst4[:, fc % 2, :], start=(fc == 0), stop=(fc == 15))
            tensor.wait_ge(c.s.h, s_base + fc + 1)
            c.mm.inc(tensor.matmul(out=r.psB[:, :], lhsT=r.ones_col[:, :],
                                   rhs=r.sq[:, fc % 2, :],
                                   start=(fc == 0), stop=(fc == 15)))
    @c.block.vector
    def _(vector):
        wait_all(vector, c.mm)
        vector.tensor_scalar_mul(out=r.stat[:, 0, :], in0=r.psA[:, :], scalar1=1.0 / DM)
        c.v.inc(vector.tensor_scalar_mul(out=r.stat[:, 1, :], in0=r.psB[:, :],
                                         scalar1=1.0 / DM))
        vector.wait_ge(c.v.h, c.v.n)
        c.v.inc(vector.tensor_mul(out=r.stat[:, 2, :], in0=r.stat[:, 0, :],
                                  in1=r.stat[:, 0, :]))
        vector.wait_ge(c.v.h, c.v.n)
        c.v.inc(vector.tensor_sub(out=r.stat[:, 3, :], in0=r.stat[:, 1, :],
                                  in1=r.stat[:, 2, :]))
    @c.block.scalar
    def _(scalar):
        wait_all(scalar, c.v)
        scalar.activation(out=r.stat[:, 4, :], in_=r.stat[:, 3, :], func=ACT.Ln,
                          bias=r.consts[:, 0:1])
        c.s.inc(scalar.activation(out=r.stat[:, 5, :], in_=r.stat[:, 4, :],
                                  func=ACT.Exp, scale=r.consts[:, 1:2]))
    @c.block.tensor
    def _(tensor):
        wait_all(tensor, c.s)
        tensor.matmul(out=r.psM[:, :], lhsT=r.ones_row[:, :], rhs=r.stat[:, 0, :],
                      start=True, stop=True)
        c.mm.inc(tensor.matmul(out=r.psR[:, :], lhsT=r.ones_row[:, :],
                               rhs=r.stat[:, 5, :], start=True, stop=True))
    v_base = c.v.n
    for fc in range(16):
        qs = c.q[fc % 2]
        @c.block.sync
        def _(sync, fc=fc, t0=t0, qs=qs):
            if fc == 0:
                wait_all(sync, c.mm)
            if fc >= 2:
                sync.wait_ge(c.v.h, v_base + fc - 1)
            sync.dma_start(out=xst4[:, fc % 2, :],
                           in_=r.x_i[fc * P:(fc + 1) * P, t0:t0 + TS]
                           ).then_inc(qs.h, 16)
            qs.n += 16
        @c.block.vector
        def _(vector, fc=fc, qs=qs):
            vector.wait_ge(qs.h, qs.n)
            if fc == 0:
                wait_all(vector, c.mm)
                if DBG:
                    wait_all(vector, c.st)
            vector.tensor_sub(out=r.sq[:, fc % 2, :], in0=xst4[:, fc % 2, :],
                              in1=r.psM[:, :])
            c.v.inc(vector.tensor_tensor(out=z[:, fc, :], in0=r.sq[:, fc % 2, :],
                                         in1=r.psR[:, :], op=AL.mult))


def _l3_main(c, r):
    nc = c.nc
    ps0, ps1, ps2, ps3 = r.ps0, r.ps1, r.ps2, r.ps3
    with (
        nc.sbuf_tensor("xst4", [P, 2, TS], F32) as xst4,
        nc.sbuf_tensor("f2", [P, 64, TS], BF16) as f2,
        nc.sbuf_tensor("gbuf", [P, 16, TS], BF16) as gbuf,
        nc.sbuf_tensor("wu_s", [P, 2, 16, P], BF16) as wu_s,
        nc.sbuf_tensor("wg_s", [P, 2, 16, P], BF16) as wg_s,
        nc.sbuf_tensor("wd_s", [P, 2, 64, P], BF16) as wd_s,
        nc.sbuf_tensor("wo_s", [P, 2, 32, P], BF16) as wo_s,
        nc.sbuf_tensor("gs", [P, TS], BF16) as gs,
        nc.sbuf_tensor("xres", [P, 2, TS], F32) as xres,
        nc.sbuf_tensor("sout", [P, 2, TS], F32) as sout,
    ):
        for ts in range(TPC // TS):
            t0 = ts * TS
            import contextlib as _ctx
            _zs = _ctx.ExitStack()
            z = _zs.enter_context(nc.sbuf_tensor(f"z{id(r) % 97}_{ts}", [P, 16, TS], BF16))
            _l3_ln(c, r, ts, xst4, z)
            if DBG and getattr(r, "z_o", None) is not None:
                @c.block.gpsimd
                def _(gpsimd, t0=t0, z=z):
                    wait_all(gpsimd, c.v)
                    for fc in range(16):
                        gpsimd.dma_start(out=r.z_o[fc * P:(fc + 1) * P, t0:t0 + TS],
                                         in_=z[:, fc, :]).then_inc(c.st.h, 16)
                    c.st.n += 16 * 16
            mm_ug = c.mm.n
            for ic in range(64):
                ws = ic % 2
                qs = c.q[ws]
                psU = ps0 if ic % 2 == 0 else ps1
                psG = ps2 if ic % 2 == 0 else ps3
                @c.block.sync
                def _(sync, ic=ic, ws=ws, qs=qs, mm_ug=mm_ug):
                    if ic >= 2:
                        sync.wait_ge(c.mm.h, mm_ug + ic - 1)
                    else:
                        wait_all(sync, c.mm)
                    sync.dma_start(
                        out=wu_s[:, ws, :, :],
                        in_=r.wup2[:, ic * P:(ic + 1) * P]
                            .rearrange("(k p) m -> p k m", p=P)).then_inc(qs.h, 16)
                    sync.dma_start(
                        out=wg_s[:, ws, :, :],
                        in_=r.wg2[:, ic * P:(ic + 1) * P]
                            .rearrange("(k p) m -> p k m", p=P)).then_inc(qs.h, 16)
                    qs.n += 32
                @c.block.tensor
                def _(tensor, ic=ic, ws=ws, psU=psU, psG=psG, qs=qs):
                    tensor.wait_ge(qs.h, qs.n)
                    wait_all(tensor, c.v, c.s)
                    for kc in range(16):
                        tensor.matmul(out=psU[:, :], lhsT=wu_s[:, ws, kc, :],
                                      rhs=z[:, kc, :], start=(kc == 0),
                                      stop=(kc == 15))
                    for kc in range(16):
                        ins = tensor.matmul(out=psG[:, :], lhsT=wg_s[:, ws, kc, :],
                                            rhs=z[:, kc, :], start=(kc == 0),
                                            stop=(kc == 15))
                    c.mm.inc(ins)
                @c.block.scalar
                def _(scalar, ic=ic, psG=psG):
                    wait_all(scalar, c.mm, c.v)
                    c.s.inc(scalar.activation(out=gs[:, :], in_=psG[:, :],
                                              func=ACT.Silu,
                                              scale=r.wsg2[:, ic:ic + 1]))
                @c.block.vector
                def _(vector, ic=ic, psU=psU):
                    wait_all(vector, c.s)
                    if DBG and ic == 0:
                        wait_all(vector, c.st)
                    c.v.inc(vector.scalar_tensor_tensor(
                        out=f2[:, ic, :], in0=psU[:, :],
                        scalar=r.wsu2[:, ic:ic + 1], in1=gs[:, :],
                        op0=AL.mult, op1=AL.mult))
            mm_dn = c.mm.n
            for oc in range(16):
                ws = oc % 2
                qs = c.q[2 + ws]
                pp = ps0 if oc % 2 == 0 else ps1
                @c.block.sync
                def _(sync, oc=oc, ws=ws, qs=qs, mm_dn=mm_dn):
                    if oc >= 2:
                        sync.wait_ge(c.mm.h, mm_dn + oc - 1)
                    else:
                        wait_all(sync, c.mm)
                    sync.dma_start(
                        out=wd_s[:, ws, :, :],
                        in_=r.wd2[:, oc * P:(oc + 1) * P]
                            .rearrange("(k p) m -> p k m", p=P)).then_inc(qs.h, 16)
                    qs.n += 16
                @c.block.tensor
                def _(tensor, oc=oc, ws=ws, pp=pp, qs=qs):
                    tensor.wait_ge(qs.h, qs.n)
                    wait_all(tensor, c.s)
                    if oc == 0:
                        wait_all(tensor, c.v)
                    for ic in range(64):
                        ins = tensor.matmul(out=pp[:, :], lhsT=wd_s[:, ws, ic, :],
                                            rhs=f2[:, ic, :], start=(ic == 0),
                                            stop=(ic == 63))
                    c.mm.inc(ins)
                @c.block.scalar
                def _(scalar, oc=oc, pp=pp):
                    wait_all(scalar, c.mm)
                    c.s.inc(scalar.activation(out=gbuf[:, oc, :], in_=pp[:, :],
                                              func=ACT.Silu,
                                              scale=r.wsd2[:, oc:oc + 1]))
            if DBG and getattr(r, "g_o", None) is not None:
                @c.block.gpsimd
                def _(gpsimd, t0=t0):
                    wait_all(gpsimd, c.s, c.v)
                    for oc in range(16):
                        gpsimd.dma_start(out=r.g_o[oc * P:(oc + 1) * P, t0:t0 + TS],
                                         in_=gbuf[:, oc, :]).then_inc(c.st.h, 16)
                    for ic in range(64):
                        gpsimd.dma_start(out=r.f2_o[ic * P:(ic + 1) * P, t0:t0 + TS],
                                         in_=f2[:, ic, :]).then_inc(c.st.h, 16)
                    c.st.n += 80 * 16
            _zs.close()
            _ys = _ctx.ExitStack()
            ymst = _ys.enter_context(
                nc.sbuf_tensor(f"ym{id(r) % 97}_{ts}", [P, 32, TS], BF16))
            if getattr(r, "ym_reg", None) is not None:
                @c.block.gpsimd
                def _(gpsimd, t0=t0):
                    wait_all(gpsimd, c.mm)
                    rq, rtmp = r.ym_reg
                    for icc in range(32):
                        gpsimd.reg_add(rtmp, rq, icc * P * T + (t0 - getattr(r, "x_off", 0)))
                        gpsimd.dma_start(out=ymst[:, icc, :],
                                         in_=AP(r.ym_i, rtmp, [[T, P], [1, TS]])
                                         ).then_inc(c.ld.h, 16)
                    c.ld.n += 32 * 16
            else:
                @c.block.sync
                def _(sync, t0=t0):
                    wait_all(sync, c.mm)
                    for icc in range(32):
                        sync.dma_start(out=ymst[:, icc, :],
                                       in_=r.ym_i[icc * P:(icc + 1) * P, t0:t0 + TS]
                                       ).then_inc(c.ld.h, 16)
                    c.ld.n += 32 * 16
            mm_op = c.mm.n
            snapo = {}
            for oc in range(16):
                ws = oc % 2
                qs = c.q[ws]
                pp = ps2 if oc % 2 == 0 else ps3
                @c.block.sync
                def _(sync, oc=oc, ws=ws, t0=t0, qs=qs, mm_op=mm_op):
                    if oc >= 2:
                        sync.wait_ge(c.mm.h, mm_op + oc - 1)
                        sync.wait_ge(c.v.h, snapo[oc - 2])
                    else:
                        wait_all(sync, c.mm, c.v)
                    sync.dma_start(
                        out=wo_s[:, ws, :, :],
                        in_=r.wo2[:, oc * P:(oc + 1) * P]
                            .rearrange("(k p) m -> p k m", p=P)).then_inc(qs.h, 16)
                    sync.dma_start(out=xres[:, ws, :],
                                   in_=r.x_i[oc * P:(oc + 1) * P, t0:t0 + TS]
                                   ).then_inc(qs.h, 16)
                    qs.n += 32
                @c.block.tensor
                def _(tensor, oc=oc, ws=ws, pp=pp, qs=qs):
                    tensor.wait_ge(qs.h, qs.n)
                    if oc - 2 in snapo:
                        tensor.wait_ge(c.v.h, snapo[oc - 2])
                    else:
                        wait_all(tensor, c.v)
                    for icc in range(32):
                        ins = tensor.matmul(out=pp[:, :], lhsT=wo_s[:, ws, icc, :],
                                            rhs=ymst[:, icc, :], start=(icc == 0),
                                            stop=(icc == 31))
                    c.mm.inc(ins)
                @c.block.vector
                def _(vector, oc=oc, ws=ws, pp=pp):
                    wait_all(vector, c.mm, c.st)
                    if oc == 0:
                        wait_all(vector, c.s)
                    vector.tensor_mul(out=sout[:, ws, :], in0=pp[:, :],
                                      in1=gbuf[:, oc, :])
                    c.v.inc(vector.tensor_add(out=sout[:, ws, :],
                                              in0=sout[:, ws, :],
                                              in1=xres[:, ws, :]))
                snapo[oc] = c.v.n
                @c.block.gpsimd
                def _(gpsimd, oc=oc, ws=ws, t0=t0):
                    wait_all(gpsimd, c.v)
                    gpsimd.dma_start(out=r.out_o[oc * P:(oc + 1) * P, t0:t0 + TS],
                                     in_=sout[:, ws, :]).then_inc(c.st.h, 16)
                    c.st.n += 16
            _ys.close()



def build_l3():
    nc = bass.Bass("TRN2", target_bir_lowering=False)
    r = _R()
    r.x_i = nc.dram_tensor("x_i", [DM, TPC], F32, kind="ExternalInput")
    r.ym_i = nc.dram_tensor("ym_i", [NI, TPC], BF16, kind="ExternalInput")
    r.woT = nc.dram_tensor("woT", [NI, DM], F32, kind="ExternalInput")
    r.wupT = nc.dram_tensor("wupT", [DM, 8192], F32, kind="ExternalInput")
    r.wgT2 = nc.dram_tensor("wgT2", [DM, 8192], F32, kind="ExternalInput")
    r.wdT = nc.dram_tensor("wdT", [8192, DM], F32, kind="ExternalInput")
    r.suu = nc.dram_tensor("suu", [P, 16], F32, kind="ExternalInput")
    r.sud = nc.dram_tensor("sud", [P, 64], F32, kind="ExternalInput")
    r.wsu = nc.dram_tensor("wsu", [P, 64], F32, kind="ExternalInput")
    r.wsg = nc.dram_tensor("wsg", [P, 64], F32, kind="ExternalInput")
    r.wsd = nc.dram_tensor("wsd", [P, 16], F32, kind="ExternalInput")
    r.h128 = nc.dram_tensor("h128", [P, P], F32, kind="ExternalInput")
    r.out_o = nc.dram_tensor("out_o", [DM, TPC], F32, kind="ExternalOutput")
    if DBG:
        r.z_o = nc.dram_tensor("z_o", [DM, TPC], BF16, kind="ExternalOutput")
        r.g_o = nc.dram_tensor("g_o", [DM, TPC], BF16, kind="ExternalOutput")
        r.f2_o = nc.dram_tensor("f2_o", [8192, TPC], BF16, kind="ExternalOutput")
    r.wup2 = nc.dram_tensor("wup2", [DM, 8192], BF16)
    r.wg2 = nc.dram_tensor("wg2", [DM, 8192], BF16)
    r.wd2 = nc.dram_tensor("wd2", [8192, DM], BF16)
    r.wo2 = nc.dram_tensor("wo2", [NI, DM], BF16)
    if DBG:
        r.xs_d = nc.dram_tensor("xs_d", [1024, T], F32, kind="ExternalOutput")
        r.sg_d = nc.dram_tensor("sg_d", [1024, T], F32, kind="ExternalOutput")
        r.ssm_d = nc.dram_tensor("ssm_d", [160, T], F32, kind="ExternalOutput")
        r.dt_d = nc.dram_tensor("dt_d", [1024, T], F32, kind="ExternalOutput")
        r.ym_d = nc.dram_tensor("ym_d", [1024, T], BF16, kind="ExternalOutput")
        r.at_d = nc.dram_tensor("at_d", [P, 128], F32, kind="ExternalOutput")
        r.da_d = nc.dram_tensor("da_d", [P, 2048], F32, kind="ExternalOutput")
        r.vt_d = nc.dram_tensor("vt_d", [P, 5 * 1024], F32, kind="ExternalOutput")
        r.ymq_d = nc.dram_tensor("ymq_d", [NI, TPC], BF16, kind="ExternalOutput")

    import contextlib
    _es = contextlib.ExitStack()
    r.ones_col = _es.enter_context(nc.sbuf_tensor("ones_col", [P, 1], F32))
    r.ones_row = _es.enter_context(nc.sbuf_tensor("ones_row", [1, P], F32))
    r.consts = _es.enter_context(nc.sbuf_tensor("consts", [1, 4], F32))
    r.h128f = _es.enter_context(nc.sbuf_tensor("h128f", [P, P], F32))
    r.h128b = _es.enter_context(nc.sbuf_tensor("h128b", [P, P], BF16))
    r.suu2 = _es.enter_context(nc.sbuf_tensor("suu2", [P, 16], F32))
    r.sud2 = _es.enter_context(nc.sbuf_tensor("sud2", [P, 64], F32))
    r.wsu2 = _es.enter_context(nc.sbuf_tensor("wsu2", [P, 64], F32))
    r.wsg2 = _es.enter_context(nc.sbuf_tensor("wsg2", [P, 64], F32))
    r.wsd2 = _es.enter_context(nc.sbuf_tensor("wsd2", [P, 16], F32))
    r.sq = _es.enter_context(nc.sbuf_tensor("sq", [P, 2, TS], F32))
    r.stat = _es.enter_context(nc.sbuf_tensor("stat", [1, 6, TS], F32))
    r.psA = _es.enter_context(nc.psum_tensor("psA", [1, TS], F32))
    r.psB = _es.enter_context(nc.psum_tensor("psB", [1, TS], F32))
    r.psM = _es.enter_context(nc.psum_tensor("psM", [P, TS], F32))
    r.psR = _es.enter_context(nc.psum_tensor("psR", [P, TS], F32))
    ps0 = _es.enter_context(nc.psum_tensor("ps0", [P, TS], F32))
    ps1 = _es.enter_context(nc.psum_tensor("ps1", [P, TS], F32))
    ps2 = _es.enter_context(nc.psum_tensor("ps2", [P, TS], F32))
    ps3 = _es.enter_context(nc.psum_tensor("ps3", [P, TS], F32))
    r.ps0, r.ps1, r.ps2, r.ps3 = ps0, ps1, ps2, ps3
    _s_ld = _es.enter_context(nc.semaphore("s_ld"))
    _s_st = _es.enter_context(nc.semaphore("s_st"))
    _s_mm = _es.enter_context(nc.semaphore("s_mm"))
    _s_v = _es.enter_context(nc.semaphore("s_v"))
    _s_s = _es.enter_context(nc.semaphore("s_s"))
    _qs = [_es.enter_context(nc.semaphore(f"s_q{i}")) for i in range(4)]
    block = _es.enter_context(nc.Block())
    if True:
        c = Ctx(nc, block, [Sem(_s_ld), Sem(_s_st), Sem(_s_mm), Sem(_s_v), Sem(_s_s)],
                [Sem(q) for q in _qs])

        @block.sync
        def _(sync):
            sync.dma_start(out=r.h128f[:, :], in_=r.h128[:, :]).then_inc(c.ld.h, 16)
            sync.dma_start(out=r.suu2[:, :], in_=r.suu[:, :]).then_inc(c.ld.h, 16)
            sync.dma_start(out=r.sud2[:, :], in_=r.sud[:, :]).then_inc(c.ld.h, 16)
            sync.dma_start(out=r.wsu2[:, :], in_=r.wsu[:, :]).then_inc(c.ld.h, 16)
            sync.dma_start(out=r.wsg2[:, :], in_=r.wsg[:, :]).then_inc(c.ld.h, 16)
            sync.dma_start(out=r.wsd2[:, :], in_=r.wsd[:, :]).then_inc(c.ld.h, 16)
            c.ld.n += 6 * 16

        @block.vector
        def _(vector):
            vector.memset(r.ones_col[:, :], 1.0)
            vector.memset(r.ones_row[:, :], 1.0)
            vector.memset(r.consts[:, 0:1], LN_EPS)
            vector.memset(r.consts[:, 1:2], -0.5)
            wait_all(vector, c.ld)
            vector.tensor_copy(out=r.h128b[:, :], in_=r.h128f[:, :])
            vector.tensor_scalar_mul(out=r.suu2[:, :], in0=r.suu2[:, :],
                                     scalar1=1.0 / (64.0 * float(np.sqrt(2048.0))))
            vector.tensor_scalar_mul(out=r.sud2[:, :], in0=r.sud2[:, :],
                                     scalar1=1.0 / (64.0 * float(np.sqrt(8192.0))))
            vector.tensor_scalar_mul(out=r.wsu2[:, :], in0=r.wsu2[:, :], scalar1=64.0)
            vector.tensor_scalar_mul(out=r.wsg2[:, :], in0=r.wsg2[:, :], scalar1=64.0)
            c.v.inc(vector.tensor_scalar_mul(out=r.wsd2[:, :], in0=r.wsd2[:, :],
                                             scalar1=64.0))
            vector.wait_ge(c.v.h, c.v.n)

        _l3_transform(c, r)
        c.barrier()

        _l3_main(c, r)

        @block.gpsimd
        def _(gpsimd):
            gpsimd.wait_ge(c.st.h, c.st.n)
    _es.close()
    return nc


def _hadamard(n):
    h = np.array([[1.0]], np.float32)
    while h.shape[0] < n:
        h = np.block([[h, h], [h, -h]])
    return h.astype(np.float32)


def _l3_inputs(inputs, xT, ymg):
    woT = np.ascontiguousarray(np.asarray(inputs["L0_attn_out_proj_w"], np.float32).T)
    wupT = np.ascontiguousarray(np.asarray(inputs["L0_ffn_W_up"], np.float32).T)
    wgT2 = np.ascontiguousarray(np.asarray(inputs["L0_ffn_W_gate"], np.float32).T)
    wdT = np.ascontiguousarray(np.asarray(inputs["L0_ffn_W_down"], np.float32).T)
    suu = np.ascontiguousarray(
        np.asarray(inputs["L0_ffn_SU_ug"], np.float32).reshape(16, P).T)
    sud = np.ascontiguousarray(
        np.asarray(inputs["L0_ffn_SU_dp"], np.float32).reshape(64, P).T)
    wsall = np.asarray(inputs["L0_ffn_Wscale_ug"], np.float32)
    wsu = np.ascontiguousarray(wsall[:8192].reshape(64, P).T)
    wsg = np.ascontiguousarray(wsall[8192:].reshape(64, P).T)
    wsd = np.ascontiguousarray(
        np.asarray(inputs["L0_ffn_Wscale_dp"], np.float32).reshape(16, P).T)
    h128 = _hadamard(P)
    maps = []
    for cc in range(8):
        b = (cc // 2) % 2
        q = 2 * (cc // 4) + (cc % 2)
        s0 = q * TPC
        maps.append({
            "x_i": np.ascontiguousarray(xT[b, :, s0:s0 + TPC]),
            "ym_i": np.ascontiguousarray(ymg[b, :, s0:s0 + TPC]),
            "woT": woT, "wupT": wupT, "wgT2": wgT2, "wdT": wdT,
            "suu": suu, "sud": sud, "wsu": wsu, "wsg": wsg, "wsd": wsd,
            "h128": h128,
        })
    return maps


def run_l3(inputs, xT, ymg):
    nc = build_l3()
    maps = _l3_inputs(inputs, xT, ymg)
    res = run_bass_kernel_spmd(nc, maps, core_ids=list(range(8)))
    out = np.empty((NB, DM, T), np.float32)
    for cc in range(8):
        b = (cc // 2) % 2
        q = 2 * (cc // 4) + (cc % 2)
        s0 = q * TPC
        out[b, :, s0:s0 + TPC] = res.results[cc]["out_o"]
    return out



def kernel_v1(**inputs):
    xT = np.ascontiguousarray(
        np.asarray(inputs["L0_attn_hidden_states"], np.float32).transpose(0, 2, 1))
    dtg, xsg, sgg, bcg = run_l1(inputs, xT)
    ymg = run_l2(inputs, dtg, xsg, sgg, bcg)
    out = run_l3(inputs, xT, ymg)
    return np.ascontiguousarray(out.transpose(0, 2, 1))



XPAD = T + 8


def _f_ln_chunk(c, r, k, xnb, xstage, sq, stat):
    base = k * HT
    @c.block.vector
    def _(vector):
        wait_all(vector, c.mm, c.s)
        c.v.inc(vector.memset(xnb[:, :, 4 * TT + 3:], 0.0))
    for (c0, w) in L1_TILES:
        @c.block.sync
        def _(sync, c0=c0, w=w):
            wait_all(sync, c.mm, c.v)
            for fc in range(16):
                sync.dma_start(out=xstage[:, fc, :w],
                               in_=r.x_pad[fc * P:(fc + 1) * P, base + c0:base + c0 + w]
                               ).then_inc(c.ld.h, 16)
            c.ld.n += 16 * 16
        @c.block.tensor
        def _(tensor, w=w):
            wait_all(tensor, c.ld, c.v)
            for fc in range(16):
                tensor.matmul(out=r.psA[:, :w], lhsT=r.ones_col[:, :],
                              rhs=xstage[:, fc, :w], start=(fc == 0), stop=(fc == 15))
        mm_base = c.mm.n
        s_base = c.s.n
        for fc in range(16):
            @c.block.scalar
            def _(scalar, w=w, fc=fc):
                if fc == 0:
                    wait_all(scalar, c.ld, c.mm, c.v)
                if fc >= 2:
                    scalar.wait_ge(c.mm.h, mm_base + fc - 1)
                c.s.inc(scalar.activation(out=sq[:, fc % 2, :w],
                                          in_=xstage[:, fc, :w], func=ACT.Square))
            @c.block.tensor
            def _(tensor, w=w, fc=fc):
                tensor.wait_ge(c.s.h, s_base + fc + 1)
                c.mm.inc(tensor.matmul(out=r.psB[:, :w], lhsT=r.ones_col[:, :],
                                       rhs=sq[:, fc % 2, :w],
                                       start=(fc == 0), stop=(fc == 15)))
        @c.block.vector
        def _(vector, w=w):
            wait_all(vector, c.mm)
            vector.tensor_scalar_mul(out=stat[:, 0, :w], in0=r.psA[:, :w],
                                     scalar1=1.0 / DM)
            c.v.inc(vector.tensor_scalar_mul(out=stat[:, 1, :w], in0=r.psB[:, :w],
                                             scalar1=1.0 / DM))
            vector.wait_ge(c.v.h, c.v.n)
            c.v.inc(vector.tensor_mul(out=stat[:, 2, :w], in0=stat[:, 0, :w],
                                      in1=stat[:, 0, :w]))
            vector.wait_ge(c.v.h, c.v.n)
            c.v.inc(vector.tensor_sub(out=stat[:, 3, :w], in0=stat[:, 1, :w],
                                      in1=stat[:, 2, :w]))
        @c.block.scalar
        def _(scalar, w=w):
            wait_all(scalar, c.v)
            scalar.activation(out=stat[:, 4, :w], in_=stat[:, 3, :w],
                              func=ACT.Ln, bias=r.consts[:, 0:1])
            c.s.inc(scalar.activation(out=stat[:, 5, :w], in_=stat[:, 4, :w],
                                      func=ACT.Exp, scale=r.consts[:, 1:2]))
        @c.block.tensor
        def _(tensor, w=w):
            wait_all(tensor, c.s)
            tensor.matmul(out=r.psM[:, :w], lhsT=r.ones_row[:, :],
                          rhs=stat[:, 0, :w], start=True, stop=True)
            c.mm.inc(tensor.matmul(out=r.psR[:, :w], lhsT=r.ones_row[:, :],
                                   rhs=stat[:, 5, :w], start=True, stop=True))
        @c.block.vector
        def _(vector, c0=c0, w=w):
            wait_all(vector, c.mm)
            for fc in range(16):
                vector.tensor_sub(out=xstage[:, fc, :w], in0=xstage[:, fc, :w],
                                  in1=r.psM[:, :w])
            for fc in range(16):
                ins = vector.tensor_tensor(out=xnb[:, fc, c0:c0 + w],
                                           in0=xstage[:, fc, :w],
                                           in1=r.psR[:, :w], op=AL.mult)
            c.v.inc(ins)


def _f_inproj_chunk(c, r, k, xnb, bufs):
    xbuf, cs, xst, xst16, sgt = bufs
    base = k * HT
    snap_v = {}
    snap_s = {}
    ln_v = c.v.n
    halo_v = {}
    for mc in range(16):
        is_x = mc < 8
        wmc = mc
        ch = (mc if is_x else mc - 8) * P
        if is_x:
            @c.block.tensor
            def _(tensor, wmc=wmc, mc=mc):
                tensor.wait_ge(c.v.h, max(ln_v, halo_v.get(mc - 1, 0)))
                for kc in range(16):
                    ins = tensor.matmul(out=r.ps2[:, 0:3], lhsT=r.wres[:, kc, wmc, :],
                                        rhs=xnb[:, kc, 0:3],
                                        start=(kc == 0), stop=(kc == 15))
                c.mm.inc(ins)
            @c.block.vector
            def _(vector):
                wait_all(vector, c.mm)
                c.v.inc(vector.tensor_copy(out=xbuf[:, 0:3], in_=r.ps2[:, 0:3]))
            halo_v[mc] = c.v.n
        for ti, (c0, w) in enumerate(L1_TILES):
            li = mc * 5 + ti
            pp = r.ps0 if li % 2 == 0 else r.ps1
            @c.block.tensor
            def _(tensor, wmc=wmc, c0=c0, w=w, pp=pp, li=li):
                if li - 2 in snap_v:
                    tensor.wait_ge(c.v.h, snap_v[li - 2])
                    tensor.wait_ge(c.s.h, snap_s[li - 2])
                else:
                    wait_all(tensor, c.v, c.s)
                for kc in range(16):
                    ins = tensor.matmul(out=pp[:, :w], lhsT=r.wres[:, kc, wmc, :],
                                        rhs=xnb[:, kc, c0:c0 + w],
                                        start=(kc == 0), stop=(kc == 15))
                c.mm.inc(ins)
            so, do, wd = _store_win(ti, w)
            do_g = do + base
            if is_x:
                @c.block.vector
                def _(vector, w=w, pp=pp, ti=ti, mc=mc):
                    small = w < 256
                    def step(ins):
                        c.v.inc(ins)
                        if small:
                            vector.wait_ge(c.v.h, c.v.n)
                    wait_all(vector, c.mm, c.st)
                    if ti > 0:
                        step(vector.tensor_copy(out=xbuf[:, 0:3],
                                                in_=xbuf[:, TT:TT + 3]))
                    step(vector.tensor_copy(out=xbuf[:, 3:3 + w], in_=pp[:, :w]))
                    step(vector.tensor_scalar_mul(out=cs[:, :w], in0=xbuf[:, 0:w],
                                                  scalar1=r.cwt[:, mc, 0:1]))
                    for kk in (1, 2, 3):
                        step(vector.scalar_tensor_tensor(
                            out=cs[:, :w], in0=xbuf[:, kk:kk + w],
                            scalar=r.cwt[:, mc, kk:kk + 1], in1=cs[:, :w],
                            op0=AL.mult, op1=AL.add))
                @c.block.scalar
                def _(scalar, w=w, mc=mc):
                    wait_all(scalar, c.v)
                    c.s.inc(scalar.activation(out=xst[:, :w], in_=cs[:, :w],
                                              func=ACT.Silu, bias=r.cbt[:, mc:mc + 1]))
                @c.block.vector
                def _(vector, w=w):
                    wait_all(vector, c.s)
                    c.v.inc(vector.tensor_copy(out=xst16[:, :w], in_=xst[:, :w]))
                if wd > 0:
                    @c.block.gpsimd
                    def _(gpsimd, ch=ch, so=so, do_g=do_g, wd=wd):
                        wait_all(gpsimd, c.v)
                        gpsimd.dma_start(
                            out=r.xs_hbm[ch:ch + P, do_g:do_g + wd],
                            in_=xst[:, so:so + wd]).then_inc(c.st.h, 16)
                        gpsimd.dma_start(
                            out=r.xsb_hbm[ch:ch + P, do_g:do_g + wd],
                            in_=xst16[:, so:so + wd]).then_inc(c.st.h, 16)
                        c.st.n += 32
            else:
                @c.block.scalar
                def _(scalar, w=w, pp=pp, ti=ti):
                    wait_all(scalar, c.mm, c.st)
                    c.s.inc(scalar.activation(out=sgt[:, ti % 2, :w], in_=pp[:, :w],
                                              func=ACT.Silu))
                if wd > 0:
                    @c.block.gpsimd
                    def _(gpsimd, ch=ch, so=so, do_g=do_g, wd=wd, ti=ti):
                        wait_all(gpsimd, c.s)
                        gpsimd.dma_start(
                            out=r.sg_hbm[ch:ch + P, do_g:do_g + wd],
                            in_=sgt[:, ti % 2, so:so + wd]).then_inc(c.st.h, 16)
                        c.st.n += 16
            snap_v[li] = c.v.n
            snap_s[li] = c.s.n


def _f_xproj(c, r):
    nc = c.nc
    import contextlib
    with contextlib.ExitStack() as es:
        xpst = es.enter_context(nc.sbuf_tensor("f_xpst", [P, 8, 160], F32))
        xpsb = es.enter_context(nc.sbuf_tensor("f_xpsb", [P, 8, 160], BF16))
        xsl = es.enter_context(nc.sbuf_tensor("f_xsl", [P, 2, TT], BF16))
        spdt = es.enter_context(nc.sbuf_tensor("f_spdt", [P, 2, TT], F32))
        spbc = es.enter_context(nc.sbuf_tensor("f_spbc", [32, 2, TT], F32))
        c.barrier()
        @c.block.sync
        def _(sync):
            sync.dma_start(out=xpst[:, :, :],
                           in_=r.xpT_s.ap().rearrange("(k p) r -> p k r", p=P)
                           ).then_inc(c.ld.h, 16)
            c.ld.n += 16
        @c.block.vector
        def _(vector):
            wait_all(vector, c.ld)
            c.v.inc(vector.tensor_copy(out=xpsb[:, :, :], in_=xpst[:, :, :]))
        c.barrier()
        for tj in range(T // TT):
            t0 = tj * TT
            for dc in range(8):
                qs = c.q[dc % 2]
                @c.block.sync
                def _(sync, t0=t0, dc=dc, qs=qs):
                    wait_all(sync, c.mm)
                    sync.dma_start(out=xsl[:, dc % 2, :],
                                   in_=r.xsb_hbm[dc * P:(dc + 1) * P, t0:t0 + TT]
                                   ).then_inc(qs.h, 16)
                    qs.n += 16
                @c.block.tensor
                def _(tensor, dc=dc, qs=qs):
                    tensor.wait_ge(qs.h, qs.n)
                    if dc == 0:
                        wait_all(tensor, c.v)
                    tensor.matmul(out=r.psM[:, :], lhsT=xpsb[:, dc, 0:P],
                                  rhs=xsl[:, dc % 2, :],
                                  start=(dc == 0), stop=(dc == 7))
                    c.mm.inc(tensor.matmul(out=r.psR[0:32, :],
                                           lhsT=xpsb[:, dc, P:160],
                                           rhs=xsl[:, dc % 2, :],
                                           start=(dc == 0), stop=(dc == 7)))
            sl = tj % 2
            @c.block.vector
            def _(vector, sl=sl):
                wait_all(vector, c.mm, c.st)
                vector.tensor_copy(out=spdt[:, sl, :], in_=r.psM[:, :])
                c.v.inc(vector.tensor_copy(out=spbc[:, sl, :], in_=r.psR[0:32, :]))
            @c.block.gpsimd
            def _(gpsimd, t0=t0, sl=sl):
                wait_all(gpsimd, c.v)
                gpsimd.dma_start(out=r.ssm_part[0:P, t0:t0 + TT],
                                 in_=spdt[:, sl, :]).then_inc(c.st.h, 16)
                gpsimd.dma_start(out=r.ssm_part[P:160, t0:t0 + TT],
                                 in_=spbc[:, sl, :]).then_inc(c.st.h, 16)
                c.st.n += 32


def _f_dtproj(c, r):
    nc = c.nc
    import contextlib
    with contextlib.ExitStack() as es:
        dtwst = es.enter_context(nc.sbuf_tensor("f_dtwst", [P, 8, P], F32))
        dtwsb = es.enter_context(nc.sbuf_tensor("f_dtwsb", [P, 8, P], BF16))
        dtbt = es.enter_context(nc.sbuf_tensor("f_dtbt", [P, 8], F32))
        dtif = es.enter_context(nc.sbuf_tensor("f_dtif", [P, 2, TT], F32))
        dtin = es.enter_context(nc.sbuf_tensor("f_dtin", [P, 2, TT], BF16))
        dpre = es.enter_context(nc.sbuf_tensor("f_dpre", [P, 2, TT], F32))
        dex = es.enter_context(nc.sbuf_tensor("f_dex", [P, TT], F32))
        dtt = es.enter_context(nc.sbuf_tensor("f_dtt", [P, 2, TT], F32))
        c.barrier()
        @c.block.sync
        def _(sync):
            sync.dma_start(out=dtwst[:, :, :],
                           in_=r.dtwT_s.ap().rearrange("p (k m) -> p k m", m=P)
                           ).then_inc(c.ld.h, 16)
            sync.dma_start(out=dtbt[:, :], in_=r.dtb_s[:, :]).then_inc(c.ld.h, 16)
            c.ld.n += 32
        @c.block.vector
        def _(vector):
            wait_all(vector, c.ld)
            c.v.inc(vector.tensor_copy(out=dtwsb[:, :, :], in_=dtwst[:, :, :]))
        c.barrier()
        for tj in range(T // TT):
            t0 = tj * TT
            sl = tj % 2
            qs = c.q[2 + sl]
            @c.block.sync
            def _(sync, t0=t0, sl=sl, qs=qs):
                wait_all(sync, c.v)
                sync.dma_start(out=dtif[:, sl, :],
                               in_=r.ssm_red[0:P, t0:t0 + TT]).then_inc(qs.h, 16)
                qs.n += 16
            @c.block.vector
            def _(vector, sl=sl, qs=qs):
                vector.wait_ge(qs.h, qs.n)
                wait_all(vector, c.mm)
                c.v.inc(vector.tensor_copy(out=dtin[:, sl, :], in_=dtif[:, sl, :]))
            for oc in range(8):
                pp = r.ps0 if oc % 2 == 0 else r.ps1
                @c.block.tensor
                def _(tensor, oc=oc, sl=sl, pp=pp):
                    wait_all(tensor, c.v)
                    if oc < 2:
                        wait_all(tensor, c.s)
                    c.mm.inc(tensor.matmul(out=pp[:, :],
                                           lhsT=dtwsb[:, oc, :],
                                           rhs=dtin[:, sl, :], start=True, stop=True))
                @c.block.vector
                def _(vector, oc=oc, pp=pp):
                    wait_all(vector, c.mm, c.s)
                    c.v.inc(vector.tensor_scalar(
                        out=dpre[:, oc % 2, :], in0=pp[:, :],
                        scalar1=dtbt[:, oc:oc + 1], scalar2=SP_CLAMP,
                        op0=AL.add, op1=AL.min))
                @c.block.scalar
                def _(scalar, oc=oc):
                    wait_all(scalar, c.v, c.st)
                    scalar.activation(out=dex[:, :], in_=dpre[:, oc % 2, :],
                                      func=ACT.Exp)
                    c.s.inc(scalar.activation(out=dtt[:, oc % 2, :], in_=dex[:, :],
                                              func=ACT.Ln, bias=r.ones_col[:, 0:1]))
                @c.block.gpsimd
                def _(gpsimd, t0=t0, oc=oc):
                    wait_all(gpsimd, c.s)
                    gpsimd.dma_start(out=r.dt_hbm[oc * P:(oc + 1) * P, t0:t0 + TT],
                                     in_=dtt[:, oc % 2, :]).then_inc(c.st.h, 16)
                    c.st.n += 16


def build_fused():
    nc = bass.Bass("TRN2", target_bir_lowering=False)
    r = _R()
    r.x_pad = nc.dram_tensor("x_pad", [DM, XPAD], F32, kind="ExternalInput")
    r.x_q = nc.dram_tensor("x_q", [DM, TPC], F32, kind="ExternalInput")
    r.qoff = nc.dram_tensor("qoff", [1, 2], mybir.dt.int32, kind="ExternalInput")
    r.wxT_s = nc.dram_tensor("wxT_s", [DM, 1024], F32, kind="ExternalInput")
    r.wgT_s = nc.dram_tensor("wgT_s", [DM, 1024], F32, kind="ExternalInput")
    r.convw_s = nc.dram_tensor("convw_s", [P, 8, 4], F32, kind="ExternalInput")
    r.convb_s = nc.dram_tensor("convb_s", [P, 8], F32, kind="ExternalInput")
    r.xpT_s = nc.dram_tensor("xpT_s", [1024, 160], F32, kind="ExternalInput")
    r.dtwT_s = nc.dram_tensor("dtwT_s", [P, 1024], F32, kind="ExternalInput")
    r.dtb_s = nc.dram_tensor("dtb_s", [P, 8], F32, kind="ExternalInput")
    r.alog = nc.dram_tensor("alog", [P, 8, 16], F32, kind="ExternalInput")
    r.dvec = nc.dram_tensor("dvec", [P, 8], F32, kind="ExternalInput")
    r.woT = nc.dram_tensor("woT", [NI, DM], F32, kind="ExternalInput")
    r.wupT = nc.dram_tensor("wupT", [DM, 8192], F32, kind="ExternalInput")
    r.wgT2 = nc.dram_tensor("wgT2", [DM, 8192], F32, kind="ExternalInput")
    r.wdT = nc.dram_tensor("wdT", [8192, DM], F32, kind="ExternalInput")
    r.suu = nc.dram_tensor("suu", [P, 16], F32, kind="ExternalInput")
    r.sud = nc.dram_tensor("sud", [P, 64], F32, kind="ExternalInput")
    r.wsu = nc.dram_tensor("wsu", [P, 64], F32, kind="ExternalInput")
    r.wsg = nc.dram_tensor("wsg", [P, 64], F32, kind="ExternalInput")
    r.wsd = nc.dram_tensor("wsd", [P, 16], F32, kind="ExternalInput")
    r.h128 = nc.dram_tensor("h128", [P, P], F32, kind="ExternalInput")
    r.out_o = nc.dram_tensor("out_o", [DM, TPC], F32, kind="ExternalOutput")
    r.xs_hbm = nc.dram_tensor("xs_hbm", [1024, T], F32)
    r.xsb_hbm = nc.dram_tensor("xsb_hbm", [1024, T], BF16)
    r.sg_hbm = nc.dram_tensor("sg_hbm", [1024, T], F32)
    r.ssm_part = nc.dram_tensor("ssm_part", [160, T], F32)
    r.ssm_red = nc.dram_tensor("ssm_red", [160, T], F32)
    r.dt_hbm = nc.dram_tensor("dt_hbm", [1024, T], F32)
    r.ym_own = nc.dram_tensor("ym_own", [1024, T], BF16)
    r.ym_all = nc.dram_tensor("ym_all", [NI, T], BF16)
    r.ym_q = nc.dram_tensor("ym_q", [NI, TPC], BF16)
    r.bc16 = nc.dram_tensor("bc16", [16, T], BF16)
    r.wup2 = nc.dram_tensor("wup2", [DM, 8192], BF16)
    r.wg2 = nc.dram_tensor("wg2", [DM, 8192], BF16)
    r.wd2 = nc.dram_tensor("wd2", [8192, DM], BF16)
    r.wo2 = nc.dram_tensor("wo2", [NI, DM], BF16)
    if DBG:
        r.xs_d = nc.dram_tensor("xs_d", [1024, T], F32, kind="ExternalOutput")
        r.sg_d = nc.dram_tensor("sg_d", [1024, T], F32, kind="ExternalOutput")
        r.ssm_d = nc.dram_tensor("ssm_d", [160, T], F32, kind="ExternalOutput")
        r.dt_d = nc.dram_tensor("dt_d", [1024, T], F32, kind="ExternalOutput")
        r.ym_d = nc.dram_tensor("ym_d", [1024, T], BF16, kind="ExternalOutput")
        r.at_d = nc.dram_tensor("at_d", [P, 128], F32, kind="ExternalOutput")
        r.da_d = nc.dram_tensor("da_d", [P, 2048], F32, kind="ExternalOutput")
        r.vt_d = nc.dram_tensor("vt_d", [P, 5 * 1024], F32, kind="ExternalOutput")
        r.ymq_d = nc.dram_tensor("ymq_d", [NI, TPC], BF16, kind="ExternalOutput")

    import contextlib
    _es = contextlib.ExitStack()
    r.ones_col = _es.enter_context(nc.sbuf_tensor("ones_col", [P, 1], F32))
    r.ones_row = _es.enter_context(nc.sbuf_tensor("ones_row", [1, P], F32))
    r.consts = _es.enter_context(nc.sbuf_tensor("consts", [1, 4], F32))
    r.h128f = _es.enter_context(nc.sbuf_tensor("h128f", [P, P], F32))
    r.h128b = _es.enter_context(nc.sbuf_tensor("h128b", [P, P], BF16))
    r.suu2 = _es.enter_context(nc.sbuf_tensor("suu2", [P, 16], F32))
    r.sud2 = _es.enter_context(nc.sbuf_tensor("sud2", [P, 64], F32))
    r.wsu2 = _es.enter_context(nc.sbuf_tensor("wsu2", [P, 64], F32))
    r.wsg2 = _es.enter_context(nc.sbuf_tensor("wsg2", [P, 64], F32))
    r.wsd2 = _es.enter_context(nc.sbuf_tensor("wsd2", [P, 16], F32))
    r.cwt = _es.enter_context(nc.sbuf_tensor("cwt", [P, 8, 4], F32))
    r.cbt = _es.enter_context(nc.sbuf_tensor("cbt", [P, 8], F32))
    r.sq = _es.enter_context(nc.sbuf_tensor("sq", [P, 2, TS], F32))
    r.stat = _es.enter_context(nc.sbuf_tensor("stat", [1, 6, TS], F32))
    r.psA = _es.enter_context(nc.psum_tensor("psA", [1, TS], F32))
    r.psB = _es.enter_context(nc.psum_tensor("psB", [1, TS], F32))
    r.psM = _es.enter_context(nc.psum_tensor("psM", [P, TS], F32))
    r.psR = _es.enter_context(nc.psum_tensor("psR", [P, TS], F32))
    r.ps0 = _es.enter_context(nc.psum_tensor("ps0", [P, TS], F32))
    r.ps1 = _es.enter_context(nc.psum_tensor("ps1", [P, TS], F32))
    r.ps2 = _es.enter_context(nc.psum_tensor("ps2", [P, TS], F32))
    r.ps3 = _es.enter_context(nc.psum_tensor("ps3", [P, TS], F32))
    _s_ld = _es.enter_context(nc.semaphore("s_ld"))
    _s_st = _es.enter_context(nc.semaphore("s_st"))
    _s_mm = _es.enter_context(nc.semaphore("s_mm"))
    _s_v = _es.enter_context(nc.semaphore("s_v"))
    _s_s = _es.enter_context(nc.semaphore("s_s"))
    _qs = [_es.enter_context(nc.semaphore(f"s_q{i}")) for i in range(4)]
    _cc = _es.enter_context(nc.semaphore("s_cc"))
    block = _es.enter_context(nc.Block())
    if True:
        c = Ctx(nc, block, [Sem(_s_ld), Sem(_s_st), Sem(_s_mm), Sem(_s_v), Sem(_s_s)],
                [Sem(q) for q in _qs])
        cc = Sem(_cc)

        @block.sync
        def _(sync):
            sync.dma_start(out=r.h128f[:, :], in_=r.h128[:, :]).then_inc(c.ld.h, 16)
            sync.dma_start(out=r.suu2[:, :], in_=r.suu[:, :]).then_inc(c.ld.h, 16)
            sync.dma_start(out=r.sud2[:, :], in_=r.sud[:, :]).then_inc(c.ld.h, 16)
            sync.dma_start(out=r.wsu2[:, :], in_=r.wsu[:, :]).then_inc(c.ld.h, 16)
            sync.dma_start(out=r.wsg2[:, :], in_=r.wsg[:, :]).then_inc(c.ld.h, 16)
            sync.dma_start(out=r.wsd2[:, :], in_=r.wsd[:, :]).then_inc(c.ld.h, 16)
            sync.dma_start(out=r.cwt[:, :, :], in_=r.convw_s[:, :, :]).then_inc(c.ld.h, 16)
            sync.dma_start(out=r.cbt[:, :], in_=r.convb_s[:, :]).then_inc(c.ld.h, 16)
            c.ld.n += 8 * 16

        @block.vector
        def _(vector):
            vector.memset(r.ones_col[:, :], 1.0)
            vector.memset(r.ones_row[:, :], 1.0)
            vector.memset(r.consts[:, 0:1], LN_EPS)
            vector.memset(r.consts[:, 1:2], -0.5)
            wait_all(vector, c.ld)
            vector.tensor_copy(out=r.h128b[:, :], in_=r.h128f[:, :])
            vector.tensor_scalar_mul(out=r.suu2[:, :], in0=r.suu2[:, :],
                                     scalar1=1.0 / (64.0 * float(np.sqrt(2048.0))))
            vector.tensor_scalar_mul(out=r.sud2[:, :], in0=r.sud2[:, :],
                                     scalar1=1.0 / (64.0 * float(np.sqrt(8192.0))))
            vector.tensor_scalar_mul(out=r.wsu2[:, :], in0=r.wsu2[:, :], scalar1=64.0)
            vector.tensor_scalar_mul(out=r.wsg2[:, :], in0=r.wsg2[:, :], scalar1=64.0)
            c.v.inc(vector.tensor_scalar_mul(out=r.wsd2[:, :], in0=r.wsd2[:, :],
                                             scalar1=64.0))
            vector.wait_ge(c.v.h, c.v.n)

        _l3_transform(c, r)
        c.barrier()

        with nc.sbuf_tensor("wres", [P, 16, 16, P], BF16) as wres:
            r.wres = wres
            with nc.sbuf_tensor("wstage", [P, 16, P], F32) as wstage:
                for half, wsrc in ((0, r.wxT_s), (1, r.wgT_s)):
                    for mc in range(8):
                        @c.block.sync
                        def _(sync, wsrc=wsrc, mc=mc):
                            wait_all(sync, c.v)
                            sync.dma_start(
                                out=wstage[:, :, :],
                                in_=wsrc[:, mc * P:(mc + 1) * P]
                                    .rearrange("(k p) m -> p k m", p=P)
                                ).then_inc(c.ld.h, 16)
                            c.ld.n += 16
                        @c.block.vector
                        def _(vector, half=half, mc=mc):
                            wait_all(vector, c.ld)
                            c.v.inc(vector.tensor_copy(
                                out=wres[:, :, half * 8 + mc, :], in_=wstage[:, :, :]))
            with (
                nc.sbuf_tensor("s1_xnb", [P, 16, HCOLS], BF16) as xnb,
                nc.sbuf_tensor("s1_xstage", [P, 16, TT], F32) as xstage,
                nc.sbuf_tensor("s1_xbuf", [P, 3 + TT], F32) as xbuf,
                nc.sbuf_tensor("s1_cs", [P, TT], F32) as cs,
                nc.sbuf_tensor("s1_xst", [P, TT], F32) as xst,
                nc.sbuf_tensor("s1_xst16", [P, TT], BF16) as xst16,
                nc.sbuf_tensor("s1_sgt", [P, 2, TT], F32) as sgt,
            ):
                for k in range(T // HT):
                    _f_ln_chunk(c, r, k, xnb, xstage, r.sq, r.stat)
                    _f_inproj_chunk(c, r, k, xnb, (xbuf, cs, xst, xst16, sgt))

        _f_xproj(c, r)

        @block.gpsimd
        def _(gpsimd):
            gpsimd.wait_ge(c.st.h, c.st.n)
            wait_all(gpsimd, c.v, c.s, c.mm)
            gpsimd.collective_compute(
                "AllReduce", AL.add,
                replica_groups=[[0, 1], [2, 3], [4, 5], [6, 7]],
                ins=[r.ssm_part.ap().opt()], outs=[r.ssm_red.ap().opt()],
            ).then_inc(cc.h, 1)
            cc.n += 1
        for eng in ("sync", "tensor", "vector", "scalar", "gpsimd"):
            @getattr(block, eng)
            def _(e):
                e.wait_ge(cc.h, cc.n)

        _f_dtproj(c, r)

        with (
            nc.sbuf_tensor("cstg", [16, 4096], F32) as cstg,
            nc.sbuf_tensor("cstb", [16, 4096], BF16) as cstb,
        ):
            for j in range(4):
                @block.sync
                def _(sync, j=j):
                    wait_all(sync, c.v, c.st)
                    sync.dma_start(out=cstg[:, :],
                                   in_=r.ssm_red[144:160, j * 4096:(j + 1) * 4096]
                                   ).then_inc(c.ld.h, 16)
                    c.ld.n += 16
                @block.vector
                def _(vector, j=j):
                    wait_all(vector, c.ld)
                    c.v.inc(vector.tensor_copy(out=cstb[:, :], in_=cstg[:, :]))
                @block.gpsimd
                def _(gpsimd, j=j):
                    wait_all(gpsimd, c.v)
                    gpsimd.dma_start(out=r.bc16[:, j * 4096:(j + 1) * 4096],
                                     in_=cstb[:, :]).then_inc(c.st.h, 16)
                    c.st.n += 16
        with contextlib.ExitStack() as es5:
            r.at = es5.enter_context(nc.sbuf_tensor("l2_at", [P, 8, 16], F32))
            r.dvt = es5.enter_context(nc.sbuf_tensor("l2_dvt", [P, 8], F32))
            r.carry = es5.enter_context(nc.sbuf_tensor("l2_carry", [P, P], F32))
            r.dtt = es5.enter_context(nc.sbuf_tensor("l2_dtt", [P, 8, TW], F32))
            r.ut = es5.enter_context(nc.sbuf_tensor("l2_ut", [P, 8, TW], F32))
            r.yt = es5.enter_context(nc.sbuf_tensor("l2_yt", [P, 8, TW], F32))
            r.xstg = es5.enter_context(nc.sbuf_tensor("l2_xstg", [P, 2, TW], F32))
            r.sgs = es5.enter_context(nc.sbuf_tensor("l2_sgs", [P, 2, TW], F32))
            r.bb = es5.enter_context(nc.sbuf_tensor("l2_bb", [P, 2, TW], F32))
            r.cb = es5.enter_context(nc.sbuf_tensor("l2_cb", [P, 2, TW], F32))
            r.dat = es5.enter_context(nc.sbuf_tensor("l2_dat", [P, 4, TW], F32))
            r.vt = es5.enter_context(nc.sbuf_tensor("l2_vt", [P, 4, TW], F32))
            r.ht = es5.enter_context(nc.sbuf_tensor("l2_ht", [P, TW], F32))
            r.pt = es5.enter_context(nc.sbuf_tensor("l2_pt", [P, 2, TW], F32))
            r.ymt = es5.enter_context(nc.sbuf_tensor("l2_ymt", [P, 2, TW], BF16))
            c.barrier()
            r.dt_i, r.xs_i, r.sg_i = r.dt_hbm, r.xs_hbm, r.sg_hbm
            r.bc_i, r.bc_row0 = r.ssm_red, 128
            r.ym_o = r.ym_own
            _l2_prologue(c, r)
            _l2_body(c, r)

        @block.gpsimd
        def _(gpsimd):
            gpsimd.wait_ge(c.st.h, c.st.n)
            wait_all(gpsimd, c.v, c.s, c.mm)
            gpsimd.collective_compute(
                "AllGather", AL.bypass,
                replica_groups=[[0, 1, 4, 5], [2, 3, 6, 7]],
                ins=[r.ym_own.ap().opt()], outs=[r.ym_all.ap().opt()],
            ).then_inc(cc.h, 1)
            cc.n += 1
        for eng in ("sync", "tensor", "vector", "scalar", "gpsimd"):
            @getattr(block, eng)
            def _(e):
                e.wait_ge(cc.h, cc.n)

        with (
            nc.sbuf_tensor("qofft", [1, 2], mybir.dt.int32) as qofft,
            nc.gpsimd.register("rq") as rq,
        ):
            @block.sync
            def _(sync):
                sync.dma_start(out=qofft[:1, :], in_=r.qoff[:1, :]).then_inc(c.ld.h, 16)
                c.ld.n += 16
            @block.gpsimd
            def _(gpsimd):
                wait_all(gpsimd, c.ld)
                gpsimd.reg_load(rq, qofft[:1, :1])
                gpsimd.dma_start(out=r.ym_q.ap(),
                                 in_=AP(r.ym_all, rq, [[T, NI], [1, TPC]])
                                 ).then_inc(c.st.h, 16)
                c.st.n += 16
                gpsimd.wait_ge(c.st.h, c.st.n)
        if DBG:
            @block.scalar
            def _(scalar):
                wait_all(scalar, c.st, c.v, c.mm, c.s)
                scalar.dma_start(out=r.xs_d.ap(), in_=r.xs_hbm.ap()).then_inc(c.ld.h, 16)
                scalar.dma_start(out=r.sg_d.ap(), in_=r.sg_hbm.ap()).then_inc(c.ld.h, 16)
                scalar.dma_start(out=r.ssm_d.ap(), in_=r.ssm_red.ap()).then_inc(c.ld.h, 16)
                scalar.dma_start(out=r.dt_d.ap(), in_=r.dt_hbm.ap()).then_inc(c.ld.h, 16)
                scalar.dma_start(out=r.ym_d.ap(), in_=r.ym_own.ap()).then_inc(c.ld.h, 16)
                scalar.dma_start(out=r.ymq_d.ap(), in_=r.ym_q.ap()).then_inc(c.ld.h, 16)
                c.ld.n += 96
                scalar.wait_ge(c.ld.h, c.ld.n)
        c.barrier()
        r.ym_i = r.ym_q
        r.ym_reg = None
        r.x_i = r.x_q
        r.x_off = 0
        _l3_main(c, r)

        @block.gpsimd
        def _(gpsimd):
            gpsimd.wait_ge(c.st.h, c.st.n)
    _es.close()
    return nc


def _fused_inputs(inputs, xT):
    w_in = np.asarray(inputs["L0_attn_in_proj_w"], np.float32)
    wxT = np.ascontiguousarray(w_in[0::2].T)
    wgT = np.ascontiguousarray(w_in[1::2].T)
    cw = np.asarray(inputs["L0_attn_conv_w"], np.float32)
    cb = np.asarray(inputs["L0_attn_conv_b"], np.float32)
    xpT = np.asarray(inputs["L0_attn_x_proj_weight"], np.float32).transpose(0, 2, 1)
    dtwT = np.asarray(inputs["L0_attn_dt_proj_weight"], np.float32).transpose(0, 2, 1)
    dtb = np.asarray(inputs["L0_attn_dt_proj_bias"], np.float32)
    alog_full = np.asarray(inputs["L0_attn_A_log"], np.float32)
    d_full = np.asarray(inputs["L0_attn_D"], np.float32)
    woT = np.ascontiguousarray(np.asarray(inputs["L0_attn_out_proj_w"], np.float32).T)
    wupT = np.ascontiguousarray(np.asarray(inputs["L0_ffn_W_up"], np.float32).T)
    wgT2 = np.ascontiguousarray(np.asarray(inputs["L0_ffn_W_gate"], np.float32).T)
    wdT = np.ascontiguousarray(np.asarray(inputs["L0_ffn_W_down"], np.float32).T)
    suu = np.ascontiguousarray(
        np.asarray(inputs["L0_ffn_SU_ug"], np.float32).reshape(16, P).T)
    sud = np.ascontiguousarray(
        np.asarray(inputs["L0_ffn_SU_dp"], np.float32).reshape(64, P).T)
    wsall = np.asarray(inputs["L0_ffn_Wscale_ug"], np.float32)
    wsu = np.ascontiguousarray(wsall[:8192].reshape(64, P).T)
    wsg = np.ascontiguousarray(wsall[8192:].reshape(64, P).T)
    wsd = np.ascontiguousarray(
        np.asarray(inputs["L0_ffn_Wscale_dp"], np.float32).reshape(16, P).T)
    h128 = _hadamard(P)
    maps = []
    for cc in range(8):
        h = cc // 4
        b = (cc // 2) % 2
        dh = cc % 2
        q = 2 * h + dh
        s0 = q * TPC
        lo = h * HD + dh * 1024
        x_pad = np.zeros((DM, XPAD), np.float32)
        x_pad[:, 3:3 + T] = xT[b]
        maps.append({
            "x_pad": x_pad,
            "x_q": np.ascontiguousarray(xT[b, :, s0:s0 + TPC]),
            "qoff": np.array([[s0, 0]], np.int32),
            "wxT_s": np.ascontiguousarray(wxT[:, lo:lo + 1024]),
            "wgT_s": np.ascontiguousarray(wgT[:, lo:lo + 1024]),
            "convw_s": np.ascontiguousarray(
                cw[lo:lo + 1024].reshape(8, P, 4).transpose(1, 0, 2)),
            "convb_s": np.ascontiguousarray(cb[lo:lo + 1024].reshape(8, P).T),
            "xpT_s": np.ascontiguousarray(xpT[h, dh * 1024:dh * 1024 + 1024, :]),
            "dtwT_s": np.ascontiguousarray(dtwT[h][:, dh * 1024:dh * 1024 + 1024]),
            "dtb_s": np.ascontiguousarray(
                dtb[h, dh * 1024:dh * 1024 + 1024].reshape(8, P).T),
            "alog": np.ascontiguousarray(
                alog_full[h, dh * 1024:dh * 1024 + 1024].reshape(8, P, 16)
                .transpose(1, 0, 2)),
            "dvec": np.ascontiguousarray(
                d_full[h, dh * 1024:dh * 1024 + 1024].reshape(8, P).T),
            "woT": woT, "wupT": wupT, "wgT2": wgT2, "wdT": wdT,
            "suu": suu, "sud": sud, "wsu": wsu, "wsg": wsg, "wsd": wsd,
            "h128": h128,
        })
    return maps


def kernel(**inputs):
    xT = np.ascontiguousarray(
        np.asarray(inputs["L0_attn_hidden_states"], np.float32).transpose(0, 2, 1))
    nc = build_fused()
    maps = _fused_inputs(inputs, xT)
    res = run_bass_kernel_spmd(nc, maps, core_ids=list(range(8)))
    out = np.empty((NB, DM, T), np.float32)
    for cc in range(8):
        b = (cc // 2) % 2
        q = 2 * (cc // 4) + (cc % 2)
        s0 = q * TPC
        out[b, :, s0:s0 + TPC] = res.results[cc]["out_o"]
    return np.ascontiguousarray(out.transpose(0, 2, 1))


# version 09
# speedup vs baseline: 1.0634x; 1.0041x over previous
import numpy as np
import ml_dtypes
import concourse.bass as bass
import concourse.mybir as mybir
from concourse.bass import AP
from concourse.bass_utils import run_bass_kernel_spmd

F32 = mybir.dt.float32
BF16 = mybir.dt.bfloat16
AL = mybir.AluOpType
ACT = mybir.ActivationFunctionType
BF = ml_dtypes.bfloat16

P = 128
T = 16384
NB = 2
NH = 2
HD = 2048
NI = 4096
DM = 2048
NST = 16
RK = 128
TPC = 4096
HT = 2048
TT = 512
HCOLS = 2056
LN_EPS = 1e-5
DBG = False
PARANOID = 0
SP_CLAMP = 80.0

L1_TILES = [(j * TT, TT) for j in range(4)] + [(4 * TT, 8)]


class Sem:
    def __init__(self, handle):
        self.h = handle
        self.n = 0

    def inc(self, ins, k=1):
        ins.then_inc(self.h, k)
        self.n += k
        return ins


def wait_all(eng, *sems):
    for s in sems:
        if s.n > 0:
            eng.wait_ge(s.h, s.n)


class Ctx:
    def __init__(self, nc, block, sems, qsems=()):
        self.nc = nc
        self.block = block
        self.ld, self.st, self.mm, self.v, self.s = sems
        self.q = list(qsems)

    def barrier(self):
        c = self
        @c.block.sync
        def _(sync):
            wait_all(sync, c.st, c.mm, c.v, c.s)
        @c.block.tensor
        def _(tensor):
            wait_all(tensor, c.ld, c.st, c.v, c.s)
        @c.block.vector
        def _(vector):
            wait_all(vector, c.ld, c.st, c.mm, c.s)
        @c.block.scalar
        def _(scalar):
            wait_all(scalar, c.ld, c.st, c.mm, c.v)
        @c.block.gpsimd
        def _(gpsimd):
            wait_all(gpsimd, c.ld, c.mm, c.v, c.s)


def _store_win(j, w):
    c0 = j * TT
    lo = max(c0, 3)
    hi = min(c0 + w, 3 + HT)
    return (lo - c0, lo - 3, hi - lo)



def _l1_phase_a(c, r, half, xnb):
    sfx = str(half)
    """LN over feature dim -> xnb (bf16, feature-major)."""
    nc = c.nc
    with (
        nc.sbuf_tensor("xstage" + sfx, [P, 16, TT], F32) as xstage,
        nc.sbuf_tensor("sq" + sfx, [P, 2, TT], F32) as sq,
        nc.sbuf_tensor("stat" + sfx, [1, 6, TT], F32) as stat,
    ):
        c.barrier()
        @c.block.vector
        def _(vector):
            c.v.inc(vector.memset(xnb[:, :, 4 * TT + 3:], 0.0))
        for (c0, w) in L1_TILES:
            @c.block.sync
            def _(sync, c0=c0, w=w):
                wait_all(sync, c.mm, c.v)
                for fc in range(16):
                    sync.dma_start(out=xstage[:, fc, :w],
                                   in_=r.xin[fc * P:(fc + 1) * P, half, c0:c0 + w]
                                   ).then_inc(c.ld.h, 16)
                c.ld.n += 16 * 16
            @c.block.tensor
            def _(tensor, w=w):
                wait_all(tensor, c.ld, c.v)
                for fc in range(16):
                    tensor.matmul(out=r.psA[:, :w], lhsT=r.ones_col[:, :],
                                  rhs=xstage[:, fc, :w], start=(fc == 0), stop=(fc == 15))
            mm_base = c.mm.n
            s_base = c.s.n
            for fc in range(16):
                @c.block.scalar
                def _(scalar, w=w, fc=fc):
                    if fc == 0:
                        wait_all(scalar, c.ld, c.mm)
                    if fc >= 2:
                        scalar.wait_ge(c.mm.h, mm_base + fc - 1)
                    c.s.inc(scalar.activation(out=sq[:, fc % 2, :w],
                                              in_=xstage[:, fc, :w], func=ACT.Square))
                @c.block.tensor
                def _(tensor, w=w, fc=fc):
                    tensor.wait_ge(c.s.h, s_base + fc + 1)
                    c.mm.inc(tensor.matmul(out=r.psB[:, :w], lhsT=r.ones_col[:, :],
                                           rhs=sq[:, fc % 2, :w],
                                           start=(fc == 0), stop=(fc == 15)))
            @c.block.vector
            def _(vector, w=w):
                wait_all(vector, c.mm)
                vector.tensor_scalar_mul(out=stat[:, 0, :w], in0=r.psA[:, :w],
                                         scalar1=1.0 / DM)
                c.v.inc(vector.tensor_scalar_mul(out=stat[:, 1, :w], in0=r.psB[:, :w],
                                                 scalar1=1.0 / DM))
                vector.wait_ge(c.v.h, c.v.n)
                c.v.inc(vector.tensor_mul(out=stat[:, 2, :w], in0=stat[:, 0, :w],
                                          in1=stat[:, 0, :w]))
                vector.wait_ge(c.v.h, c.v.n)
                c.v.inc(vector.tensor_sub(out=stat[:, 3, :w], in0=stat[:, 1, :w],
                                          in1=stat[:, 2, :w]))
            @c.block.scalar
            def _(scalar, w=w):
                wait_all(scalar, c.v)
                scalar.activation(out=stat[:, 4, :w], in_=stat[:, 3, :w],
                                  func=ACT.Ln, bias=r.consts[:, 0:1])
                c.s.inc(scalar.activation(out=stat[:, 5, :w], in_=stat[:, 4, :w],
                                          func=ACT.Exp, scale=r.consts[:, 1:2]))
            @c.block.tensor
            def _(tensor, w=w):
                wait_all(tensor, c.s)
                tensor.matmul(out=r.psM[:, :w], lhsT=r.ones_row[:, :],
                              rhs=stat[:, 0, :w], start=True, stop=True)
                c.mm.inc(tensor.matmul(out=r.psR[:, :w], lhsT=r.ones_row[:, :],
                                       rhs=stat[:, 5, :w], start=True, stop=True))
            @c.block.vector
            def _(vector, c0=c0, w=w):
                wait_all(vector, c.mm)
                for fc in range(16):
                    vector.tensor_sub(out=xstage[:, fc, :w], in0=xstage[:, fc, :w],
                                      in1=r.psM[:, :w])
                for fc in range(16):
                    ins = vector.tensor_tensor(out=xnb[:, fc, c0:c0 + w],
                                               in0=xstage[:, fc, :w],
                                               in1=r.psR[:, :w], op=AL.mult)
                c.v.inc(ins)


def _l1_phase_b(c, r, half, xnb):
    nc = c.nc
    sfx = str(half)
    with (
        nc.sbuf_tensor("wst" + sfx, [P, 2, 16, P], F32) as wst,
        nc.sbuf_tensor("wsb" + sfx, [P, 2, 16, P], BF16) as wsb,
        nc.sbuf_tensor("xbuf" + sfx, [P, 3 + TT], F32) as xbuf,
        nc.sbuf_tensor("cs" + sfx, [P, TT], F32) as cs,
        nc.sbuf_tensor("xst" + sfx, [P, TT], F32) as xst,
        nc.sbuf_tensor("xst16" + sfx, [P, TT], BF16) as xst16,
        nc.sbuf_tensor("sgt" + sfx, [P, 2, TT], F32) as sgt,
        nc.sbuf_tensor("pjdbg" + sfx, [P, 2, 8], F32) as pjdbg,
        nc.psum_tensor("psH" + sfx, [P, 3], F32) as psH,
    ):
        c.barrier()
        for mc in range(64):
            is_x = mc < 32
            wT = r.wxT if is_x else r.wgT
            wc = mc if is_x else mc - 32
            h, dc = wc // 16, wc % 16
            ws = mc % 2
            @c.block.sync
            def _(sync, wT=wT, wc=wc, ws=ws):
                wait_all(sync, c.v)
                sync.dma_start(
                    out=wst[:, ws, :, :],
                    in_=wT[:, wc * P:(wc + 1) * P].rearrange("(k p) m -> p k m", p=P)
                ).then_inc(c.ld.h, 16)
                c.ld.n += 16
            @c.block.vector
            def _(vector, ws=ws):
                wait_all(vector, c.ld, c.mm)
                c.v.inc(vector.tensor_copy(out=wsb[:, ws, :, :], in_=wst[:, ws, :, :]))
            if is_x:
                @c.block.tensor
                def _(tensor, ws=ws):
                    wait_all(tensor, c.v)
                    for kc in range(16):
                        ins = tensor.matmul(out=psH[:, :], lhsT=wsb[:, ws, kc, :],
                                            rhs=xnb[:, kc, 0:3],
                                            start=(kc == 0), stop=(kc == 15))
                    c.mm.inc(ins)
                @c.block.vector
                def _(vector):
                    wait_all(vector, c.mm)
                    c.v.inc(vector.tensor_copy(out=xbuf[:, 0:3], in_=psH[:, :]))
            for ti, (c0, w) in enumerate(L1_TILES):
                pp = r.psP[ti % 2]
                @c.block.tensor
                def _(tensor, ws=ws, c0=c0, w=w, pp=pp):
                    wait_all(tensor, c.v, c.s)
                    for kc in range(16):
                        ins = tensor.matmul(out=pp[:, :w], lhsT=wsb[:, ws, kc, :],
                                            rhs=xnb[:, kc, c0:c0 + w],
                                            start=(kc == 0), stop=(kc == 15))
                    c.mm.inc(ins)
                so, do, wd = _store_win(ti, w)
                do_g = do + half * HT
                if DBG and half == 0 and ti == 4 and mc in (0, 32):
                    @c.block.vector
                    def _(vector, pp=pp, mc=mc):
                        wait_all(vector, c.mm)
                        c.v.inc(vector.tensor_copy(out=pjdbg[:, mc // 32, :],
                                                   in_=pp[:, :8]))
                    @c.block.gpsimd
                    def _(gpsimd, mc=mc):
                        wait_all(gpsimd, c.v)
                        gpsimd.dma_start(out=r.pj_o[mc // 32, :, :],
                                         in_=pjdbg[:, mc // 32, :]).then_inc(c.st.h, 16)
                        c.st.n += 16
                if is_x:
                    @c.block.vector
                    def _(vector, w=w, pp=pp, ti=ti, mc=mc):
                        small = w < 256
                        def step(ins):
                            c.v.inc(ins)
                            if small:
                                vector.wait_ge(c.v.h, c.v.n)
                        wait_all(vector, c.mm, c.st)
                        if ti > 0:
                            step(vector.tensor_copy(out=xbuf[:, 0:3],
                                                    in_=xbuf[:, TT:TT + 3]))
                        step(vector.tensor_copy(out=xbuf[:, 3:3 + w], in_=pp[:, :w]))
                        step(vector.tensor_scalar_mul(out=cs[:, :w], in0=xbuf[:, 0:w],
                                                      scalar1=r.cwt[:, mc, 0:1]))
                        for k in (1, 2, 3):
                            step(vector.scalar_tensor_tensor(
                                out=cs[:, :w], in0=xbuf[:, k:k + w],
                                scalar=r.cwt[:, mc, k:k + 1], in1=cs[:, :w],
                                op0=AL.mult, op1=AL.add))
                    @c.block.scalar
                    def _(scalar, w=w, mc=mc):
                        wait_all(scalar, c.v)
                        c.s.inc(scalar.activation(out=xst[:, :w], in_=cs[:, :w],
                                                  func=ACT.Silu, bias=r.cbt[:, mc:mc + 1]))
                    @c.block.vector
                    def _(vector, w=w):
                        wait_all(vector, c.s)
                        c.v.inc(vector.tensor_copy(out=xst16[:, :w], in_=xst[:, :w]))
                    if wd > 0:
                        @c.block.gpsimd
                        def _(gpsimd, h=h, dc=dc, so=so, do_g=do_g, wd=wd):
                            wait_all(gpsimd, c.v)
                            gpsimd.dma_start(
                                out=r.xs_o[h, dc * P:(dc + 1) * P, do_g:do_g + wd],
                                in_=xst[:, so:so + wd]).then_inc(c.st.h, 16)
                            gpsimd.dma_start(
                                out=r.xsb[h, dc * P:(dc + 1) * P, do_g:do_g + wd],
                                in_=xst16[:, so:so + wd]).then_inc(c.st.h, 16)
                            c.st.n += 32
                else:
                    @c.block.scalar
                    def _(scalar, w=w, pp=pp, ti=ti):
                        wait_all(scalar, c.mm, c.st)
                        c.s.inc(scalar.activation(out=sgt[:, ti % 2, :w], in_=pp[:, :w],
                                                  func=ACT.Silu))
                    if wd > 0:
                        @c.block.gpsimd
                        def _(gpsimd, h=h, dc=dc, so=so, do_g=do_g, wd=wd, ti=ti):
                            wait_all(gpsimd, c.s)
                            gpsimd.dma_start(
                                out=r.sg_o[h, dc * P:(dc + 1) * P, do_g:do_g + wd],
                                in_=sgt[:, ti % 2, so:so + wd]).then_inc(c.st.h, 16)
                            c.st.n += 16


def _l1_phase_cd(c, r):
    nc = c.nc
    with (
        nc.sbuf_tensor("xpst", [P, NH, 16, 160], F32) as xpst,
        nc.sbuf_tensor("xpsb", [P, NH, 16, 160], BF16) as xpsb,
        nc.sbuf_tensor("dtwst", [P, NH, HD], F32) as dtwst,
        nc.sbuf_tensor("dtwsb", [P, NH, HD], BF16) as dtwsb,
        nc.sbuf_tensor("dtbt", [P, NH, 16], F32) as dtbt,
        nc.sbuf_tensor("xsl", [P, 2, TT], BF16) as xsl,
        nc.sbuf_tensor("dtin", [P, TT], BF16) as dtin,
        nc.sbuf_tensor("bct", [32, TT], F32) as bct,
        nc.sbuf_tensor("dpre", [P, 2, TT], F32) as dpre,
        nc.sbuf_tensor("dex", [P, TT], F32) as dex,
        nc.sbuf_tensor("dtt", [P, 2, TT], F32) as dtt,
    ):
        c.barrier()
        @c.block.sync
        def _(sync):
            sync.dma_start(out=xpst[:, :, :, :],
                           in_=r.xpT.ap().rearrange("h (k p) r -> p h k r", p=P)
                           ).then_inc(c.ld.h, 16)
            sync.dma_start(out=dtwst[:, :, :],
                           in_=r.dtwT.ap().rearrange("h p d -> p h d")).then_inc(c.ld.h, 16)
            sync.dma_start(out=dtbt[:, :, :],
                           in_=r.dtb.ap().rearrange("h p k -> p h k")).then_inc(c.ld.h, 16)
            c.ld.n += 48
        @c.block.vector
        def _(vector):
            wait_all(vector, c.ld)
            vector.tensor_copy(out=xpsb[:, :, :, :], in_=xpst[:, :, :, :])
            c.v.inc(vector.tensor_copy(out=dtwsb[:, :, :], in_=dtwst[:, :, :]))
        c.barrier()
        for h in range(NH):
            for tj in range(TPC // TT):
                t0 = tj * TT
                for dc in range(16):
                    qs = c.q[dc % 2]
                    @c.block.sync
                    def _(sync, h=h, t0=t0, dc=dc, qs=qs):
                        wait_all(sync, c.mm)
                        sync.dma_start(out=xsl[:, dc % 2, :],
                                       in_=r.xsb[h, dc * P:(dc + 1) * P, t0:t0 + TT]
                                       ).then_inc(qs.h, 16)
                        qs.n += 16
                    @c.block.tensor
                    def _(tensor, h=h, dc=dc, qs=qs):
                        tensor.wait_ge(qs.h, qs.n)
                        if dc == 0:
                            wait_all(tensor, c.v)
                        tensor.matmul(out=r.psM[:, :], lhsT=xpsb[:, h, dc, 0:P],
                                      rhs=xsl[:, dc % 2, :],
                                      start=(dc == 0), stop=(dc == 15))
                        c.mm.inc(tensor.matmul(out=r.psR[0:32, :],
                                               lhsT=xpsb[:, h, dc, P:160],
                                               rhs=xsl[:, dc % 2, :],
                                               start=(dc == 0), stop=(dc == 15)))
                @c.block.vector
                def _(vector):
                    wait_all(vector, c.mm, c.st)
                    vector.tensor_copy(out=dtin[:, :], in_=r.psM[:, :])
                    c.v.inc(vector.tensor_copy(out=bct[:, :], in_=r.psR[0:32, :]))
                @c.block.gpsimd
                def _(gpsimd, h=h, t0=t0):
                    wait_all(gpsimd, c.v)
                    gpsimd.dma_start(out=r.bc_o[h, :, t0:t0 + TT],
                                     in_=bct[:, :]).then_inc(c.st.h, 16)
                    c.st.n += 16
                for oc in range(16):
                    pp = r.psP[oc % 2]
                    @c.block.tensor
                    def _(tensor, h=h, oc=oc, pp=pp):
                        wait_all(tensor, c.v)
                        if oc < 2:
                            wait_all(tensor, c.s)
                        c.mm.inc(tensor.matmul(out=pp[:, :],
                                               lhsT=dtwsb[:, h, oc * P:(oc + 1) * P],
                                               rhs=dtin[:, :], start=True, stop=True))
                    @c.block.vector
                    def _(vector, h=h, oc=oc, pp=pp):
                        wait_all(vector, c.mm, c.s)
                        c.v.inc(vector.tensor_scalar(
                            out=dpre[:, oc % 2, :], in0=pp[:, :],
                            scalar1=dtbt[:, h, oc:oc + 1], scalar2=SP_CLAMP,
                            op0=AL.add, op1=AL.min))
                    @c.block.scalar
                    def _(scalar, oc=oc):
                        wait_all(scalar, c.v, c.st)
                        scalar.activation(out=dex[:, :], in_=dpre[:, oc % 2, :],
                                          func=ACT.Exp)
                        c.s.inc(scalar.activation(out=dtt[:, oc % 2, :], in_=dex[:, :],
                                                  func=ACT.Ln, bias=r.ones_col[:, 0:1]))
                    @c.block.gpsimd
                    def _(gpsimd, h=h, t0=t0, oc=oc):
                        wait_all(gpsimd, c.s)
                        gpsimd.dma_start(out=r.dt_o[h, oc * P:(oc + 1) * P, t0:t0 + TT],
                                         in_=dtt[:, oc % 2, :]).then_inc(c.st.h, 16)
                        c.st.n += 16
        @c.block.gpsimd
        def _(gpsimd):
            gpsimd.wait_ge(c.st.h, c.st.n)


class _R:
    pass


def build_l1():
    nc = bass.Bass("TRN2", target_bir_lowering=False)
    r = _R()
    r.xin = nc.dram_tensor("xin", [DM, 2, HCOLS], F32, kind="ExternalInput")
    r.wxT = nc.dram_tensor("wxT", [DM, NI], F32, kind="ExternalInput")
    r.wgT = nc.dram_tensor("wgT", [DM, NI], F32, kind="ExternalInput")
    r.convw = nc.dram_tensor("convw", [P, 32, 4], F32, kind="ExternalInput")
    r.convb = nc.dram_tensor("convb", [P, 32], F32, kind="ExternalInput")
    r.xpT = nc.dram_tensor("xpT", [NH, HD, 160], F32, kind="ExternalInput")
    r.dtwT = nc.dram_tensor("dtwT", [NH, RK, HD], F32, kind="ExternalInput")
    r.dtb = nc.dram_tensor("dtb", [NH, P, 16], F32, kind="ExternalInput")
    r.dt_o = nc.dram_tensor("dt_o", [NH, HD, TPC], F32, kind="ExternalOutput")
    r.xs_o = nc.dram_tensor("xs_o", [NH, HD, TPC], F32, kind="ExternalOutput")
    r.sg_o = nc.dram_tensor("sg_o", [NH, HD, TPC], F32, kind="ExternalOutput")
    r.bc_o = nc.dram_tensor("bc_o", [NH, 32, TPC], F32, kind="ExternalOutput")
    r.xsb = nc.dram_tensor("xsb", [NH, HD, TPC], BF16)
    if DBG:
        r.xnb_o = nc.dram_tensor("xnb_o", [2, DM, HCOLS], BF16, kind="ExternalOutput")
        r.pj_o = nc.dram_tensor("pj_o", [2, P, 8], F32, kind="ExternalOutput")

    import contextlib
    _es = contextlib.ExitStack()
    ones_col = _es.enter_context(nc.sbuf_tensor("ones_col", [P, 1], F32))
    ones_row = _es.enter_context(nc.sbuf_tensor("ones_row", [1, P], F32))
    consts = _es.enter_context(nc.sbuf_tensor("consts", [1, 4], F32))
    cwt = _es.enter_context(nc.sbuf_tensor("cwt", [P, 32, 4], F32))
    cbt = _es.enter_context(nc.sbuf_tensor("cbt", [P, 32], F32))
    psA = _es.enter_context(nc.psum_tensor("psA", [1, TT], F32))
    psB = _es.enter_context(nc.psum_tensor("psB", [1, TT], F32))
    psM = _es.enter_context(nc.psum_tensor("psM", [P, TT], F32))
    psR = _es.enter_context(nc.psum_tensor("psR", [P, TT], F32))
    psP0 = _es.enter_context(nc.psum_tensor("psP0", [P, TT], F32))
    psP1 = _es.enter_context(nc.psum_tensor("psP1", [P, TT], F32))
    _s_ld = _es.enter_context(nc.semaphore("s_ld"))
    _s_st = _es.enter_context(nc.semaphore("s_st"))
    _s_mm = _es.enter_context(nc.semaphore("s_mm"))
    _s_v = _es.enter_context(nc.semaphore("s_v"))
    _s_s = _es.enter_context(nc.semaphore("s_s"))
    _s_q0 = _es.enter_context(nc.semaphore("s_q0"))
    _s_q1 = _es.enter_context(nc.semaphore("s_q1"))
    _s_q2 = _es.enter_context(nc.semaphore("s_q2"))
    _s_q3 = _es.enter_context(nc.semaphore("s_q3"))
    block = _es.enter_context(nc.Block())
    if True:
        r.ones_col, r.ones_row, r.cwt, r.cbt = ones_col, ones_row, cwt, cbt
        r.consts = consts
        r.psA, r.psB, r.psM, r.psR = psA, psB, psM, psR
        r.psP = [psP0, psP1]
        c = Ctx(nc, block, [Sem(_s_ld), Sem(_s_st), Sem(_s_mm), Sem(_s_v), Sem(_s_s)],
                [Sem(_s_q0), Sem(_s_q1), Sem(_s_q2), Sem(_s_q3)])

        @block.sync
        def _(sync):
            sync.dma_start(out=cwt[:, :, :], in_=r.convw[:, :, :]).then_inc(c.ld.h, 16)
            sync.dma_start(out=cbt[:, :], in_=r.convb[:, :]).then_inc(c.ld.h, 16)
            c.ld.n += 32

        @block.vector
        def _(vector):
            vector.memset(ones_col[:, :], 1.0)
            vector.memset(ones_row[:, :], 1.0)
            vector.memset(consts[:, 0:1], LN_EPS)
            c.v.inc(vector.memset(consts[:, 1:2], -0.5))

        for half in range(2):
            with nc.sbuf_tensor(f"xnb{half}", [P, 16, HCOLS], BF16) as xnb:
                _l1_phase_a(c, r, half, xnb)
                if DBG:
                    c.barrier()
                    @block.gpsimd
                    def _(gpsimd, half=half, xnb=xnb):
                        for fc in range(16):
                            gpsimd.dma_start(
                                out=r.xnb_o[half, fc * P:(fc + 1) * P, :],
                                in_=xnb[:, fc, :]).then_inc(c.st.h, 16)
                        c.st.n += 16 * 16
                _l1_phase_b(c, r, half, xnb)
        _l1_phase_cd(c, r)

    _es.close()
    return nc



def _l1_inputs(inputs, xT):
    w_in = np.asarray(inputs["L0_attn_in_proj_w"], np.float32)
    wxT = np.ascontiguousarray(w_in[0::2].T)
    wgT = np.ascontiguousarray(w_in[1::2].T)
    cw = np.asarray(inputs["L0_attn_conv_w"], np.float32)
    convw = np.ascontiguousarray(cw.reshape(32, P, 4).transpose(1, 0, 2))
    cb = np.asarray(inputs["L0_attn_conv_b"], np.float32)
    convb = np.ascontiguousarray(cb.reshape(32, P).T)
    xpT = np.ascontiguousarray(
        np.asarray(inputs["L0_attn_x_proj_weight"], np.float32).transpose(0, 2, 1))
    dtwT = np.ascontiguousarray(
        np.asarray(inputs["L0_attn_dt_proj_weight"], np.float32).transpose(0, 2, 1))
    dtb = np.ascontiguousarray(
        np.asarray(inputs["L0_attn_dt_proj_bias"], np.float32)
        .reshape(NH, 16, P).transpose(0, 2, 1))
    maps = []
    for c in range(8):
        b = (c // 2) % 2
        q = 2 * (c // 4) + (c % 2)
        s0 = q * TPC
        xin = np.zeros((DM, 2, HCOLS), np.float32)
        for k in range(2):
            base = s0 + HT * k
            lo = max(base - 3, 0)
            seg = xT[b, :, lo:base + HT]
            xin[:, k, 3 - (base - lo):3 + HT] = seg
        maps.append({
            "xin": xin, "wxT": wxT, "wgT": wgT, "convw": convw, "convb": convb,
            "xpT": xpT, "dtwT": dtwT, "dtb": dtb,
        })
    return maps


def run_l1(inputs, xT):
    nc = build_l1()
    maps = _l1_inputs(inputs, xT)
    res = run_bass_kernel_spmd(nc, maps, core_ids=list(range(8)))
    dtg = np.empty((NH, NB, HD, T), np.float32)
    xsg = np.empty((NH, NB, HD, T), np.float32)
    sgg = np.empty((NH, NB, HD, T), np.float32)
    bcg = np.empty((NH, NB, 32, T), np.float32)
    for c in range(8):
        b = (c // 2) % 2
        q = 2 * (c // 4) + (c % 2)
        s0 = q * TPC
        rr = res.results[c]
        dtg[:, b, :, s0:s0 + TPC] = rr["dt_o"]
        xsg[:, b, :, s0:s0 + TPC] = rr["xs_o"]
        sgg[:, b, :, s0:s0 + TPC] = rr["sg_o"]
        bcg[:, b, :, s0:s0 + TPC] = rr["bc_o"]
    return dtg, xsg, sgg, bcg



TW = 1024


def _l2_prologue(c, r):
    at, dvt, carry = r.at, r.dvt, r.carry
    block = c.block

    @block.sync
    def _(sync):
        sync.dma_start(out=at[:, :, :], in_=r.alog[:, :, :]).then_inc(c.ld.h, 16)
        sync.dma_start(out=dvt[:, :], in_=r.dvec[:, :]).then_inc(c.ld.h, 16)
        c.ld.n += 32

    @block.scalar
    def _(scalar):
        wait_all(scalar, c.ld)
        c.s.inc(scalar.activation(out=at[:, :, :], in_=at[:, :, :], func=ACT.Exp))

    @block.vector
    def _(vector):
        vector.memset(carry[:, :], 0.0)
        wait_all(vector, c.s)
        c.v.inc(vector.tensor_scalar_mul(out=at[:, :, :], in0=at[:, :, :],
                                         scalar1=-1.0))
        vector.wait_ge(c.v.h, c.v.n)


def _l2_body(c, r):
    if DBG and getattr(r, "at_d", None) is not None:
        @c.block.scalar
        def _(scalar):
            wait_all(scalar, c.v, c.s)
            scalar.dma_start(out=r.at_d.ap(),
                             in_=r.at.ap().rearrange("p a b -> p (a b)")).then_inc(c.ld.h, 16)
            scalar.dma_start(out=r.da_d[:, TW:2 * TW],
                             in_=r.dt_i[0:P, 0:TW]).then_inc(c.ld.h, 16)
            c.ld.n += 32
            scalar.wait_ge(c.ld.h, c.ld.n)
    bc0 = getattr(r, "bc_row0", 0)
    cbsrc = getattr(r, "cb_src", None)
    at, dvt, carry = r.at, r.dvt, r.carry
    dtt, ut, yt = r.dtt, r.ut, r.yt
    xstg, sgs, bb, cb = r.xstg, r.sgs, r.bb, r.cb
    dat, vt, ht, pt, ymt = r.dat, r.vt, r.ht, r.pt, r.ymt
    s_it, s_g, s_bc, s_sg = c.q[0], c.q[1], c.q[2], c.q[3]
    carry_prev = [None]
    it_ctr = [0]
    for tj in range(T // TW):
        t0 = tj * TW
        for dc in range(8):
            @c.block.sync
            def _(sync, t0=t0, dc=dc):
                wait_all(sync, c.v)
                sync.dma_start(out=dtt[:, dc, :],
                               in_=r.dt_i[dc * P:(dc + 1) * P, t0:t0 + TW]
                               ).then_inc(c.ld.h, 16)
                sync.dma_start(out=xstg[:, dc % 2, :],
                               in_=r.xs_i[dc * P:(dc + 1) * P, t0:t0 + TW]
                               ).then_inc(c.ld.h, 16)
                c.ld.n += 32
            @c.block.vector
            def _(vector, dc=dc):
                wait_all(vector, c.ld)
                vector.wait_ge(s_g.h, s_g.n)
                vector.tensor_mul(out=ut[:, dc, :], in0=dtt[:, dc, :],
                                  in1=xstg[:, dc % 2, :])
                c.v.inc(vector.tensor_scalar_mul(out=yt[:, dc, :],
                                                 in0=xstg[:, dc % 2, :],
                                                 scalar1=dvt[:, dc:dc + 1]))
        uv = c.v.n
        for n in range(16):
            ns = n % 2
            @c.block.sync
            def _(sync, t0=t0, n=n, ns=ns):
                if s_g.n >= 8:
                    sync.wait_ge(s_g.h, s_g.n - 7)
                if s_it.n >= 8:
                    sync.wait_ge(s_it.h, s_it.n - 7)
                sync.dma_start(out=bb[:, ns, :],
                               in_=AP(r.bc_i, (bc0 + n) * T + t0, [[0, P], [1, TW]])
                               ).then_inc(s_bc.h, 16)
                sync.dma_start(out=cb[:, ns, :],
                               in_=AP(cbsrc if cbsrc is not None else r.bc_i,
                                      ((0 if cbsrc is not None else bc0 + 16) + n) * T
                                      + t0, [[0, P], [1, TW]])
                               ).then_inc(s_bc.h, 16)
                s_bc.n += 32
            for dcp in range(4):
                pair = (2 * dcp, 2 * dcp + 1)
                slots = []
                for dc in pair:
                    it = it_ctr[0]
                    it_ctr[0] += 1
                    slot = it % 4
                    slots.append(slot)
                    @c.block.scalar
                    def _(scalar, n=n, dc=dc, slot=slot, uv=uv):
                        scalar.wait_ge(c.v.h, uv)
                        if s_it.n >= 4:
                            scalar.wait_ge(s_it.h, s_it.n - 3)
                        c.s.inc(scalar.activation(out=dat[:, slot, :],
                                                  in_=dtt[:, dc, :], func=ACT.Exp,
                                                  scale=at[:, dc, n:n + 1]))
                    @c.block.gpsimd
                    def _(gpsimd, n=n, dc=dc, ns=ns, slot=slot, uv=uv):
                        gpsimd.wait_ge(c.v.h, uv)
                        gpsimd.wait_ge(s_bc.h, s_bc.n)
                        if s_it.n >= 4:
                            gpsimd.wait_ge(s_it.h, s_it.n - 3)
                        s_g.inc(gpsimd.tensor_tensor(out=vt[:, slot, :],
                                                     in0=ut[:, dc, :],
                                                     in1=bb[:, ns, :], op=AL.mult))
                    s_it.n += 1
                s_it.n -= 2
                @c.block.vector
                def _(vector, n=n, ns=ns, pair=pair, slots=tuple(slots)):
                    vector.wait_ge(s_g.h, s_g.n)
                    vector.wait_ge(c.s.h, c.s.n)
                    vector.wait_ge(s_bc.h, s_bc.n)
                    for dc, slot in zip(pair, slots):
                        cidx = dc * 16 + n
                        s_it.inc(vector.tensor_tensor_scan(
                            out=ht[:, :], data0=dat[:, slot, :],
                            data1=vt[:, slot, :],
                            initial=carry[:, cidx:cidx + 1],
                            op0=AL.mult, op1=AL.add))
                        vector.tensor_mul(out=pt[:, slot % 2, :], in0=ht[:, :],
                                          in1=cb[:, ns, :])
                        vector.tensor_copy(out=carry[:, cidx:cidx + 1],
                                           in_=ht[:, TW - 1:TW])
                        c.v.inc(vector.tensor_add(out=yt[:, dc, :],
                                                  in0=yt[:, dc, :],
                                                  in1=pt[:, slot % 2, :]))
        for dc in range(8):
            @c.block.sync
            def _(sync, t0=t0, dc=dc):
                wait_all(sync, c.v)
                sync.dma_start(out=sgs[:, dc % 2, :],
                               in_=r.sg_i[dc * P:(dc + 1) * P, t0:t0 + TW]
                               ).then_inc(s_sg.h, 16)
                s_sg.n += 16
            @c.block.vector
            def _(vector, dc=dc):
                vector.wait_ge(s_sg.h, s_sg.n)
                vector.wait_ge(s_g.h, s_g.n)
                wait_all(vector, c.st)
                c.v.inc(vector.tensor_mul(out=ymt[:, dc % 2, :], in0=yt[:, dc, :],
                                          in1=sgs[:, dc % 2, :]))
            @c.block.gpsimd
            def _(gpsimd, t0=t0, dc=dc):
                wait_all(gpsimd, c.v)
                gpsimd.dma_start(out=r.ym_o[dc * P:(dc + 1) * P, t0:t0 + TW],
                                 in_=ymt[:, dc % 2, :]).then_inc(c.st.h, 16)
                c.st.n += 16


def build_l2():
    nc = bass.Bass("TRN2", target_bir_lowering=False)
    r = _R()
    r.dt_i = nc.dram_tensor("dt_i", [8 * P, T], F32, kind="ExternalInput")
    r.xs_i = nc.dram_tensor("xs_i", [8 * P, T], F32, kind="ExternalInput")
    r.sg_i = nc.dram_tensor("sg_i", [8 * P, T], F32, kind="ExternalInput")
    r.bc_i = nc.dram_tensor("bc_i", [32, T], F32, kind="ExternalInput")
    r.alog = nc.dram_tensor("alog", [P, 8, 16], F32, kind="ExternalInput")
    r.dvec = nc.dram_tensor("dvec", [P, 8], F32, kind="ExternalInput")
    r.ym_o = nc.dram_tensor("ym_o", [8 * P, T], BF16, kind="ExternalOutput")

    import contextlib
    _es = contextlib.ExitStack()
    at = _es.enter_context(nc.sbuf_tensor("at", [P, 8, 16], F32))
    dvt = _es.enter_context(nc.sbuf_tensor("dvt", [P, 8], F32))
    carry = _es.enter_context(nc.sbuf_tensor("carry", [P, P], F32))
    dtt = _es.enter_context(nc.sbuf_tensor("dtt", [P, 8, TW], F32))
    ut = _es.enter_context(nc.sbuf_tensor("ut", [P, 8, TW], F32))
    yt = _es.enter_context(nc.sbuf_tensor("yt", [P, 8, TW], F32))
    xstg = _es.enter_context(nc.sbuf_tensor("xstg", [P, 2, TW], F32))
    sgs = _es.enter_context(nc.sbuf_tensor("sgs", [P, 2, TW], F32))
    bb = _es.enter_context(nc.sbuf_tensor("bb", [P, 2, TW], F32))
    cb = _es.enter_context(nc.sbuf_tensor("cb", [P, 2, TW], F32))
    dat = _es.enter_context(nc.sbuf_tensor("dat", [P, 2, TW], F32))
    vt = _es.enter_context(nc.sbuf_tensor("vt", [P, TW], F32))
    ht = _es.enter_context(nc.sbuf_tensor("ht", [P, TW], F32))
    pt = _es.enter_context(nc.sbuf_tensor("pt", [P, TW], F32))
    ymt = _es.enter_context(nc.sbuf_tensor("ymt", [P, 2, TW], BF16))
    _s_ld = _es.enter_context(nc.semaphore("s_ld"))
    _s_st = _es.enter_context(nc.semaphore("s_st"))
    _s_mm = _es.enter_context(nc.semaphore("s_mm"))
    _s_v = _es.enter_context(nc.semaphore("s_v"))
    _s_s = _es.enter_context(nc.semaphore("s_s"))
    _qs = [_es.enter_context(nc.semaphore(f"s_q{i}")) for i in range(4)]
    block = _es.enter_context(nc.Block())
    if True:
        c = Ctx(nc, block, [Sem(_s_ld), Sem(_s_st), Sem(_s_mm), Sem(_s_v), Sem(_s_s)],
                [Sem(q) for q in _qs])
        r.at, r.dvt, r.carry = at, dvt, carry
        r.dtt, r.ut, r.yt = dtt, ut, yt
        r.xstg, r.sgs, r.bb, r.cb = xstg, sgs, bb, cb
        r.dat, r.vt, r.ht, r.pt, r.ymt = dat, vt, ht, pt, ymt
        _l2_prologue(c, r)
        _l2_body(c, r)
        @block.gpsimd
        def _(gpsimd):
            gpsimd.wait_ge(c.st.h, c.st.n)
    _es.close()
    return nc


def _l2_inputs(inputs, dtg, xsg, sgg, bcg):
    alog_full = np.asarray(inputs["L0_attn_A_log"], np.float32)
    d_full = np.asarray(inputs["L0_attn_D"], np.float32)
    maps = []
    for c in range(8):
        h = c // 4
        b = (c // 2) % 2
        dh = c % 2
        sl = slice(dh * 1024, dh * 1024 + 1024)
        maps.append({
            "dt_i": np.ascontiguousarray(dtg[h, b, sl]),
            "xs_i": np.ascontiguousarray(xsg[h, b, sl]),
            "sg_i": np.ascontiguousarray(sgg[h, b, sl]),
            "bc_i": np.ascontiguousarray(bcg[h, b]),
            "alog": np.ascontiguousarray(
                alog_full[h, sl].reshape(8, P, 16).transpose(1, 0, 2)),
            "dvec": np.ascontiguousarray(d_full[h, sl].reshape(8, P).T),
        })
    return maps


def run_l2(inputs, dtg, xsg, sgg, bcg):
    nc = build_l2()
    maps = _l2_inputs(inputs, dtg, xsg, sgg, bcg)
    res = run_bass_kernel_spmd(nc, maps, core_ids=list(range(8)))
    ymg = np.empty((NB, NI, T), BF)
    for c in range(8):
        h = c // 4
        b = (c // 2) % 2
        dh = c % 2
        ymg[b, h * HD + dh * 1024: h * HD + dh * 1024 + 1024] = res.results[c]["ym_o"]
    return ymg



TS = 512


def _bfly(c, vector, tiles, tiles2, nchunks):
    import math
    nst = int(math.log2(nchunks))
    assert nst % 2 == 0
    bufs = (tiles, tiles2)
    h = 1
    s = 0
    while h < nchunks:
        src_t, dst_t = bufs[s % 2], bufs[(s + 1) % 2]
        for j in range(0, nchunks, 2 * h):
            for k in range(j, j + h):
                a, b = src_t(k), src_t(k + h)
                vector.tensor_add(out=dst_t(k), in0=a, in1=b)
                vector.tensor_sub(out=dst_t(k + h), in0=a, in1=b)
        h *= 2
        s += 1


def _l3_transform(c, r):
    nc = c.nc
    import contextlib
    with contextlib.ExitStack() as es:
        wstg = es.enter_context(nc.sbuf_tensor("t_wstg", [P, 16, TS], F32))
        ps0, ps1 = r.ps0, r.ps1
        c.barrier()

        es_ug = contextlib.ExitStack()
        wb = es_ug.enter_context(nc.sbuf_tensor("t_wb", [P, 16, TS], BF16))
        wb2 = es_ug.enter_context(nc.sbuf_tensor("t_wb2", [P, 16, TS], BF16))
        for wsrc, wdst, sucol in ((r.wupT, r.wup2, 0), (r.wgT2, r.wg2, 1)):
            for mc in range(16):
                @c.block.sync
                def _(sync, wsrc=wsrc, mc=mc):
                    wait_all(sync, c.v, c.st)
                    sync.dma_start(
                        out=wstg[:, :, :],
                        in_=wsrc[:, mc * TS:(mc + 1) * TS]
                            .rearrange("(k p) m -> p k m", p=P)).then_inc(c.ld.h, 16)
                    c.ld.n += 16
                @c.block.vector
                def _(vector):
                    wait_all(vector, c.ld, c.mm)
                    c.v.inc(vector.tensor_copy(out=wb[:, :, :], in_=wstg[:, :, :]))
                for fc in range(16):
                    pp = ps0 if fc % 2 == 0 else ps1
                    @c.block.tensor
                    def _(tensor, fc=fc, pp=pp):
                        wait_all(tensor, c.v)
                        c.mm.inc(tensor.matmul(out=pp[:, :], lhsT=r.h128b[:, :],
                                               rhs=wb[:, fc, :], start=True, stop=True))
                    @c.block.vector
                    def _(vector, fc=fc, pp=pp):
                        wait_all(vector, c.mm)
                        c.v.inc(vector.tensor_copy(out=wb[:, fc, :], in_=pp[:, :]))
                @c.block.vector
                def _(vector, sucol=sucol):
                    wait_all(vector, c.mm)
                    _bfly(c, vector, lambda i: wb[:, i, :], lambda i: wb2[:, i, :], 16)
                    for fc in range(16):
                        ins = vector.tensor_scalar_mul(out=wb[:, fc, :], in0=wb[:, fc, :],
                                                       scalar1=r.suu2[:, fc:fc + 1])
                    c.v.inc(ins)
                @c.block.gpsimd
                def _(gpsimd, wdst=wdst, mc=mc):
                    wait_all(gpsimd, c.v)
                    gpsimd.dma_start(
                        out=wdst[:, mc * TS:(mc + 1) * TS]
                            .rearrange("(k p) m -> p k m", p=P),
                        in_=wb[:, :, :]).then_inc(c.st.h, 16)
                    c.st.n += 16

        es_ug.close()
        for mc in range(4):
            with contextlib.ExitStack() as es2:
                wd64 = es2.enter_context(
                    nc.sbuf_tensor(f"t_wd64_{mc}", [P, 64, TS], BF16))
                wd64b = es2.enter_context(
                    nc.sbuf_tensor(f"t_wd64b_{mc}", [P, 64, TS], BF16))
                for qq in range(4):
                    @c.block.sync
                    def _(sync, mc=mc, qq=qq):
                        wait_all(sync, c.v, c.st)
                        sync.dma_start(
                            out=wstg[:, :, :],
                            in_=r.wdT[qq * 2048:(qq + 1) * 2048, mc * TS:(mc + 1) * TS]
                                .rearrange("(k p) m -> p k m", p=P)).then_inc(c.ld.h, 16)
                        c.ld.n += 16
                    @c.block.vector
                    def _(vector, qq=qq):
                        wait_all(vector, c.ld)
                        c.v.inc(vector.tensor_copy(out=wd64[:, qq * 16:(qq + 1) * 16, :],
                                                   in_=wstg[:, :, :]))
                for ic in range(64):
                    pp = ps0 if ic % 2 == 0 else ps1
                    @c.block.tensor
                    def _(tensor, ic=ic, pp=pp):
                        wait_all(tensor, c.v)
                        c.mm.inc(tensor.matmul(out=pp[:, :], lhsT=r.h128b[:, :],
                                               rhs=wd64[:, ic, :], start=True, stop=True))
                    @c.block.vector
                    def _(vector, ic=ic, pp=pp):
                        wait_all(vector, c.mm)
                        c.v.inc(vector.tensor_copy(out=wd64[:, ic, :], in_=pp[:, :]))
                @c.block.vector
                def _(vector):
                    wait_all(vector, c.mm)
                    _bfly(c, vector, lambda i: wd64[:, i, :], lambda i: wd64b[:, i, :], 64)
                    for ic in range(64):
                        ins = vector.tensor_scalar_mul(out=wd64[:, ic, :],
                                                       in0=wd64[:, ic, :],
                                                       scalar1=r.sud2[:, ic:ic + 1])
                    c.v.inc(ins)
                @c.block.gpsimd
                def _(gpsimd, mc=mc):
                    wait_all(gpsimd, c.v)
                    gpsimd.dma_start(
                        out=r.wd2[:, mc * TS:(mc + 1) * TS]
                            .rearrange("(k p) m -> p k m", p=P),
                        in_=wd64[:, :, :]).then_inc(c.st.h, 16)
                    c.st.n += 16

        wob = es.enter_context(nc.sbuf_tensor("t_wob", [P, 16, TS], BF16))
        for mc in range(4):
            for qq in range(2):
                @c.block.sync
                def _(sync, mc=mc, qq=qq):
                    wait_all(sync, c.v, c.st)
                    sync.dma_start(
                        out=wstg[:, :, :],
                        in_=r.woT[qq * 2048:(qq + 1) * 2048, mc * TS:(mc + 1) * TS]
                            .rearrange("(k p) m -> p k m", p=P)).then_inc(c.ld.h, 16)
                    c.ld.n += 16
                @c.block.vector
                def _(vector):
                    wait_all(vector, c.ld, c.st)
                    c.v.inc(vector.tensor_copy(out=wob[:, :, :], in_=wstg[:, :, :]))
                @c.block.gpsimd
                def _(gpsimd, mc=mc, qq=qq):
                    wait_all(gpsimd, c.v)
                    gpsimd.dma_start(
                        out=r.wo2[qq * 2048:(qq + 1) * 2048, mc * TS:(mc + 1) * TS]
                            .rearrange("(k p) m -> p k m", p=P),
                        in_=wob[:, :, :]).then_inc(c.st.h, 16)
                    c.st.n += 16


def _l3_ln(c, r, ts, xst4, z):
    t0 = ts * TS + getattr(r, "x_off", 0)
    mm_base = c.mm.n
    s_base = c.s.n
    for fc in range(16):
        qs = c.q[fc % 2]
        @c.block.sync
        def _(sync, fc=fc, t0=t0, qs=qs):
            if fc == 0:
                wait_all(sync, c.mm, c.v)
            if fc >= 2:
                sync.wait_ge(c.mm.h, mm_base + fc - 1)
            sync.dma_start(out=xst4[:, fc % 2, :],
                           in_=r.x_i[fc * P:(fc + 1) * P, t0:t0 + TS]
                           ).then_inc(qs.h, 16)
            qs.n += 16
        @c.block.scalar
        def _(scalar, fc=fc, qs=qs):
            scalar.wait_ge(qs.h, qs.n)
            if fc == 0:
                wait_all(scalar, c.mm, c.v)
            if fc >= 2:
                scalar.wait_ge(c.mm.h, mm_base + fc - 1)
            c.s.inc(scalar.activation(out=r.sq[:, fc % 2, :], in_=xst4[:, fc % 2, :],
                                      func=ACT.Square))
        @c.block.tensor
        def _(tensor, fc=fc, qs=qs):
            tensor.wait_ge(qs.h, qs.n)
            if fc == 0:
                wait_all(tensor, c.v)
            tensor.matmul(out=r.psA[:, :], lhsT=r.ones_col[:, :],
                          rhs=xst4[:, fc % 2, :], start=(fc == 0), stop=(fc == 15))
            tensor.wait_ge(c.s.h, s_base + fc + 1)
            c.mm.inc(tensor.matmul(out=r.psB[:, :], lhsT=r.ones_col[:, :],
                                   rhs=r.sq[:, fc % 2, :],
                                   start=(fc == 0), stop=(fc == 15)))
    @c.block.vector
    def _(vector):
        wait_all(vector, c.mm)
        vector.tensor_scalar_mul(out=r.stat[:, 0, :], in0=r.psA[:, :], scalar1=1.0 / DM)
        c.v.inc(vector.tensor_scalar_mul(out=r.stat[:, 1, :], in0=r.psB[:, :],
                                         scalar1=1.0 / DM))
        vector.wait_ge(c.v.h, c.v.n)
        c.v.inc(vector.tensor_mul(out=r.stat[:, 2, :], in0=r.stat[:, 0, :],
                                  in1=r.stat[:, 0, :]))
        vector.wait_ge(c.v.h, c.v.n)
        c.v.inc(vector.tensor_sub(out=r.stat[:, 3, :], in0=r.stat[:, 1, :],
                                  in1=r.stat[:, 2, :]))
    @c.block.scalar
    def _(scalar):
        wait_all(scalar, c.v)
        scalar.activation(out=r.stat[:, 4, :], in_=r.stat[:, 3, :], func=ACT.Ln,
                          bias=r.consts[:, 0:1])
        c.s.inc(scalar.activation(out=r.stat[:, 5, :], in_=r.stat[:, 4, :],
                                  func=ACT.Exp, scale=r.consts[:, 1:2]))
    @c.block.tensor
    def _(tensor):
        wait_all(tensor, c.s)
        tensor.matmul(out=r.psM[:, :], lhsT=r.ones_row[:, :], rhs=r.stat[:, 0, :],
                      start=True, stop=True)
        c.mm.inc(tensor.matmul(out=r.psR[:, :], lhsT=r.ones_row[:, :],
                               rhs=r.stat[:, 5, :], start=True, stop=True))
    v_base = c.v.n
    for fc in range(16):
        qs = c.q[fc % 2]
        @c.block.sync
        def _(sync, fc=fc, t0=t0, qs=qs):
            if fc == 0:
                wait_all(sync, c.mm)
            if fc >= 2:
                sync.wait_ge(c.v.h, v_base + fc - 1)
            sync.dma_start(out=xst4[:, fc % 2, :],
                           in_=r.x_i[fc * P:(fc + 1) * P, t0:t0 + TS]
                           ).then_inc(qs.h, 16)
            qs.n += 16
        @c.block.vector
        def _(vector, fc=fc, qs=qs):
            vector.wait_ge(qs.h, qs.n)
            if fc == 0:
                wait_all(vector, c.mm)
                if DBG:
                    wait_all(vector, c.st)
            vector.tensor_sub(out=r.sq[:, fc % 2, :], in0=xst4[:, fc % 2, :],
                              in1=r.psM[:, :])
            c.v.inc(vector.tensor_tensor(out=z[:, fc, :], in0=r.sq[:, fc % 2, :],
                                         in1=r.psR[:, :], op=AL.mult))


def _l3_main(c, r):
    nc = c.nc
    ps0, ps1, ps2, ps3 = r.ps0, r.ps1, r.ps2, r.ps3
    with (
        nc.sbuf_tensor("xst4", [P, 2, TS], F32) as xst4,
        nc.sbuf_tensor("f2", [P, 64, TS], BF16) as f2,
        nc.sbuf_tensor("gbuf", [P, 16, TS], BF16) as gbuf,
        nc.sbuf_tensor("wu_s", [P, 2, 16, P], BF16) as wu_s,
        nc.sbuf_tensor("wg_s", [P, 2, 16, P], BF16) as wg_s,
        nc.sbuf_tensor("wd_s", [P, 2, 64, P], BF16) as wd_s,
        nc.sbuf_tensor("wo_s", [P, 2, 32, P], BF16) as wo_s,
        nc.sbuf_tensor("gs", [P, TS], BF16) as gs,
        nc.sbuf_tensor("xres", [P, 2, TS], F32) as xres,
        nc.sbuf_tensor("sout", [P, 2, TS], F32) as sout,
    ):
        for ts in range(TPC // TS):
            t0 = ts * TS
            import contextlib as _ctx
            _zs = _ctx.ExitStack()
            z = _zs.enter_context(nc.sbuf_tensor(f"z{id(r) % 97}_{ts}", [P, 16, TS], BF16))
            _l3_ln(c, r, ts, xst4, z)
            if DBG and getattr(r, "z_o", None) is not None:
                @c.block.gpsimd
                def _(gpsimd, t0=t0, z=z):
                    wait_all(gpsimd, c.v)
                    for fc in range(16):
                        gpsimd.dma_start(out=r.z_o[fc * P:(fc + 1) * P, t0:t0 + TS],
                                         in_=z[:, fc, :]).then_inc(c.st.h, 16)
                    c.st.n += 16 * 16
            mm_ug = c.mm.n
            for ic in range(64):
                ws = ic % 2
                qs = c.q[ws]
                psU = ps0 if ic % 2 == 0 else ps1
                psG = ps2 if ic % 2 == 0 else ps3
                @c.block.sync
                def _(sync, ic=ic, ws=ws, qs=qs, mm_ug=mm_ug):
                    if ic >= 2:
                        sync.wait_ge(c.mm.h, mm_ug + ic - 1)
                    else:
                        wait_all(sync, c.mm)
                    sync.dma_start(
                        out=wu_s[:, ws, :, :],
                        in_=r.wup2[:, ic * P:(ic + 1) * P]
                            .rearrange("(k p) m -> p k m", p=P)).then_inc(qs.h, 16)
                    sync.dma_start(
                        out=wg_s[:, ws, :, :],
                        in_=r.wg2[:, ic * P:(ic + 1) * P]
                            .rearrange("(k p) m -> p k m", p=P)).then_inc(qs.h, 16)
                    qs.n += 32
                @c.block.tensor
                def _(tensor, ic=ic, ws=ws, psU=psU, psG=psG, qs=qs):
                    tensor.wait_ge(qs.h, qs.n)
                    wait_all(tensor, c.v, c.s)
                    for kc in range(16):
                        tensor.matmul(out=psU[:, :], lhsT=wu_s[:, ws, kc, :],
                                      rhs=z[:, kc, :], start=(kc == 0),
                                      stop=(kc == 15))
                    for kc in range(16):
                        ins = tensor.matmul(out=psG[:, :], lhsT=wg_s[:, ws, kc, :],
                                            rhs=z[:, kc, :], start=(kc == 0),
                                            stop=(kc == 15))
                    c.mm.inc(ins)
                @c.block.scalar
                def _(scalar, ic=ic, psG=psG):
                    wait_all(scalar, c.mm, c.v)
                    c.s.inc(scalar.activation(out=gs[:, :], in_=psG[:, :],
                                              func=ACT.Silu,
                                              scale=r.wsg2[:, ic:ic + 1]))
                @c.block.vector
                def _(vector, ic=ic, psU=psU):
                    wait_all(vector, c.s)
                    if DBG and ic == 0:
                        wait_all(vector, c.st)
                    c.v.inc(vector.scalar_tensor_tensor(
                        out=f2[:, ic, :], in0=psU[:, :],
                        scalar=r.wsu2[:, ic:ic + 1], in1=gs[:, :],
                        op0=AL.mult, op1=AL.mult))
            mm_dn = c.mm.n
            for oc in range(16):
                ws = oc % 2
                qs = c.q[2 + ws]
                pp = ps0 if oc % 2 == 0 else ps1
                @c.block.sync
                def _(sync, oc=oc, ws=ws, qs=qs, mm_dn=mm_dn):
                    if oc >= 2:
                        sync.wait_ge(c.mm.h, mm_dn + oc - 1)
                    else:
                        wait_all(sync, c.mm)
                    sync.dma_start(
                        out=wd_s[:, ws, :, :],
                        in_=r.wd2[:, oc * P:(oc + 1) * P]
                            .rearrange("(k p) m -> p k m", p=P)).then_inc(qs.h, 16)
                    qs.n += 16
                @c.block.tensor
                def _(tensor, oc=oc, ws=ws, pp=pp, qs=qs):
                    tensor.wait_ge(qs.h, qs.n)
                    wait_all(tensor, c.s)
                    if oc == 0:
                        wait_all(tensor, c.v)
                    for ic in range(64):
                        ins = tensor.matmul(out=pp[:, :], lhsT=wd_s[:, ws, ic, :],
                                            rhs=f2[:, ic, :], start=(ic == 0),
                                            stop=(ic == 63))
                    c.mm.inc(ins)
                @c.block.scalar
                def _(scalar, oc=oc, pp=pp):
                    wait_all(scalar, c.mm)
                    c.s.inc(scalar.activation(out=gbuf[:, oc, :], in_=pp[:, :],
                                              func=ACT.Silu,
                                              scale=r.wsd2[:, oc:oc + 1]))
            if DBG and getattr(r, "g_o", None) is not None:
                @c.block.gpsimd
                def _(gpsimd, t0=t0):
                    wait_all(gpsimd, c.s, c.v)
                    for oc in range(16):
                        gpsimd.dma_start(out=r.g_o[oc * P:(oc + 1) * P, t0:t0 + TS],
                                         in_=gbuf[:, oc, :]).then_inc(c.st.h, 16)
                    for ic in range(64):
                        gpsimd.dma_start(out=r.f2_o[ic * P:(ic + 1) * P, t0:t0 + TS],
                                         in_=f2[:, ic, :]).then_inc(c.st.h, 16)
                    c.st.n += 80 * 16
            _zs.close()
            _ys = _ctx.ExitStack()
            ymst = _ys.enter_context(
                nc.sbuf_tensor(f"ym{id(r) % 97}_{ts}", [P, 32, TS], BF16))
            if getattr(r, "ym_reg", None) is not None:
                @c.block.gpsimd
                def _(gpsimd, t0=t0):
                    wait_all(gpsimd, c.mm)
                    rq, rtmp = r.ym_reg
                    for icc in range(32):
                        gpsimd.reg_add(rtmp, rq, icc * P * T + (t0 - getattr(r, "x_off", 0)))
                        gpsimd.dma_start(out=ymst[:, icc, :],
                                         in_=AP(r.ym_i, rtmp, [[T, P], [1, TS]])
                                         ).then_inc(c.ld.h, 16)
                    c.ld.n += 32 * 16
            else:
                @c.block.sync
                def _(sync, t0=t0):
                    wait_all(sync, c.mm)
                    for icc in range(32):
                        sync.dma_start(out=ymst[:, icc, :],
                                       in_=r.ym_i[icc * P:(icc + 1) * P, t0:t0 + TS]
                                       ).then_inc(c.ld.h, 16)
                    c.ld.n += 32 * 16
            mm_op = c.mm.n
            snapo = {}
            for oc in range(16):
                ws = oc % 2
                qs = c.q[ws]
                pp = ps2 if oc % 2 == 0 else ps3
                @c.block.sync
                def _(sync, oc=oc, ws=ws, t0=t0, qs=qs, mm_op=mm_op):
                    if oc >= 2:
                        sync.wait_ge(c.mm.h, mm_op + oc - 1)
                        sync.wait_ge(c.v.h, snapo[oc - 2])
                    else:
                        wait_all(sync, c.mm, c.v)
                    sync.dma_start(
                        out=wo_s[:, ws, :, :],
                        in_=r.wo2[:, oc * P:(oc + 1) * P]
                            .rearrange("(k p) m -> p k m", p=P)).then_inc(qs.h, 16)
                    sync.dma_start(out=xres[:, ws, :],
                                   in_=r.x_i[oc * P:(oc + 1) * P, t0:t0 + TS]
                                   ).then_inc(qs.h, 16)
                    qs.n += 32
                @c.block.tensor
                def _(tensor, oc=oc, ws=ws, pp=pp, qs=qs):
                    tensor.wait_ge(qs.h, qs.n)
                    if oc - 2 in snapo:
                        tensor.wait_ge(c.v.h, snapo[oc - 2])
                    else:
                        wait_all(tensor, c.v)
                    for icc in range(32):
                        ins = tensor.matmul(out=pp[:, :], lhsT=wo_s[:, ws, icc, :],
                                            rhs=ymst[:, icc, :], start=(icc == 0),
                                            stop=(icc == 31))
                    c.mm.inc(ins)
                @c.block.vector
                def _(vector, oc=oc, ws=ws, pp=pp):
                    wait_all(vector, c.mm, c.st)
                    if oc == 0:
                        wait_all(vector, c.s)
                    vector.tensor_mul(out=sout[:, ws, :], in0=pp[:, :],
                                      in1=gbuf[:, oc, :])
                    c.v.inc(vector.tensor_add(out=sout[:, ws, :],
                                              in0=sout[:, ws, :],
                                              in1=xres[:, ws, :]))
                snapo[oc] = c.v.n
                @c.block.gpsimd
                def _(gpsimd, oc=oc, ws=ws, t0=t0):
                    wait_all(gpsimd, c.v)
                    gpsimd.dma_start(out=r.out_o[oc * P:(oc + 1) * P, t0:t0 + TS],
                                     in_=sout[:, ws, :]).then_inc(c.st.h, 16)
                    c.st.n += 16
            _ys.close()



def build_l3():
    nc = bass.Bass("TRN2", target_bir_lowering=False)
    r = _R()
    r.x_i = nc.dram_tensor("x_i", [DM, TPC], F32, kind="ExternalInput")
    r.ym_i = nc.dram_tensor("ym_i", [NI, TPC], BF16, kind="ExternalInput")
    r.woT = nc.dram_tensor("woT", [NI, DM], F32, kind="ExternalInput")
    r.wupT = nc.dram_tensor("wupT", [DM, 8192], F32, kind="ExternalInput")
    r.wgT2 = nc.dram_tensor("wgT2", [DM, 8192], F32, kind="ExternalInput")
    r.wdT = nc.dram_tensor("wdT", [8192, DM], F32, kind="ExternalInput")
    r.suu = nc.dram_tensor("suu", [P, 16], F32, kind="ExternalInput")
    r.sud = nc.dram_tensor("sud", [P, 64], F32, kind="ExternalInput")
    r.wsu = nc.dram_tensor("wsu", [P, 64], F32, kind="ExternalInput")
    r.wsg = nc.dram_tensor("wsg", [P, 64], F32, kind="ExternalInput")
    r.wsd = nc.dram_tensor("wsd", [P, 16], F32, kind="ExternalInput")
    r.h128 = nc.dram_tensor("h128", [P, P], F32, kind="ExternalInput")
    r.out_o = nc.dram_tensor("out_o", [DM, TPC], F32, kind="ExternalOutput")
    if DBG:
        r.z_o = nc.dram_tensor("z_o", [DM, TPC], BF16, kind="ExternalOutput")
        r.g_o = nc.dram_tensor("g_o", [DM, TPC], BF16, kind="ExternalOutput")
        r.f2_o = nc.dram_tensor("f2_o", [8192, TPC], BF16, kind="ExternalOutput")
    r.wup2 = nc.dram_tensor("wup2", [DM, 8192], BF16)
    r.wg2 = nc.dram_tensor("wg2", [DM, 8192], BF16)
    r.wd2 = nc.dram_tensor("wd2", [8192, DM], BF16)
    r.wo2 = nc.dram_tensor("wo2", [NI, DM], BF16)
    if DBG:
        r.xs_d = nc.dram_tensor("xs_d", [1024, T], F32, kind="ExternalOutput")
        r.sg_d = nc.dram_tensor("sg_d", [1024, T], F32, kind="ExternalOutput")
        r.ssm_d = nc.dram_tensor("ssm_d", [160, T], F32, kind="ExternalOutput")
        r.dt_d = nc.dram_tensor("dt_d", [1024, T], F32, kind="ExternalOutput")
        r.ym_d = nc.dram_tensor("ym_d", [1024, T], BF16, kind="ExternalOutput")
        r.at_d = nc.dram_tensor("at_d", [P, 128], F32, kind="ExternalOutput")
        r.da_d = nc.dram_tensor("da_d", [P, 2048], F32, kind="ExternalOutput")
        r.vt_d = nc.dram_tensor("vt_d", [P, 5 * 1024], F32, kind="ExternalOutput")
        r.ymq_d = nc.dram_tensor("ymq_d", [NI, TPC], BF16, kind="ExternalOutput")

    import contextlib
    _es = contextlib.ExitStack()
    r.ones_col = _es.enter_context(nc.sbuf_tensor("ones_col", [P, 1], F32))
    r.ones_row = _es.enter_context(nc.sbuf_tensor("ones_row", [1, P], F32))
    r.consts = _es.enter_context(nc.sbuf_tensor("consts", [1, 4], F32))
    r.h128f = _es.enter_context(nc.sbuf_tensor("h128f", [P, P], F32))
    r.h128b = _es.enter_context(nc.sbuf_tensor("h128b", [P, P], BF16))
    r.suu2 = _es.enter_context(nc.sbuf_tensor("suu2", [P, 16], F32))
    r.sud2 = _es.enter_context(nc.sbuf_tensor("sud2", [P, 64], F32))
    r.wsu2 = _es.enter_context(nc.sbuf_tensor("wsu2", [P, 64], F32))
    r.wsg2 = _es.enter_context(nc.sbuf_tensor("wsg2", [P, 64], F32))
    r.wsd2 = _es.enter_context(nc.sbuf_tensor("wsd2", [P, 16], F32))
    r.sq = _es.enter_context(nc.sbuf_tensor("sq", [P, 2, TS], F32))
    r.stat = _es.enter_context(nc.sbuf_tensor("stat", [1, 6, TS], F32))
    r.psA = _es.enter_context(nc.psum_tensor("psA", [1, TS], F32))
    r.psB = _es.enter_context(nc.psum_tensor("psB", [1, TS], F32))
    r.psM = _es.enter_context(nc.psum_tensor("psM", [P, TS], F32))
    r.psR = _es.enter_context(nc.psum_tensor("psR", [P, TS], F32))
    ps0 = _es.enter_context(nc.psum_tensor("ps0", [P, TS], F32))
    ps1 = _es.enter_context(nc.psum_tensor("ps1", [P, TS], F32))
    ps2 = _es.enter_context(nc.psum_tensor("ps2", [P, TS], F32))
    ps3 = _es.enter_context(nc.psum_tensor("ps3", [P, TS], F32))
    r.ps0, r.ps1, r.ps2, r.ps3 = ps0, ps1, ps2, ps3
    _s_ld = _es.enter_context(nc.semaphore("s_ld"))
    _s_st = _es.enter_context(nc.semaphore("s_st"))
    _s_mm = _es.enter_context(nc.semaphore("s_mm"))
    _s_v = _es.enter_context(nc.semaphore("s_v"))
    _s_s = _es.enter_context(nc.semaphore("s_s"))
    _qs = [_es.enter_context(nc.semaphore(f"s_q{i}")) for i in range(4)]
    block = _es.enter_context(nc.Block())
    if True:
        c = Ctx(nc, block, [Sem(_s_ld), Sem(_s_st), Sem(_s_mm), Sem(_s_v), Sem(_s_s)],
                [Sem(q) for q in _qs])

        @block.sync
        def _(sync):
            sync.dma_start(out=r.h128f[:, :], in_=r.h128[:, :]).then_inc(c.ld.h, 16)
            sync.dma_start(out=r.suu2[:, :], in_=r.suu[:, :]).then_inc(c.ld.h, 16)
            sync.dma_start(out=r.sud2[:, :], in_=r.sud[:, :]).then_inc(c.ld.h, 16)
            sync.dma_start(out=r.wsu2[:, :], in_=r.wsu[:, :]).then_inc(c.ld.h, 16)
            sync.dma_start(out=r.wsg2[:, :], in_=r.wsg[:, :]).then_inc(c.ld.h, 16)
            sync.dma_start(out=r.wsd2[:, :], in_=r.wsd[:, :]).then_inc(c.ld.h, 16)
            c.ld.n += 6 * 16

        @block.vector
        def _(vector):
            vector.memset(r.ones_col[:, :], 1.0)
            vector.memset(r.ones_row[:, :], 1.0)
            vector.memset(r.consts[:, 0:1], LN_EPS)
            vector.memset(r.consts[:, 1:2], -0.5)
            wait_all(vector, c.ld)
            vector.tensor_copy(out=r.h128b[:, :], in_=r.h128f[:, :])
            vector.tensor_scalar_mul(out=r.suu2[:, :], in0=r.suu2[:, :],
                                     scalar1=1.0 / (64.0 * float(np.sqrt(2048.0))))
            vector.tensor_scalar_mul(out=r.sud2[:, :], in0=r.sud2[:, :],
                                     scalar1=1.0 / (64.0 * float(np.sqrt(8192.0))))
            vector.tensor_scalar_mul(out=r.wsu2[:, :], in0=r.wsu2[:, :], scalar1=64.0)
            vector.tensor_scalar_mul(out=r.wsg2[:, :], in0=r.wsg2[:, :], scalar1=64.0)
            c.v.inc(vector.tensor_scalar_mul(out=r.wsd2[:, :], in0=r.wsd2[:, :],
                                             scalar1=64.0))
            vector.wait_ge(c.v.h, c.v.n)

        _l3_transform(c, r)
        c.barrier()

        _l3_main(c, r)

        @block.gpsimd
        def _(gpsimd):
            gpsimd.wait_ge(c.st.h, c.st.n)
    _es.close()
    return nc


def _hadamard(n):
    h = np.array([[1.0]], np.float32)
    while h.shape[0] < n:
        h = np.block([[h, h], [h, -h]])
    return h.astype(np.float32)


def _l3_inputs(inputs, xT, ymg):
    woT = np.ascontiguousarray(np.asarray(inputs["L0_attn_out_proj_w"], np.float32).T)
    wupT = np.ascontiguousarray(np.asarray(inputs["L0_ffn_W_up"], np.float32).T)
    wgT2 = np.ascontiguousarray(np.asarray(inputs["L0_ffn_W_gate"], np.float32).T)
    wdT = np.ascontiguousarray(np.asarray(inputs["L0_ffn_W_down"], np.float32).T)
    suu = np.ascontiguousarray(
        np.asarray(inputs["L0_ffn_SU_ug"], np.float32).reshape(16, P).T)
    sud = np.ascontiguousarray(
        np.asarray(inputs["L0_ffn_SU_dp"], np.float32).reshape(64, P).T)
    wsall = np.asarray(inputs["L0_ffn_Wscale_ug"], np.float32)
    wsu = np.ascontiguousarray(wsall[:8192].reshape(64, P).T)
    wsg = np.ascontiguousarray(wsall[8192:].reshape(64, P).T)
    wsd = np.ascontiguousarray(
        np.asarray(inputs["L0_ffn_Wscale_dp"], np.float32).reshape(16, P).T)
    h128 = _hadamard(P)
    maps = []
    for cc in range(8):
        b = (cc // 2) % 2
        q = 2 * (cc // 4) + (cc % 2)
        s0 = q * TPC
        maps.append({
            "x_i": np.ascontiguousarray(xT[b, :, s0:s0 + TPC]),
            "ym_i": np.ascontiguousarray(ymg[b, :, s0:s0 + TPC]),
            "woT": woT, "wupT": wupT, "wgT2": wgT2, "wdT": wdT,
            "suu": suu, "sud": sud, "wsu": wsu, "wsg": wsg, "wsd": wsd,
            "h128": h128,
        })
    return maps


def run_l3(inputs, xT, ymg):
    nc = build_l3()
    maps = _l3_inputs(inputs, xT, ymg)
    res = run_bass_kernel_spmd(nc, maps, core_ids=list(range(8)))
    out = np.empty((NB, DM, T), np.float32)
    for cc in range(8):
        b = (cc // 2) % 2
        q = 2 * (cc // 4) + (cc % 2)
        s0 = q * TPC
        out[b, :, s0:s0 + TPC] = res.results[cc]["out_o"]
    return out



def kernel_v1(**inputs):
    xT = np.ascontiguousarray(
        np.asarray(inputs["L0_attn_hidden_states"], np.float32).transpose(0, 2, 1))
    dtg, xsg, sgg, bcg = run_l1(inputs, xT)
    ymg = run_l2(inputs, dtg, xsg, sgg, bcg)
    out = run_l3(inputs, xT, ymg)
    return np.ascontiguousarray(out.transpose(0, 2, 1))



XPAD = T + 8


def _f_ln_chunk(c, r, k, xnb, xstage, sq, stat):
    base = k * HT
    @c.block.vector
    def _(vector):
        wait_all(vector, c.mm, c.s)
        c.v.inc(vector.memset(xnb[:, :, 4 * TT + 3:], 0.0))
    for (c0, w) in L1_TILES:
        @c.block.sync
        def _(sync, c0=c0, w=w):
            wait_all(sync, c.mm, c.v)
            for fc in range(16):
                sync.dma_start(out=xstage[:, fc, :w],
                               in_=r.x_pad[fc * P:(fc + 1) * P, base + c0:base + c0 + w]
                               ).then_inc(c.ld.h, 16)
            c.ld.n += 16 * 16
        @c.block.tensor
        def _(tensor, w=w):
            wait_all(tensor, c.ld, c.v)
            for fc in range(16):
                tensor.matmul(out=r.psA[:, :w], lhsT=r.ones_col[:, :],
                              rhs=xstage[:, fc, :w], start=(fc == 0), stop=(fc == 15))
        mm_base = c.mm.n
        s_base = c.s.n
        for fc in range(16):
            @c.block.scalar
            def _(scalar, w=w, fc=fc):
                if fc == 0:
                    wait_all(scalar, c.ld, c.mm, c.v)
                if fc >= 2:
                    scalar.wait_ge(c.mm.h, mm_base + fc - 1)
                c.s.inc(scalar.activation(out=sq[:, fc % 2, :w],
                                          in_=xstage[:, fc, :w], func=ACT.Square))
            @c.block.tensor
            def _(tensor, w=w, fc=fc):
                tensor.wait_ge(c.s.h, s_base + fc + 1)
                c.mm.inc(tensor.matmul(out=r.psB[:, :w], lhsT=r.ones_col[:, :],
                                       rhs=sq[:, fc % 2, :w],
                                       start=(fc == 0), stop=(fc == 15)))
        @c.block.vector
        def _(vector, w=w):
            wait_all(vector, c.mm)
            vector.tensor_scalar_mul(out=stat[:, 0, :w], in0=r.psA[:, :w],
                                     scalar1=1.0 / DM)
            c.v.inc(vector.tensor_scalar_mul(out=stat[:, 1, :w], in0=r.psB[:, :w],
                                             scalar1=1.0 / DM))
            vector.wait_ge(c.v.h, c.v.n)
            c.v.inc(vector.tensor_mul(out=stat[:, 2, :w], in0=stat[:, 0, :w],
                                      in1=stat[:, 0, :w]))
            vector.wait_ge(c.v.h, c.v.n)
            c.v.inc(vector.tensor_sub(out=stat[:, 3, :w], in0=stat[:, 1, :w],
                                      in1=stat[:, 2, :w]))
        @c.block.scalar
        def _(scalar, w=w):
            wait_all(scalar, c.v)
            scalar.activation(out=stat[:, 4, :w], in_=stat[:, 3, :w],
                              func=ACT.Ln, bias=r.consts[:, 0:1])
            c.s.inc(scalar.activation(out=stat[:, 5, :w], in_=stat[:, 4, :w],
                                      func=ACT.Exp, scale=r.consts[:, 1:2]))
        @c.block.tensor
        def _(tensor, w=w):
            wait_all(tensor, c.s)
            tensor.matmul(out=r.psM[:, :w], lhsT=r.ones_row[:, :],
                          rhs=stat[:, 0, :w], start=True, stop=True)
            c.mm.inc(tensor.matmul(out=r.psR[:, :w], lhsT=r.ones_row[:, :],
                                   rhs=stat[:, 5, :w], start=True, stop=True))
        @c.block.vector
        def _(vector, c0=c0, w=w):
            wait_all(vector, c.mm)
            for fc in range(16):
                vector.tensor_sub(out=xstage[:, fc, :w], in0=xstage[:, fc, :w],
                                  in1=r.psM[:, :w])
            for fc in range(16):
                ins = vector.tensor_tensor(out=xnb[:, fc, c0:c0 + w],
                                           in0=xstage[:, fc, :w],
                                           in1=r.psR[:, :w], op=AL.mult)
            c.v.inc(ins)


def _f_inproj_chunk(c, r, k, xnb, bufs):
    xbuf, cs, xst, xst16, sgt = bufs
    base = k * HT
    snap_v = {}
    snap_s = {}
    ln_v = c.v.n
    halo_v = {}
    for mc in range(16):
        is_x = mc < 8
        wmc = mc
        ch = (mc if is_x else mc - 8) * P
        if is_x:
            @c.block.tensor
            def _(tensor, wmc=wmc, mc=mc):
                tensor.wait_ge(c.v.h, max(ln_v, halo_v.get(mc - 1, 0)))
                for kc in range(16):
                    ins = tensor.matmul(out=r.ps2[:, 0:3], lhsT=r.wres[:, kc, wmc, :],
                                        rhs=xnb[:, kc, 0:3],
                                        start=(kc == 0), stop=(kc == 15))
                c.mm.inc(ins)
            @c.block.vector
            def _(vector):
                wait_all(vector, c.mm)
                c.v.inc(vector.tensor_copy(out=xbuf[:, 0:3], in_=r.ps2[:, 0:3]))
            halo_v[mc] = c.v.n
        for ti, (c0, w) in enumerate(L1_TILES):
            li = mc * 5 + ti
            pp = r.ps0 if li % 2 == 0 else r.ps1
            @c.block.tensor
            def _(tensor, wmc=wmc, c0=c0, w=w, pp=pp, li=li):
                if li - 2 in snap_v:
                    tensor.wait_ge(c.v.h, snap_v[li - 2])
                    tensor.wait_ge(c.s.h, snap_s[li - 2])
                else:
                    wait_all(tensor, c.v, c.s)
                for kc in range(16):
                    ins = tensor.matmul(out=pp[:, :w], lhsT=r.wres[:, kc, wmc, :],
                                        rhs=xnb[:, kc, c0:c0 + w],
                                        start=(kc == 0), stop=(kc == 15))
                c.mm.inc(ins)
            so, do, wd = _store_win(ti, w)
            do_g = do + base
            if is_x:
                @c.block.vector
                def _(vector, w=w, pp=pp, ti=ti, mc=mc):
                    small = w < 256
                    def step(ins):
                        c.v.inc(ins)
                        if small:
                            vector.wait_ge(c.v.h, c.v.n)
                    wait_all(vector, c.mm, c.st)
                    if ti > 0:
                        step(vector.tensor_copy(out=xbuf[:, 0:3],
                                                in_=xbuf[:, TT:TT + 3]))
                    step(vector.tensor_copy(out=xbuf[:, 3:3 + w], in_=pp[:, :w]))
                    step(vector.tensor_scalar_mul(out=cs[:, :w], in0=xbuf[:, 0:w],
                                                  scalar1=r.cwt[:, mc, 0:1]))
                    for kk in (1, 2, 3):
                        step(vector.scalar_tensor_tensor(
                            out=cs[:, :w], in0=xbuf[:, kk:kk + w],
                            scalar=r.cwt[:, mc, kk:kk + 1], in1=cs[:, :w],
                            op0=AL.mult, op1=AL.add))
                @c.block.scalar
                def _(scalar, w=w, mc=mc):
                    wait_all(scalar, c.v)
                    c.s.inc(scalar.activation(out=xst[:, :w], in_=cs[:, :w],
                                              func=ACT.Silu, bias=r.cbt[:, mc:mc + 1]))
                @c.block.vector
                def _(vector, w=w):
                    wait_all(vector, c.s)
                    c.v.inc(vector.tensor_copy(out=xst16[:, :w], in_=xst[:, :w]))
                if wd > 0:
                    @c.block.gpsimd
                    def _(gpsimd, ch=ch, so=so, do_g=do_g, wd=wd):
                        wait_all(gpsimd, c.v)
                        gpsimd.dma_start(
                            out=r.xs_hbm[ch:ch + P, do_g:do_g + wd],
                            in_=xst[:, so:so + wd]).then_inc(c.st.h, 16)
                        gpsimd.dma_start(
                            out=r.xsb_hbm[ch:ch + P, do_g:do_g + wd],
                            in_=xst16[:, so:so + wd]).then_inc(c.st.h, 16)
                        c.st.n += 32
            else:
                @c.block.scalar
                def _(scalar, w=w, pp=pp, ti=ti):
                    wait_all(scalar, c.mm, c.st)
                    c.s.inc(scalar.activation(out=sgt[:, ti % 2, :w], in_=pp[:, :w],
                                              func=ACT.Silu))
                if wd > 0:
                    @c.block.gpsimd
                    def _(gpsimd, ch=ch, so=so, do_g=do_g, wd=wd, ti=ti):
                        wait_all(gpsimd, c.s)
                        gpsimd.dma_start(
                            out=r.sg_hbm[ch:ch + P, do_g:do_g + wd],
                            in_=sgt[:, ti % 2, so:so + wd]).then_inc(c.st.h, 16)
                        c.st.n += 16
            snap_v[li] = c.v.n
            snap_s[li] = c.s.n


def _f_xproj(c, r):
    nc = c.nc
    import contextlib
    with contextlib.ExitStack() as es:
        xpst = es.enter_context(nc.sbuf_tensor("f_xpst", [P, 8, 160], F32))
        xpsb = es.enter_context(nc.sbuf_tensor("f_xpsb", [P, 8, 160], BF16))
        xsl = es.enter_context(nc.sbuf_tensor("f_xsl", [P, 2, TT], BF16))
        spdt = es.enter_context(nc.sbuf_tensor("f_spdt", [P, 2, TT], F32))
        spbc = es.enter_context(nc.sbuf_tensor("f_spbc", [32, 2, TT], F32))
        c.barrier()
        @c.block.sync
        def _(sync):
            sync.dma_start(out=xpst[:, :, :],
                           in_=r.xpT_s.ap().rearrange("(k p) r -> p k r", p=P)
                           ).then_inc(c.ld.h, 16)
            c.ld.n += 16
        @c.block.vector
        def _(vector):
            wait_all(vector, c.ld)
            c.v.inc(vector.tensor_copy(out=xpsb[:, :, :], in_=xpst[:, :, :]))
        c.barrier()
        for tj in range(T // TT):
            t0 = tj * TT
            for dc in range(8):
                qs = c.q[dc % 2]
                @c.block.sync
                def _(sync, t0=t0, dc=dc, qs=qs):
                    wait_all(sync, c.mm)
                    sync.dma_start(out=xsl[:, dc % 2, :],
                                   in_=r.xsb_hbm[dc * P:(dc + 1) * P, t0:t0 + TT]
                                   ).then_inc(qs.h, 16)
                    qs.n += 16
                @c.block.tensor
                def _(tensor, dc=dc, qs=qs):
                    tensor.wait_ge(qs.h, qs.n)
                    if dc == 0:
                        wait_all(tensor, c.v)
                    tensor.matmul(out=r.psM[:, :], lhsT=xpsb[:, dc, 0:P],
                                  rhs=xsl[:, dc % 2, :],
                                  start=(dc == 0), stop=(dc == 7))
                    c.mm.inc(tensor.matmul(out=r.psR[0:32, :],
                                           lhsT=xpsb[:, dc, P:160],
                                           rhs=xsl[:, dc % 2, :],
                                           start=(dc == 0), stop=(dc == 7)))
            sl = tj % 2
            @c.block.vector
            def _(vector, sl=sl):
                wait_all(vector, c.mm, c.st)
                vector.tensor_copy(out=spdt[:, sl, :], in_=r.psM[:, :])
                c.v.inc(vector.tensor_copy(out=spbc[:, sl, :], in_=r.psR[0:32, :]))
            @c.block.gpsimd
            def _(gpsimd, t0=t0, sl=sl):
                wait_all(gpsimd, c.v)
                gpsimd.dma_start(out=r.ssm_part[0:P, t0:t0 + TT],
                                 in_=spdt[:, sl, :]).then_inc(c.st.h, 16)
                gpsimd.dma_start(out=r.ssm_part[P:160, t0:t0 + TT],
                                 in_=spbc[:, sl, :]).then_inc(c.st.h, 16)
                c.st.n += 32


def _f_dtproj(c, r):
    nc = c.nc
    import contextlib
    with contextlib.ExitStack() as es:
        dtwst = es.enter_context(nc.sbuf_tensor("f_dtwst", [P, 8, P], F32))
        dtwsb = es.enter_context(nc.sbuf_tensor("f_dtwsb", [P, 8, P], BF16))
        dtbt = es.enter_context(nc.sbuf_tensor("f_dtbt", [P, 8], F32))
        dtif = es.enter_context(nc.sbuf_tensor("f_dtif", [P, 2, TT], F32))
        dtin = es.enter_context(nc.sbuf_tensor("f_dtin", [P, 2, TT], BF16))
        dpre = es.enter_context(nc.sbuf_tensor("f_dpre", [P, 2, TT], F32))
        dex = es.enter_context(nc.sbuf_tensor("f_dex", [P, TT], F32))
        dtt = es.enter_context(nc.sbuf_tensor("f_dtt", [P, 2, TT], F32))
        c.barrier()
        @c.block.sync
        def _(sync):
            sync.dma_start(out=dtwst[:, :, :],
                           in_=r.dtwT_s.ap().rearrange("p (k m) -> p k m", m=P)
                           ).then_inc(c.ld.h, 16)
            sync.dma_start(out=dtbt[:, :], in_=r.dtb_s[:, :]).then_inc(c.ld.h, 16)
            c.ld.n += 32
        @c.block.vector
        def _(vector):
            wait_all(vector, c.ld)
            c.v.inc(vector.tensor_copy(out=dtwsb[:, :, :], in_=dtwst[:, :, :]))
        c.barrier()
        for tj in range(T // TT):
            t0 = tj * TT
            sl = tj % 2
            qs = c.q[2 + sl]
            @c.block.sync
            def _(sync, t0=t0, sl=sl, qs=qs):
                wait_all(sync, c.v)
                sync.dma_start(out=dtif[:, sl, :],
                               in_=r.ssm_red[0:P, t0:t0 + TT]).then_inc(qs.h, 16)
                qs.n += 16
            @c.block.vector
            def _(vector, sl=sl, qs=qs):
                vector.wait_ge(qs.h, qs.n)
                wait_all(vector, c.mm)
                c.v.inc(vector.tensor_copy(out=dtin[:, sl, :], in_=dtif[:, sl, :]))
            for oc in range(8):
                pp = r.ps0 if oc % 2 == 0 else r.ps1
                @c.block.tensor
                def _(tensor, oc=oc, sl=sl, pp=pp):
                    wait_all(tensor, c.v)
                    if oc < 2:
                        wait_all(tensor, c.s)
                    c.mm.inc(tensor.matmul(out=pp[:, :],
                                           lhsT=dtwsb[:, oc, :],
                                           rhs=dtin[:, sl, :], start=True, stop=True))
                @c.block.vector
                def _(vector, oc=oc, pp=pp):
                    wait_all(vector, c.mm, c.s)
                    c.v.inc(vector.tensor_scalar(
                        out=dpre[:, oc % 2, :], in0=pp[:, :],
                        scalar1=dtbt[:, oc:oc + 1], scalar2=SP_CLAMP,
                        op0=AL.add, op1=AL.min))
                @c.block.scalar
                def _(scalar, oc=oc):
                    wait_all(scalar, c.v, c.st)
                    scalar.activation(out=dex[:, :], in_=dpre[:, oc % 2, :],
                                      func=ACT.Exp)
                    c.s.inc(scalar.activation(out=dtt[:, oc % 2, :], in_=dex[:, :],
                                              func=ACT.Ln, bias=r.ones_col[:, 0:1]))
                @c.block.gpsimd
                def _(gpsimd, t0=t0, oc=oc):
                    wait_all(gpsimd, c.s)
                    gpsimd.dma_start(out=r.dt_hbm[oc * P:(oc + 1) * P, t0:t0 + TT],
                                     in_=dtt[:, oc % 2, :]).then_inc(c.st.h, 16)
                    c.st.n += 16


def build_fused():
    nc = bass.Bass("TRN2", target_bir_lowering=False)
    r = _R()
    r.x_pad = nc.dram_tensor("x_pad", [DM, XPAD], F32, kind="ExternalInput")
    r.x_q = nc.dram_tensor("x_q", [DM, TPC], F32, kind="ExternalInput")
    r.qoff = nc.dram_tensor("qoff", [1, 2], mybir.dt.int32, kind="ExternalInput")
    r.wxT_s = nc.dram_tensor("wxT_s", [DM, 1024], F32, kind="ExternalInput")
    r.wgT_s = nc.dram_tensor("wgT_s", [DM, 1024], F32, kind="ExternalInput")
    r.convw_s = nc.dram_tensor("convw_s", [P, 8, 4], F32, kind="ExternalInput")
    r.convb_s = nc.dram_tensor("convb_s", [P, 8], F32, kind="ExternalInput")
    r.xpT_s = nc.dram_tensor("xpT_s", [1024, 160], F32, kind="ExternalInput")
    r.dtwT_s = nc.dram_tensor("dtwT_s", [P, 1024], F32, kind="ExternalInput")
    r.dtb_s = nc.dram_tensor("dtb_s", [P, 8], F32, kind="ExternalInput")
    r.alog = nc.dram_tensor("alog", [P, 8, 16], F32, kind="ExternalInput")
    r.dvec = nc.dram_tensor("dvec", [P, 8], F32, kind="ExternalInput")
    r.woT = nc.dram_tensor("woT", [NI, DM], F32, kind="ExternalInput")
    r.wupT = nc.dram_tensor("wupT", [DM, 8192], F32, kind="ExternalInput")
    r.wgT2 = nc.dram_tensor("wgT2", [DM, 8192], F32, kind="ExternalInput")
    r.wdT = nc.dram_tensor("wdT", [8192, DM], F32, kind="ExternalInput")
    r.suu = nc.dram_tensor("suu", [P, 16], F32, kind="ExternalInput")
    r.sud = nc.dram_tensor("sud", [P, 64], F32, kind="ExternalInput")
    r.wsu = nc.dram_tensor("wsu", [P, 64], F32, kind="ExternalInput")
    r.wsg = nc.dram_tensor("wsg", [P, 64], F32, kind="ExternalInput")
    r.wsd = nc.dram_tensor("wsd", [P, 16], F32, kind="ExternalInput")
    r.h128 = nc.dram_tensor("h128", [P, P], F32, kind="ExternalInput")
    r.out_o = nc.dram_tensor("out_o", [DM, TPC], F32, kind="ExternalOutput")
    r.xs_hbm = nc.dram_tensor("xs_hbm", [1024, T], F32)
    r.xsb_hbm = nc.dram_tensor("xsb_hbm", [1024, T], BF16)
    r.sg_hbm = nc.dram_tensor("sg_hbm", [1024, T], F32)
    r.ssm_part = nc.dram_tensor("ssm_part", [160, T], F32)
    r.ssm_red = nc.dram_tensor("ssm_red", [160, T], F32)
    r.dt_hbm = nc.dram_tensor("dt_hbm", [1024, T], F32)
    r.ym_own = nc.dram_tensor("ym_own", [1024, T], BF16)
    r.ym_all = nc.dram_tensor("ym_all", [NI, T], BF16)
    r.ym_q = nc.dram_tensor("ym_q", [NI, TPC], BF16)
    r.bc16 = nc.dram_tensor("bc16", [16, T], BF16)
    r.wup2 = nc.dram_tensor("wup2", [DM, 8192], BF16)
    r.wg2 = nc.dram_tensor("wg2", [DM, 8192], BF16)
    r.wd2 = nc.dram_tensor("wd2", [8192, DM], BF16)
    r.wo2 = nc.dram_tensor("wo2", [NI, DM], BF16)
    if DBG:
        r.xs_d = nc.dram_tensor("xs_d", [1024, T], F32, kind="ExternalOutput")
        r.sg_d = nc.dram_tensor("sg_d", [1024, T], F32, kind="ExternalOutput")
        r.ssm_d = nc.dram_tensor("ssm_d", [160, T], F32, kind="ExternalOutput")
        r.dt_d = nc.dram_tensor("dt_d", [1024, T], F32, kind="ExternalOutput")
        r.ym_d = nc.dram_tensor("ym_d", [1024, T], BF16, kind="ExternalOutput")
        r.at_d = nc.dram_tensor("at_d", [P, 128], F32, kind="ExternalOutput")
        r.da_d = nc.dram_tensor("da_d", [P, 2048], F32, kind="ExternalOutput")
        r.vt_d = nc.dram_tensor("vt_d", [P, 5 * 1024], F32, kind="ExternalOutput")
        r.ymq_d = nc.dram_tensor("ymq_d", [NI, TPC], BF16, kind="ExternalOutput")

    import contextlib
    _es = contextlib.ExitStack()
    r.ones_col = _es.enter_context(nc.sbuf_tensor("ones_col", [P, 1], F32))
    r.ones_row = _es.enter_context(nc.sbuf_tensor("ones_row", [1, P], F32))
    r.consts = _es.enter_context(nc.sbuf_tensor("consts", [1, 4], F32))
    r.h128f = _es.enter_context(nc.sbuf_tensor("h128f", [P, P], F32))
    r.h128b = _es.enter_context(nc.sbuf_tensor("h128b", [P, P], BF16))
    r.suu2 = _es.enter_context(nc.sbuf_tensor("suu2", [P, 16], F32))
    r.sud2 = _es.enter_context(nc.sbuf_tensor("sud2", [P, 64], F32))
    r.wsu2 = _es.enter_context(nc.sbuf_tensor("wsu2", [P, 64], F32))
    r.wsg2 = _es.enter_context(nc.sbuf_tensor("wsg2", [P, 64], F32))
    r.wsd2 = _es.enter_context(nc.sbuf_tensor("wsd2", [P, 16], F32))
    r.cwt = _es.enter_context(nc.sbuf_tensor("cwt", [P, 8, 4], F32))
    r.cbt = _es.enter_context(nc.sbuf_tensor("cbt", [P, 8], F32))
    r.sq = _es.enter_context(nc.sbuf_tensor("sq", [P, 2, TS], F32))
    r.stat = _es.enter_context(nc.sbuf_tensor("stat", [1, 6, TS], F32))
    r.psA = _es.enter_context(nc.psum_tensor("psA", [1, TS], F32))
    r.psB = _es.enter_context(nc.psum_tensor("psB", [1, TS], F32))
    r.psM = _es.enter_context(nc.psum_tensor("psM", [P, TS], F32))
    r.psR = _es.enter_context(nc.psum_tensor("psR", [P, TS], F32))
    r.ps0 = _es.enter_context(nc.psum_tensor("ps0", [P, TS], F32))
    r.ps1 = _es.enter_context(nc.psum_tensor("ps1", [P, TS], F32))
    r.ps2 = _es.enter_context(nc.psum_tensor("ps2", [P, TS], F32))
    r.ps3 = _es.enter_context(nc.psum_tensor("ps3", [P, TS], F32))
    _s_ld = _es.enter_context(nc.semaphore("s_ld"))
    _s_st = _es.enter_context(nc.semaphore("s_st"))
    _s_mm = _es.enter_context(nc.semaphore("s_mm"))
    _s_v = _es.enter_context(nc.semaphore("s_v"))
    _s_s = _es.enter_context(nc.semaphore("s_s"))
    _qs = [_es.enter_context(nc.semaphore(f"s_q{i}")) for i in range(4)]
    _cc = _es.enter_context(nc.semaphore("s_cc"))
    block = _es.enter_context(nc.Block())
    if True:
        c = Ctx(nc, block, [Sem(_s_ld), Sem(_s_st), Sem(_s_mm), Sem(_s_v), Sem(_s_s)],
                [Sem(q) for q in _qs])
        cc = Sem(_cc)

        @block.sync
        def _(sync):
            sync.dma_start(out=r.h128f[:, :], in_=r.h128[:, :]).then_inc(c.ld.h, 16)
            sync.dma_start(out=r.suu2[:, :], in_=r.suu[:, :]).then_inc(c.ld.h, 16)
            sync.dma_start(out=r.sud2[:, :], in_=r.sud[:, :]).then_inc(c.ld.h, 16)
            sync.dma_start(out=r.wsu2[:, :], in_=r.wsu[:, :]).then_inc(c.ld.h, 16)
            sync.dma_start(out=r.wsg2[:, :], in_=r.wsg[:, :]).then_inc(c.ld.h, 16)
            sync.dma_start(out=r.wsd2[:, :], in_=r.wsd[:, :]).then_inc(c.ld.h, 16)
            sync.dma_start(out=r.cwt[:, :, :], in_=r.convw_s[:, :, :]).then_inc(c.ld.h, 16)
            sync.dma_start(out=r.cbt[:, :], in_=r.convb_s[:, :]).then_inc(c.ld.h, 16)
            c.ld.n += 8 * 16

        @block.vector
        def _(vector):
            vector.memset(r.ones_col[:, :], 1.0)
            vector.memset(r.ones_row[:, :], 1.0)
            vector.memset(r.consts[:, 0:1], LN_EPS)
            vector.memset(r.consts[:, 1:2], -0.5)
            wait_all(vector, c.ld)
            vector.tensor_copy(out=r.h128b[:, :], in_=r.h128f[:, :])
            vector.tensor_scalar_mul(out=r.suu2[:, :], in0=r.suu2[:, :],
                                     scalar1=1.0 / (64.0 * float(np.sqrt(2048.0))))
            vector.tensor_scalar_mul(out=r.sud2[:, :], in0=r.sud2[:, :],
                                     scalar1=1.0 / (64.0 * float(np.sqrt(8192.0))))
            vector.tensor_scalar_mul(out=r.wsu2[:, :], in0=r.wsu2[:, :], scalar1=64.0)
            vector.tensor_scalar_mul(out=r.wsg2[:, :], in0=r.wsg2[:, :], scalar1=64.0)
            c.v.inc(vector.tensor_scalar_mul(out=r.wsd2[:, :], in0=r.wsd2[:, :],
                                             scalar1=64.0))
            vector.wait_ge(c.v.h, c.v.n)

        _l3_transform(c, r)
        c.barrier()

        with nc.sbuf_tensor("wres", [P, 16, 16, P], BF16) as wres:
            r.wres = wres
            with nc.sbuf_tensor("wstage", [P, 16, P], F32) as wstage:
                for half, wsrc in ((0, r.wxT_s), (1, r.wgT_s)):
                    for mc in range(8):
                        @c.block.sync
                        def _(sync, wsrc=wsrc, mc=mc):
                            wait_all(sync, c.v)
                            sync.dma_start(
                                out=wstage[:, :, :],
                                in_=wsrc[:, mc * P:(mc + 1) * P]
                                    .rearrange("(k p) m -> p k m", p=P)
                                ).then_inc(c.ld.h, 16)
                            c.ld.n += 16
                        @c.block.vector
                        def _(vector, half=half, mc=mc):
                            wait_all(vector, c.ld)
                            c.v.inc(vector.tensor_copy(
                                out=wres[:, :, half * 8 + mc, :], in_=wstage[:, :, :]))
            with (
                nc.sbuf_tensor("s1_xnb", [P, 16, HCOLS], BF16) as xnb,
                nc.sbuf_tensor("s1_xstage", [P, 16, TT], F32) as xstage,
                nc.sbuf_tensor("s1_xbuf", [P, 3 + TT], F32) as xbuf,
                nc.sbuf_tensor("s1_cs", [P, TT], F32) as cs,
                nc.sbuf_tensor("s1_xst", [P, TT], F32) as xst,
                nc.sbuf_tensor("s1_xst16", [P, TT], BF16) as xst16,
                nc.sbuf_tensor("s1_sgt", [P, 2, TT], F32) as sgt,
            ):
                for k in range(T // HT):
                    _f_ln_chunk(c, r, k, xnb, xstage, r.sq, r.stat)
                    _f_inproj_chunk(c, r, k, xnb, (xbuf, cs, xst, xst16, sgt))

        _f_xproj(c, r)

        @block.gpsimd
        def _(gpsimd):
            gpsimd.wait_ge(c.st.h, c.st.n)
            wait_all(gpsimd, c.v, c.s, c.mm)
            gpsimd.collective_compute(
                "AllReduce", AL.add,
                replica_groups=[[0, 1], [2, 3], [4, 5], [6, 7]],
                ins=[r.ssm_part.ap().opt()], outs=[r.ssm_red.ap().opt()],
            ).then_inc(cc.h, 1)
            cc.n += 1
        for eng in ("sync", "tensor", "vector", "scalar", "gpsimd"):
            @getattr(block, eng)
            def _(e):
                e.wait_ge(cc.h, cc.n)

        _f_dtproj(c, r)

        with (
            nc.sbuf_tensor("cstg", [16, 4096], F32) as cstg,
            nc.sbuf_tensor("cstb", [16, 4096], BF16) as cstb,
        ):
            for j in range(4):
                @block.sync
                def _(sync, j=j):
                    wait_all(sync, c.v, c.st)
                    sync.dma_start(out=cstg[:, :],
                                   in_=r.ssm_red[144:160, j * 4096:(j + 1) * 4096]
                                   ).then_inc(c.ld.h, 16)
                    c.ld.n += 16
                @block.vector
                def _(vector, j=j):
                    wait_all(vector, c.ld)
                    c.v.inc(vector.tensor_copy(out=cstb[:, :], in_=cstg[:, :]))
                @block.gpsimd
                def _(gpsimd, j=j):
                    wait_all(gpsimd, c.v)
                    gpsimd.dma_start(out=r.bc16[:, j * 4096:(j + 1) * 4096],
                                     in_=cstb[:, :]).then_inc(c.st.h, 16)
                    c.st.n += 16
        with contextlib.ExitStack() as es5:
            r.at = es5.enter_context(nc.sbuf_tensor("l2_at", [P, 8, 16], F32))
            r.dvt = es5.enter_context(nc.sbuf_tensor("l2_dvt", [P, 8], F32))
            r.carry = es5.enter_context(nc.sbuf_tensor("l2_carry", [P, P], F32))
            r.dtt = es5.enter_context(nc.sbuf_tensor("l2_dtt", [P, 8, TW], F32))
            r.ut = es5.enter_context(nc.sbuf_tensor("l2_ut", [P, 8, TW], F32))
            r.yt = es5.enter_context(nc.sbuf_tensor("l2_yt", [P, 8, TW], F32))
            r.xstg = es5.enter_context(nc.sbuf_tensor("l2_xstg", [P, 2, TW], F32))
            r.sgs = es5.enter_context(nc.sbuf_tensor("l2_sgs", [P, 2, TW], F32))
            r.bb = es5.enter_context(nc.sbuf_tensor("l2_bb", [P, 2, TW], F32))
            r.cb = es5.enter_context(nc.sbuf_tensor("l2_cb", [P, 2, TW], F32))
            r.dat = es5.enter_context(nc.sbuf_tensor("l2_dat", [P, 4, TW], F32))
            r.vt = es5.enter_context(nc.sbuf_tensor("l2_vt", [P, 4, TW], F32))
            r.ht = es5.enter_context(nc.sbuf_tensor("l2_ht", [P, TW], F32))
            r.pt = es5.enter_context(nc.sbuf_tensor("l2_pt", [P, 2, TW], F32))
            r.ymt = es5.enter_context(nc.sbuf_tensor("l2_ymt", [P, 2, TW], BF16))
            c.barrier()
            r.dt_i, r.xs_i, r.sg_i = r.dt_hbm, r.xs_hbm, r.sg_hbm
            r.bc_i, r.bc_row0 = r.ssm_red, 128
            r.ym_o = r.ym_own
            _l2_prologue(c, r)
            _l2_body(c, r)

        @block.gpsimd
        def _(gpsimd):
            gpsimd.wait_ge(c.st.h, c.st.n)
            wait_all(gpsimd, c.v, c.s, c.mm)
            gpsimd.collective_compute(
                "AllGather", AL.bypass,
                replica_groups=[[0, 1, 4, 5], [2, 3, 6, 7]],
                ins=[r.ym_own.ap().opt()], outs=[r.ym_all.ap().opt()],
            ).then_inc(cc.h, 1)
            cc.n += 1
        for eng in ("sync", "tensor", "vector", "scalar", "gpsimd"):
            @getattr(block, eng)
            def _(e):
                e.wait_ge(cc.h, cc.n)

        with (
            nc.sbuf_tensor("qofft", [1, 2], mybir.dt.int32) as qofft,
            nc.gpsimd.register("rq") as rq,
        ):
            @block.sync
            def _(sync):
                sync.dma_start(out=qofft[:1, :], in_=r.qoff[:1, :]).then_inc(c.ld.h, 16)
                c.ld.n += 16
            @block.gpsimd
            def _(gpsimd):
                wait_all(gpsimd, c.ld)
                gpsimd.reg_load(rq, qofft[:1, :1])
                gpsimd.dma_start(out=r.ym_q.ap(),
                                 in_=AP(r.ym_all, rq, [[T, NI], [1, TPC]])
                                 ).then_inc(c.st.h, 16)
                c.st.n += 16
                gpsimd.wait_ge(c.st.h, c.st.n)
        if DBG:
            @block.scalar
            def _(scalar):
                wait_all(scalar, c.st, c.v, c.mm, c.s)
                scalar.dma_start(out=r.xs_d.ap(), in_=r.xs_hbm.ap()).then_inc(c.ld.h, 16)
                scalar.dma_start(out=r.sg_d.ap(), in_=r.sg_hbm.ap()).then_inc(c.ld.h, 16)
                scalar.dma_start(out=r.ssm_d.ap(), in_=r.ssm_red.ap()).then_inc(c.ld.h, 16)
                scalar.dma_start(out=r.dt_d.ap(), in_=r.dt_hbm.ap()).then_inc(c.ld.h, 16)
                scalar.dma_start(out=r.ym_d.ap(), in_=r.ym_own.ap()).then_inc(c.ld.h, 16)
                scalar.dma_start(out=r.ymq_d.ap(), in_=r.ym_q.ap()).then_inc(c.ld.h, 16)
                c.ld.n += 96
                scalar.wait_ge(c.ld.h, c.ld.n)
        c.barrier()
        r.ym_i = r.ym_q
        r.ym_reg = None
        r.x_i = r.x_q
        r.x_off = 0
        _l3_main(c, r)

        @block.gpsimd
        def _(gpsimd):
            gpsimd.wait_ge(c.st.h, c.st.n)
    _es.close()
    return nc


def _fused_inputs(inputs, xT):
    w_in = np.asarray(inputs["L0_attn_in_proj_w"], np.float32)
    wxT = np.ascontiguousarray(w_in[0::2].T)
    wgT = np.ascontiguousarray(w_in[1::2].T)
    cw = np.asarray(inputs["L0_attn_conv_w"], np.float32)
    cb = np.asarray(inputs["L0_attn_conv_b"], np.float32)
    xpT = np.asarray(inputs["L0_attn_x_proj_weight"], np.float32).transpose(0, 2, 1)
    dtwT = np.asarray(inputs["L0_attn_dt_proj_weight"], np.float32).transpose(0, 2, 1)
    dtb = np.asarray(inputs["L0_attn_dt_proj_bias"], np.float32)
    alog_full = np.asarray(inputs["L0_attn_A_log"], np.float32)
    d_full = np.asarray(inputs["L0_attn_D"], np.float32)
    woT = np.ascontiguousarray(np.asarray(inputs["L0_attn_out_proj_w"], np.float32).T)
    wupT = np.ascontiguousarray(np.asarray(inputs["L0_ffn_W_up"], np.float32).T)
    wgT2 = np.ascontiguousarray(np.asarray(inputs["L0_ffn_W_gate"], np.float32).T)
    wdT = np.ascontiguousarray(np.asarray(inputs["L0_ffn_W_down"], np.float32).T)
    suu = np.ascontiguousarray(
        np.asarray(inputs["L0_ffn_SU_ug"], np.float32).reshape(16, P).T)
    sud = np.ascontiguousarray(
        np.asarray(inputs["L0_ffn_SU_dp"], np.float32).reshape(64, P).T)
    wsall = np.asarray(inputs["L0_ffn_Wscale_ug"], np.float32)
    wsu = np.ascontiguousarray(wsall[:8192].reshape(64, P).T)
    wsg = np.ascontiguousarray(wsall[8192:].reshape(64, P).T)
    wsd = np.ascontiguousarray(
        np.asarray(inputs["L0_ffn_Wscale_dp"], np.float32).reshape(16, P).T)
    h128 = _hadamard(P)
    maps = []
    for cc in range(8):
        h = cc // 4
        b = (cc // 2) % 2
        dh = cc % 2
        q = 2 * h + dh
        s0 = q * TPC
        lo = h * HD + dh * 1024
        x_pad = np.zeros((DM, XPAD), np.float32)
        x_pad[:, 3:3 + T] = xT[b]
        maps.append({
            "x_pad": x_pad,
            "x_q": np.ascontiguousarray(xT[b, :, s0:s0 + TPC]),
            "qoff": np.array([[s0, 0]], np.int32),
            "wxT_s": np.ascontiguousarray(wxT[:, lo:lo + 1024]),
            "wgT_s": np.ascontiguousarray(wgT[:, lo:lo + 1024]),
            "convw_s": np.ascontiguousarray(
                cw[lo:lo + 1024].reshape(8, P, 4).transpose(1, 0, 2)),
            "convb_s": np.ascontiguousarray(cb[lo:lo + 1024].reshape(8, P).T),
            "xpT_s": np.ascontiguousarray(xpT[h, dh * 1024:dh * 1024 + 1024, :]),
            "dtwT_s": np.ascontiguousarray(dtwT[h][:, dh * 1024:dh * 1024 + 1024]),
            "dtb_s": np.ascontiguousarray(
                dtb[h, dh * 1024:dh * 1024 + 1024].reshape(8, P).T),
            "alog": np.ascontiguousarray(
                alog_full[h, dh * 1024:dh * 1024 + 1024].reshape(8, P, 16)
                .transpose(1, 0, 2)),
            "dvec": np.ascontiguousarray(
                d_full[h, dh * 1024:dh * 1024 + 1024].reshape(8, P).T),
            "woT": woT, "wupT": wupT, "wgT2": wgT2, "wdT": wdT,
            "suu": suu, "sud": sud, "wsu": wsu, "wsg": wsg, "wsd": wsd,
            "h128": h128,
        })
    return maps


def kernel(**inputs):
    xT = np.ascontiguousarray(
        np.asarray(inputs["L0_attn_hidden_states"], np.float32).transpose(0, 2, 1))
    nc = build_fused()
    maps = _fused_inputs(inputs, xT)
    res = run_bass_kernel_spmd(nc, maps, core_ids=list(range(8)))
    out = np.empty((NB, DM, T), np.float32)
    for cc in range(8):
        b = (cc // 2) % 2
        q = 2 * (cc // 4) + (cc % 2)
        s0 = q * TPC
        out[b, :, s0:s0 + TPC] = res.results[cc]["out_o"]
    return np.ascontiguousarray(out.transpose(0, 2, 1))
